# version 10
# baseline (speedup 1.0000x reference)
import numpy as np
from contextlib import ExitStack

import concourse.bass as bass
import concourse.mybir as mybir
from concourse.bass_utils import run_bass_kernel_spmd

F32 = mybir.dt.float32
BF16 = mybir.dt.bfloat16
AF = mybir.ActivationFunctionType
ALU = mybir.AluOpType
AX = mybir.AxisListType

T = 1024
NT = 8
D = 4096
DC = 32
LN_EPS = 1e-5
NEG_INF = -1.0e9
SCALE_ATTN = 0.125
MOE_SCALING = 2.0
MAX_THRESHOLD = 0.125


def _copy(e, out, in_):
    if hasattr(e, "tensor_copy"):
        return e.tensor_copy(out, in_)
    return e.copy(out, in_)


class Ctr:
    __slots__ = ("h", "n")

    def __init__(self, h):
        self.h = h
        self.n = 0


class Prog:

    ENGINES = ("tensor", "vector", "scalar", "gpsimd", "sync")

    def __init__(self, nc, ctx):
        self.nc = nc
        self.ctx = ctx
        self.items = {e: [] for e in self.ENGINES}
        self.phase_marks = {}

    def mark_phase(self, name):
        self.phase_marks[name] = {e: len(self.items[e]) for e in self.ENGINES}

    def mksem(self, name):
        return Ctr(self.ctx.enter_context(self.nc.semaphore(name)))

    def op(self, eng, fn, inc=None, k=1):
        if inc is not None:
            inc.n += k
            sem, kk = inc.h, k

            def run(e, fn=fn, sem=sem, kk=kk):
                fn(e).then_inc(sem, kk)
            self.items[eng].append(run)
            return inc.n
        self.items[eng].append(lambda e, fn=fn: fn(e))
        return None

    def wait(self, eng, ctr, val=None):
        v = ctr.n if val is None else val
        if v is None or v <= 0:
            return
        self.items[eng].append(lambda e, h=ctr.h, v=v: e.wait_ge(h, v))

    def emit(self, stop_at=None):
        cut = self.phase_marks.get(stop_at) if stop_at else None
        with self.nc.Block() as block:
            for name in self.ENGINES:
                items = self.items[name]
                if cut is not None:
                    items = items[:cut[name]]
                if not items:
                    continue

                def body(eng, items=items):
                    for run in items:
                        run(eng)
                getattr(block, name)(body)


SERIALIZE_Y2 = False


def build_program(taps=(), stop_at=None):
    nc = bass.Bass("TRN2", target_bir_lowering=False)

    def din(name, shape):
        return nc.dram_tensor(name, list(shape), F32, kind="ExternalInput").ap()

    x_in = din("x_in", (T + 2, D))
    conv_w = din("conv_w", (3, D, 3072))
    conv_b = din("conv_b", (3072,))
    wq_d = din("wq", (1024, 1024))
    bq_d = din("bq", (1024,))
    wk_d = din("wk", (1024, 1024))
    bk_d = din("bk", (1024,))
    wv_d = din("wv", (1024, 1024))
    bv_d = din("bv", (1024,))
    wo_d = din("wo", (1024, 1024))
    bo_d = din("bo", (1024,))
    proj_w = din("proj_w", (1024, D))
    proj_b = din("proj_b", (D,))
    W_base = din("W_base", (D, D))
    b_base = din("b_base", (D,))
    W_gate = din("W_gate", (8, D))
    W_thr = din("W_thr", (D,))
    b_thr = din("b_thr", (1,))
    A_w = din("A_w", (128, D))
    B_w = din("B_w", (8, D, 16))

    y_out = nc.dram_tensor("y", [T, D], F32, kind="ExternalOutput").ap()
    x2s = nc.dram_tensor("x2s", [T, D], F32, kind="Internal").ap()
    wbase_bf = nc.dram_tensor("wbase_bf", [D, D], BF16, kind="Internal").ap()
    bwf_dram = nc.dram_tensor("bwf_dram", [128, D], F32, kind="Internal").ap()

    tap_outs = {}

    def tapout(name, shape, dtype):
        ap = nc.dram_tensor("tap_" + name, list(shape), dtype,
                            kind="ExternalOutput").ap()
        tap_outs[name] = ap
        return ap

    ctx = ExitStack()
    nc.ctx.enter_context(ctx)
    P = Prog(nc, ctx)

    pe = P.mksem("pe")
    dv = P.mksem("dv")
    ac = P.mksem("ac")
    po = P.mksem("po")
    qg = P.mksem("qg")
    qs = P.mksem("qs")
    qc = P.mksem("qc")
    cw = tuple(P.mksem(f"cw{i}") for i in range(4))
    xts = (P.mksem("xt0"), P.mksem("xt1"))
    wbs = (P.mksem("wb0"), P.mksem("wb1"))
    pws = (P.mksem("pw0"), P.mksem("pw1"))
    xcs = tuple(P.mksem(f"xc{i}") for i in range(4))
    x2st = (P.mksem("x2sa"), P.mksem("x2sb"))
    xrs = (P.mksem("xr0"), P.mksem("xr1"))
    xes = (P.mksem("xe0"), P.mksem("xe1"))
    wts = (P.mksem("wt0"), P.mksem("wt1"))
    yss = (P.mksem("ys0"), P.mksem("ys1"))
    tps = P.mksem("tps")

    sb = lambda *a, **kw: ctx.enter_context(nc.sbuf_tensor(*a, **kw))
    psum = lambda c, *a: c.enter_context(nc.psum_tensor(*a))

    ident_bf = sb("ident_bf", [128, 128], BF16)
    ident_f = sb("ident_f", [128, 128], F32)
    ones_bf = sb("ones_bf", [1, 128], BF16)
    ones_f = sb("ones_f", [1, 128], F32)
    convb_col = sb("convb_col", [128, 24], F32)
    bqs_col = sb("bqs_col", [128, 8], F32)
    bk_col = sb("bk_col", [128, 8], F32)
    bo_col = sb("bo_col", [128, 8], F32)
    nbthr_col = sb("nbthr_col", [128, 1], F32)
    eps_col = sb("eps_col", [128, 1], F32)
    c_sums = sb("c_sums", [128, NT, 8], F32)
    c_sumsq = sb("c_sumsq", [128, NT, 8], F32)

    def dma_tap(name, src_fn, shape, dtype, waits):
        if name not in taps:
            return
        ap = tapout(name, shape, dtype)
        for (ctr, v) in waits:
            P.wait("scalar", ctr, v)
        P.op("scalar", lambda s, ap=ap: s.dma_start(out=ap, in_=src_fn()),
             inc=tps, k=16)

    pctx = ExitStack()
    stage = pctx.enter_context(nc.sbuf_tensor("stage_row", [1, 3072], F32, side="right"))
    tp_f = (pctx.enter_context(nc.psum_tensor("tp_fa", [128, 128], F32)),
            pctx.enter_context(nc.psum_tensor("tp_fb", [128, 128], F32)))

    P.op("gpsimd", lambda g: g.memset(ident_bf[:, :], 1.0), inc=po)
    P.wait("gpsimd", po)
    P.op("gpsimd", lambda g: g.affine_select(
        out=ident_bf[:, :], in_=ident_bf[:, :], pattern=[[-1, 128]],
        compare_op=ALU.is_equal, fill=0.0, base=0, channel_multiplier=1), inc=po)
    P.op("gpsimd", lambda g: g.memset(ident_f[:, :], 1.0), inc=po)
    P.wait("gpsimd", po)
    P.op("gpsimd", lambda g: g.affine_select(
        out=ident_f[:, :], in_=ident_f[:, :], pattern=[[-1, 128]],
        compare_op=ALU.is_equal, fill=0.0, base=0, channel_multiplier=1), inc=po)
    P.op("gpsimd", lambda g: g.memset(ones_bf[:, :], 1.0), inc=po)
    P.op("gpsimd", lambda g: g.memset(ones_f[:, :], 1.0), inc=po)
    P.op("gpsimd", lambda g: g.memset(eps_col[:, :], LN_EPS), inc=po)
    m_po_const = po.n

    rows = [
        (conv_b, 3072, convb_col, 24),
        (bq_d, 1024, bqs_col, 8),
        (bk_d, 1024, bk_col, 8),
        (bo_d, 1024, bo_col, 8),
    ]
    for src, n, dst_col, nch in rows:
        P.wait("sync", pe)
        P.op("sync", lambda s, src=src, n=n:
             s.dma_start(out=stage[0:1, 0:n], in_=src[None, :]), inc=qs, k=16)
        P.wait("tensor", qs)
        P.wait("tensor", po, m_po_const)
        for c in range(nch):
            slot = tp_f[c % 2]
            if dv.n >= 2:
                P.wait("tensor", dv, dv.n - 1)
            P.op("tensor", lambda t, c=c, slot=slot:
                 t.transpose(slot[:, 0:1], stage[0:1, c * 128:(c + 1) * 128],
                             ident_f[0:1, 0:1]), inc=pe)
            P.wait("vector", pe)
            P.op("vector", lambda v, c=c, slot=slot, dst_col=dst_col:
                 v.tensor_copy(dst_col[:, c:c + 1], slot[:, 0:1]), inc=dv)
    P.wait("vector", dv)
    P.op("vector", lambda v: v.tensor_scalar(
        out=bqs_col[:, :], in0=bqs_col[:, :], scalar1=SCALE_ATTN,
        scalar2=None, op0=ALU.mult), inc=dv)

    P.wait("sync", pe)
    P.op("sync", lambda s: s.dma_start(out=stage[0:1, 0:1], in_=b_thr[None, :]),
         inc=qs, k=16)
    P.wait("tensor", qs)
    P.wait("tensor", dv)
    P.op("tensor", lambda t: t.matmul(tp_f[0][:, 0:1], ones_f[0:1, 0:128],
                                      stage[0:1, 0:1], start=True, stop=True),
         inc=pe)
    P.wait("vector", pe)
    P.op("vector", lambda v: v.tensor_scalar(
        out=nbthr_col[:, :], in0=tp_f[0][:, 0:1], scalar1=-1.0, scalar2=None,
        op0=ALU.mult), inc=dv)
    m_prologue_pe = pe.n

    P.mark_phase("prologue")
    abctx = ExitStack()
    asb = lambda *a, **kw: abctx.enter_context(nc.sbuf_tensor(*a, **kw))
    qkvT = asb("qkvT", [128, 24, T], BF16)

    axctx = ExitStack()
    xlnT = axctx.enter_context(nc.sbuf_tensor("xlnT", [128, DC, T + 2], BF16))
    cwp = [axctx.enter_context(nc.sbuf_tensor(f"cwp{i}", [128, 3, 512], BF16))
           for i in range(4)]
    actx = ExitStack()
    lsb = lambda *a, **kw: actx.enter_context(nc.sbuf_tensor(*a, **kw))
    xt = [lsb("xt0s", [128, D], F32), lsb("xt1s", [128, D], F32)]
    dump1 = lsb("dump1", [128, D], BF16)
    xln_bf = [lsb("xlnbf0", [128, D], BF16), lsb("xlnbf1", [128, D], BF16)]
    lnsm = lsb("lnsm", [128, 16], F32)
    tpb = [actx.enter_context(nc.psum_tensor(f"tpb{i}", [128, 128], BF16))
           for i in range(4)]

    m_norm = {}
    m_tp = {}
    tp_evac = [None] * 4
    m_xlnT_dv = 0
    m_xlnT_ac = 0
    xcnt = [0, 0]
    for t in range(9):
        slot = t % 2
        nrow = 128 if t < 8 else 2
        r0 = t * 128 if t < 8 else 1024
        col0 = r0
        if t >= 2:
            P.wait("sync", dv, m_norm[t - 2])
        else:
            P.wait("sync", dv)
        P.op("sync", lambda s, slot=slot, r0=r0, nrow=nrow:
             s.dma_start(out=xt[slot][0:nrow, :], in_=x_in[r0:r0 + nrow, :]),
             inc=xts[slot], k=16)
        xcnt[slot] += 1
        P.wait("vector", xts[slot], 16 * xcnt[slot])
        m_s1 = P.op("vector", lambda v, slot=slot, nrow=nrow: v.tensor_scalar(
            out=dump1[0:nrow, :], in0=xt[slot][0:nrow, :], scalar1=1.0,
            scalar2=None, op0=ALU.mult, op1=ALU.add,
            accum_out=lnsm[0:nrow, slot:slot + 1]), inc=dv)
        P.wait("scalar", xts[slot], 16 * xcnt[slot])
        if t >= 2:
            P.wait("scalar", pe, m_tp[t - 2])
        m_s2 = P.op("scalar", lambda s, slot=slot, nrow=nrow: s.activation(
            xln_bf[slot][0:nrow, :], xt[slot][0:nrow, :], AF.Square,
            accum_out=lnsm[0:nrow, 2 + slot:3 + slot]), inc=ac)
        P.wait("vector", dv, m_s1)
        P.wait("vector", ac, m_s2)
        c_mn, c_msq, c_nv = 4 + slot, 6 + slot, 8 + slot
        P.op("vector", lambda v, slot=slot, nrow=nrow, c=c_mn: v.tensor_scalar(
            out=lnsm[0:nrow, c:c + 1], in0=lnsm[0:nrow, slot:slot + 1],
            scalar1=1.0 / D, scalar2=None, op0=ALU.mult), inc=dv)
        P.wait("vector", dv)
        P.op("vector", lambda v, slot=slot, nrow=nrow, c=c_msq: v.tensor_scalar(
            out=lnsm[0:nrow, c:c + 1], in0=lnsm[0:nrow, 2 + slot:3 + slot],
            scalar1=1.0 / D, scalar2=None, op0=ALU.mult), inc=dv)
        P.wait("vector", dv)
        m_nv = P.op("vector", lambda v, nrow=nrow, cm=c_mn, cq=c_msq, cn=c_nv:
                    v.scalar_tensor_tensor(
                        out=lnsm[0:nrow, cn:cn + 1], in0=lnsm[0:nrow, cm:cm + 1],
                        scalar=lnsm[0:nrow, cm:cm + 1], in1=lnsm[0:nrow, cq:cq + 1],
                        op0=ALU.mult, op1=ALU.subtract), inc=dv)
        P.wait("scalar", dv, m_nv)
        c_ln, c_rs = 10 + slot, 12 + slot
        P.op("scalar", lambda s, nrow=nrow, cn=c_nv, cl=c_ln: s.activation(
            lnsm[0:nrow, cl:cl + 1], lnsm[0:nrow, cn:cn + 1], AF.Ln,
            scale=-1.0, bias=eps_col[0:nrow, :]), inc=ac)
        P.wait("scalar", ac)
        m_rstd = P.op("scalar", lambda s, nrow=nrow, cl=c_ln, cr=c_rs:
                      s.activation(lnsm[0:nrow, cr:cr + 1],
                                   lnsm[0:nrow, cl:cl + 1], AF.Exp, scale=-0.5),
                      inc=ac)
        P.wait("vector", ac, m_rstd)
        m_norm[t] = P.op("vector", lambda v, slot=slot, nrow=nrow, cm=c_mn, cr=c_rs:
                         v.tensor_scalar(
                             out=xln_bf[slot][0:nrow, :], in0=xt[slot][0:nrow, :],
                             scalar1=lnsm[0:nrow, cm:cm + 1],
                             scalar2=lnsm[0:nrow, cr:cr + 1],
                             op0=ALU.subtract, op1=ALU.mult), inc=dv)
        P.wait("tensor", dv, m_norm[t])
        for dc in range(DC):
            s4 = dc % 4
            if tp_evac[s4] is not None:
                P.wait("tensor", tp_evac[s4][0], tp_evac[s4][1])
            P.op("tensor", lambda t_, slot=slot, nrow=nrow, dc=dc, s4=s4:
                 t_.transpose(tpb[s4][:, 0:nrow],
                              xln_bf[slot][0:nrow, dc * 128:(dc + 1) * 128],
                              ident_bf[0:nrow, 0:nrow]), inc=pe)
            eng, ctr = ("vector", dv) if dc % 2 == 0 else ("scalar", ac)
            P.wait(eng, pe)
            mv = P.op(eng, lambda e, dc=dc, s4=s4, col0=col0, nrow=nrow:
                      _copy(e, xlnT[:, dc, col0:col0 + nrow],
                            tpb[s4][:, 0:nrow]), inc=ctr)
            tp_evac[s4] = (ctr, mv)
        m_tp[t] = pe.n
    m_xlnT_dv = dv.n
    m_xlnT_ac = ac.n

    dma_tap("xlnT", lambda: xlnT[:, :, :], (128, DC, T + 2), BF16,
            [(dv, m_xlnT_dv), (ac, m_xlnT_ac)])
    P.mark_phase("ln1")

    actx.close()
    pctx.close()
    cctx = ExitStack()
    pc = [cctx.enter_context(nc.psum_tensor(f"pc{i}", [128, 512], F32))
          for i in range(8)]

    P.wait("tensor", dv, m_xlnT_dv)
    P.wait("tensor", ac, m_xlnT_ac)
    m_piece = {}
    m_conv_evac = [ (None, None) ] * 6
    conv_bank_evac = {}
    cwcnt = [0, 0, 0, 0]
    for ms in range(6):
        for k in range(DC):
            p = ms * DC + k
            slot = p % 4
            if p >= 4:
                P.wait("gpsimd", pe, m_piece[p - 4])
            P.op("gpsimd", lambda g, k=k, ms=ms, slot=slot: g.dma_start(
                out=cwp[slot][:, :, :],
                in_=conv_w[:, k * 128:(k + 1) * 128,
                           ms * 512:(ms + 1) * 512].rearrange("w d m -> d w m")),
                inc=cw[slot], k=16)
            cwcnt[slot] += 1
            P.wait("tensor", cw[slot], 16 * cwcnt[slot])
            for w in range(3):
                for msub in range(4):
                    for half in range(2):
                        last_of_piece = (w == 2 and msub == 3 and half == 1)
                        bank = msub * 2 + half
                        if k == 0 and w == 0 and ms >= 1:
                            g = conv_bank_evac.get((ms - 1, bank))
                            if g is not None:
                                P.wait("tensor", g[0], g[1])
                        P.op("tensor", lambda t_, k=k, w=w, msub=msub,
                             half=half, slot=slot:
                             t_.matmul(pc[msub * 2 + half][:, :],
                                       cwp[slot][:, w, msub * 128:(msub + 1) * 128],
                                       xlnT[:, k, 512 * half + w: 512 * half + w + 512],
                                       start=(k == 0 and w == 0),
                                       stop=(k == DC - 1 and w == 2)),
                             inc=pe if last_of_piece else None)
            m_piece[p] = pe.n
        m_group = pe.n
        for msub in range(4):
            for half in range(2):
                mchunk = ms * 4 + msub
                bank = msub * 2 + half
                tile = pc[bank]
                if msub < 2:
                    P.wait("vector", pe, m_group)
                    mv = P.op("vector", lambda v, tile=tile, mchunk=mchunk, half=half:
                              v.tensor_scalar(
                                  out=qkvT[:, mchunk, 512 * half:512 * half + 512],
                                  in0=tile[:, :], scalar1=convb_col[:, mchunk:mchunk + 1],
                                  scalar2=None, op0=ALU.add), inc=dv)
                    conv_bank_evac[(ms, bank)] = (dv, mv)
                else:
                    P.wait("scalar", pe, m_group)
                    mv = P.op("scalar", lambda s, tile=tile, mchunk=mchunk, half=half:
                              s.activation(
                                  qkvT[:, mchunk, 512 * half:512 * half + 512],
                                  tile[:, :], AF.Identity,
                                  bias=convb_col[:, mchunk:mchunk + 1]), inc=ac)
                    conv_bank_evac[(ms, bank)] = (ac, mv)
        m_conv_evac[ms] = (dv.n, ac.n)
    m_qkvT_dv, m_qkvT_ac = dv.n, ac.n
    m_conv_done_pe = pe.n

    dma_tap("qkvT", lambda: qkvT[:, :, :], (128, 24, T), BF16,
            [(dv, m_qkvT_dv), (ac, m_qkvT_ac)])
    P.mark_phase("conv")

    cctx.close()
    axctx.close()

    wbctx = ExitStack()
    wbuf = [wbctx.enter_context(nc.sbuf_tensor("wbuf0", [128, 8, 1024], BF16, side="right")),
            wbctx.enter_context(nc.sbuf_tensor("wbuf1", [128, 8, 1024], BF16, side="right"))]
    wbcnt = [0, 0]
    P.op("gpsimd", lambda g: g.dma_start(
        out=wbuf[0][:, :, :],
        in_=wq_d.rearrange("(c p) e -> p c e", p=128)), inc=wbs[0], k=16)
    wbcnt[0] += 1
    P.op("gpsimd", lambda g: g.dma_start(
        out=wbuf[1][:, :, :],
        in_=wk_d.rearrange("(c p) e -> p c e", p=128)), inc=wbs[1], k=16)
    wbcnt[1] += 1

    bctx = ExitStack()
    bsb = lambda *a, **kw: bctx.enter_context(nc.sbuf_tensor(*a, **kw))
    qhT = bsb("qhT", [128, 8, T], BF16)
    khT = bsb("khT", [128, 8, T], BF16)
    vh = bsb("vh", [128, 8, 1024], BF16)

    bqkctx = ExitStack()
    pq = [bqkctx.enter_context(nc.psum_tensor(f"pq{i}", [128, 512], F32))
          for i in range(4)]
    bvrow = bqkctx.enter_context(nc.sbuf_tensor("bvrow", [1, 1024], BF16))
    g1 = P.mksem("g1")
    qg_wo_sem = g1
    m_qkevac = {}
    for proj, (wslot, dstT, bias_col, scale) in enumerate(
            ((0, qhT, bqs_col, SCALE_ATTN), (1, khT, bk_col, 1.0))):
        P.wait("tensor", wbs[wslot], 16 * wbcnt[wslot])
        if proj == 0:
            P.wait("tensor", dv, m_qkvT_dv)
            P.wait("tensor", ac, m_qkvT_ac)
        for hc in range(8):
            for half in range(2):
                tile = pq[(hc % 2) * 2 + half]
                key = (proj, hc - 2, half)
                if key in m_qkevac:
                    P.wait("tensor", ac, m_qkevac[key])
                mlast = None
                for kc in range(8):
                    mlast = P.op("tensor", lambda t_, tile=tile, wslot=wslot,
                                 kc=kc, hc=hc, half=half:
                                 t_.matmul(tile[:, :],
                                           wbuf[wslot][:, kc, hc * 128:(hc + 1) * 128],
                                           qkvT[:, (0 if wslot == 0 else 8) + kc,
                                                half * 512:half * 512 + 512],
                                           start=(kc == 0), stop=(kc == 7)),
                                 inc=pe if kc == 7 else None)
                P.wait("scalar", pe, mlast)
                m_qkevac[(proj, hc, half)] = P.op(
                    "scalar", lambda s, tile=tile, dstT=dstT, hc=hc, half=half,
                    bias_col=bias_col, scale=scale:
                    s.activation(dstT[:, hc, half * 512:half * 512 + 512],
                                 tile[:, :], AF.Identity, scale=scale,
                                 bias=bias_col[:, hc:hc + 1]), inc=ac)
        if proj == 0:
            m_qproj_pe = pe.n
    m_qk_ac = ac.n

    P.wait("gpsimd", pe, m_qproj_pe)
    P.op("gpsimd", lambda g: g.dma_start(
        out=wbuf[0][:, :, :],
        in_=wv_d.rearrange("(c p) e -> p c e", p=128)), inc=wbs[0], k=16)
    wbcnt[0] += 1
    P.op("gpsimd", lambda g: g.dma_start(out=bvrow[0:1, :], in_=bv_d[None, :]),
         inc=g1, k=16)
    m_qg_bv = g1.n
    for e_ in range(8):
        P.op("sync", lambda s, e_=e_: s.dma_start(
            out=bwf_dram.rearrange("p (c e r) -> p c e r", e=8, r=16)[:, :, e_, :],
            in_=B_w[e_, :, :].rearrange("(c p) r -> p c r", p=128)),
            inc=qc, k=16)
    m_qc_relay = qc.n

    m_vevac = {}
    P.wait("tensor", wbs[0], 16 * wbcnt[0])
    P.wait("tensor", g1, m_qg_bv)
    P.wait("tensor", ac, m_qk_ac)
    for tt in range(NT):
        for hh in range(2):
            tile = pq[(tt % 2) * 2 + hh]
            key = (tt - 2, hh)
            if key in m_vevac:
                P.wait("tensor", dv, m_vevac[key])
            for kc in range(8):
                P.op("tensor", lambda t_, tile=tile, kc=kc, tt=tt, hh=hh:
                     t_.matmul(tile[:, :],
                               qkvT[:, 16 + kc, tt * 128:(tt + 1) * 128],
                               wbuf[0][:, kc, hh * 512:hh * 512 + 512],
                               start=(kc == 0), stop=False))
            mlast = P.op("tensor", lambda t_, tile=tile, hh=hh:
                         t_.matmul(tile[:, :], ones_bf[0:1, 0:128],
                                   bvrow[0:1, hh * 512:hh * 512 + 512],
                                   start=False, stop=True), inc=pe)
            P.wait("vector", pe, mlast)
            m_vevac[(tt, hh)] = P.op(
                "vector", lambda v, tile=tile, tt=tt, hh=hh:
                v.tensor_copy(vh[:, tt, hh * 512:hh * 512 + 512], tile[:, :]),
                inc=dv)
    m_vh_dv = dv.n
    m_vproj_pe = pe.n
    bqkctx.close()
    wbctx.close()

    P.mark_phase("qkv")
    dma_tap("qhT", lambda: qhT[:, :, :], (128, 8, T), BF16, [(ac, m_qk_ac)])
    dma_tap("khT", lambda: khT[:, :, :], (128, 8, T), BF16, [(ac, m_qk_ac)])
    dma_tap("vh", lambda: vh[:, :, :], (128, 8, 1024), BF16, [(dv, m_vh_dv)])
    P.mark_phase("qkv_taps")

    octx = ExitStack()
    oT = octx.enter_context(nc.sbuf_tensor("oT", [128, 8, T], BF16))
    wo_sb = octx.enter_context(nc.sbuf_tensor("wo_sb", [128, 8, 1024], BF16))
    P.wait("gpsimd", pe, m_conv_done_pe)
    P.op("gpsimd", lambda g: g.dma_start(
        out=wo_sb[:, :, :],
        in_=wo_d.rearrange("(c p) e -> p c e", p=128)), inc=g1, k=16)
    m_qg_wo = g1.n

    b2ctx = ExitStack()
    a2 = lambda *a, **kw: b2ctx.enter_context(nc.sbuf_tensor(*a, **kw))
    mask_bf = a2("mask_bf", [128, 128], BF16)
    Pf = [a2("Pf0", [128, 512], F32), a2("Pf1", [128, 512], F32)]
    Pbf = [a2("Pbf0", [128, 512], BF16), a2("Pbf1", [128, 512], BF16)]
    PT_sb = [a2("PTsb0", [128, 128], BF16), a2("PTsb1", [128, 128], BF16)]
    att_sm = a2("att_sm", [128, 24], F32)
    SpB = b2ctx.enter_context(nc.psum_tensor("SpB", [128, 2048], F32))
    PTp = [b2ctx.enter_context(nc.psum_tensor(f"PTp{i}", [128, 128], BF16))
           for i in range(2)]
    OTp = [b2ctx.enter_context(nc.psum_tensor(f"OTp{i}", [128, 128], F32))
           for i in range(2)]

    P.op("gpsimd", lambda g: g.memset(mask_bf[:, :], 0.0), inc=po)
    P.wait("gpsimd", po)
    m_mask = P.op("gpsimd", lambda g: g.affine_select(
        out=mask_bf[:, :], in_=mask_bf[:, :], pattern=[[-1, 128]],
        compare_op=ALU.is_ge, fill=NEG_INF, base=0, channel_multiplier=1),
        inc=po)
    g4s = P.mksem("g4s")
    for i in range(8):
        P.op("gpsimd", lambda g, i=i: g.dma_start(
            out=wbase_bf[i * 512:(i + 1) * 512, :],
            in_=W_base[i * 512:(i + 1) * 512, :]), inc=g4s, k=16)
    m_qg_wbase = g4s.n

    m_spb_used = {}
    m_pbf_used = [None, None]
    m_ptevac = [None, None]
    m_ptpevac = [None, None]
    m_otevac = [None, None]
    P.wait("tensor", ac, m_qk_ac)
    P.wait("tensor", dv, m_vh_dv)
    P.wait("tensor", po, m_mask)
    groups = [(b, hg) for b in range(NT) for hg in range(4)]
    gstate = {}

    def attn_p1a(g):
        b, hg = groups[g]
        gs = g % 2
        if g >= 1:
            P.wait("tensor", ac, m_spb_used[g - 1])
        mS = None
        for i in range(4):
            h = 4 * hg + i
            ch, hf = h // 2, h % 2
            P.op("tensor", lambda t_, i=i, ch=ch, hf=hf, b=b:
                 t_.matmul(SpB[:, 512 * i:512 * i + 128],
                           qhT[64 * hf:64 * hf + 64, ch, 128 * b:128 * b + 128],
                           khT[64 * hf:64 * hf + 64, ch, 128 * b:128 * b + 128],
                           start=True, stop=False,
                           tile_position=(64 * hf, 0)))
            mS = P.op("tensor", lambda t_, i=i:
                      t_.matmul(SpB[:, 512 * i:512 * i + 128],
                                ident_bf[:, :], mask_bf[:, :],
                                start=False, stop=True), inc=pe)
        P.wait("vector", pe, mS)
        mneg = P.op("vector", lambda v, gs=gs: v.reduce_max(
            att_sm[:, 12 * gs:12 * gs + 4],
            SpB.rearrange("p (i k) -> p i k", i=4)[:, :, 0:128],
            axis=AX.X, negate=True), inc=dv)
        P.wait("scalar", dv, mneg)
        mexp = None
        for i in range(4):
            mexp = P.op("scalar", lambda s, gs=gs, i=i: s.activation(
                Pf[gs][:, 128 * i:128 * (i + 1)],
                SpB[:, 512 * i:512 * i + 128], AF.Exp,
                bias=att_sm[:, 12 * gs + i:12 * gs + i + 1],
                accum_out=att_sm[:, 12 * gs + 4 + i:12 * gs + 5 + i]),
                inc=ac if i == 3 else None)
        m_spb_used[g] = mexp
        gstate[g] = mexp

    def attn_p1b(g):
        gs = g % 2
        mexp = gstate.pop(g)
        P.wait("vector", ac, mexp)
        mri = P.op("vector", lambda v, gs=gs: v.reciprocal(
            att_sm[:, 12 * gs + 8:12 * gs + 12],
            att_sm[:, 12 * gs + 4:12 * gs + 8]), inc=dv)
        P.wait("vector", dv, mri)
        if m_pbf_used[gs] is not None:
            P.wait("vector", pe, m_pbf_used[gs])
        mpn = None
        for i in range(4):
            mpn = P.op("vector", lambda v, gs=gs, i=i: v.tensor_scalar(
                out=Pbf[gs][:, 128 * i:128 * (i + 1)],
                in0=Pf[gs][:, 128 * i:128 * (i + 1)],
                scalar1=att_sm[:, 12 * gs + 8 + i:12 * gs + 9 + i],
                scalar2=None, op0=ALU.mult), inc=dv if i == 3 else None)
        gstate[(g, "pn")] = mpn

    def attn_p2(g):
        b, hg = groups[g]
        gs = g % 2
        mpn = gstate.pop((g, "pn"))
        P.wait("tensor", dv, mpn)
        mo_bank = [None, None]
        mt = None
        for i in range(4):
            h = 4 * hg + i
            ps_slot = i % 2
            ot_bank = i // 2
            if m_ptpevac[ps_slot] is not None:
                P.wait("tensor", m_ptpevac[ps_slot][0], m_ptpevac[ps_slot][1])
            mt = P.op("tensor", lambda t_, gs=gs, i=i, ps_slot=ps_slot:
                      t_.transpose(PTp[ps_slot][:, :],
                                   Pbf[gs][:, 128 * i:128 * (i + 1)],
                                   ident_bf[:, :]), inc=pe)
            if m_ptevac[ps_slot] is not None:
                P.wait("vector", m_ptevac[ps_slot][0], m_ptevac[ps_slot][1])
            P.wait("vector", pe, mt)
            mc = P.op("vector", lambda v, ps_slot=ps_slot:
                      v.tensor_copy(PT_sb[ps_slot][:, :], PTp[ps_slot][:, :]),
                      inc=dv)
            m_ptpevac[ps_slot] = (dv, mc)
            P.wait("tensor", dv, mc)
            if i % 2 == 0 and m_otevac[ot_bank] is not None:
                P.wait("tensor", m_otevac[ot_bank][0], m_otevac[ot_bank][1])
            if i % 2 == 1:
                P.wait("tensor", pe, mo_bank[ot_bank])
            mo = P.op("tensor", lambda t_, i=i, h=h, b=b, ps_slot=ps_slot,
                      ot_bank=ot_bank:
                      t_.matmul(OTp[ot_bank][64 * (i % 2):64 * (i % 2) + 64, :],
                                vh[:, b, 64 * h:64 * h + 64],
                                PT_sb[ps_slot][:, :],
                                start=True, stop=True,
                                tile_position=(0, 64 * (i % 2))),
                      inc=pe)
            mo_bank[ot_bank] = mo
            m_ptevac[ps_slot] = (pe, mo)
        m_pbf_used[gs] = mt
        for j in range(2):
            P.wait("scalar", pe, mo_bank[j])
            mev = P.op("scalar", lambda s, hg=hg, b=b, j=j:
                       s.activation(oT[:, 2 * hg + j, 128 * b:128 * b + 128],
                                    OTp[j][:, :], AF.Copy), inc=ac)
            m_otevac[j] = (ac, mev)

    for g in range(len(groups)):
        attn_p1a(g)
        if g >= 1:
            attn_p2(g - 1)
        attn_p1b(g)
    attn_p2(len(groups) - 1)
    m_oT_ac = ac.n
    b2ctx.close()

    dma_tap("oT", lambda: oT[:, :, :], (128, 8, T), BF16, [(ac, m_oT_ac)])
    P.mark_phase("attn")

    cc = ExitStack()
    csb = lambda *a, **kw: cc.enter_context(nc.sbuf_tensor(*a, **kw))
    attn_midT = csb("attn_midT", [128, 8, T], BF16)
    pwb = [csb("pwb0", [128, 8, 512], BF16), csb("pwb1", [128, 8, 512], BF16)]
    xcb = [csb("xcb0", [128, 512], F32), csb("xcb1", [128, 512], F32)]
    x2p_sb = [csb("x2p0", [128, 512], F32), csb("x2p1", [128, 512], F32)]
    dump2 = csb("dump2", [128, 512], F32)
    sqd = [csb("sqd0", [128, 512], BF16), csb("sqd1", [128, 512], BF16)]
    projb_row = csb("projb_row", [1, D], BF16)
    cyctx = ExitStack()
    py = [cyctx.enter_context(nc.psum_tensor(f"py{i}", [128, 512], F32))
          for i in range(4)]

    g2s = P.mksem("g2s")
    P.wait("gpsimd", pe, m_vproj_pe)
    P.op("gpsimd", lambda g: g.dma_start(out=projb_row[0:1, :],
                                         in_=proj_b[None, :]), inc=g2s, k=16)
    m_qg_projb = g2s.n

    m_y1evac = {}
    P.wait("tensor", ac, m_oT_ac)
    P.wait("tensor", qg_wo_sem, m_qg_wo)
    for wc in range(8):
        for half in range(2):
            tile = py[(wc % 2) * 2 + half]
            key = (wc - 2, half)
            if key in m_y1evac:
                P.wait("tensor", ac, m_y1evac[key])
            mlast = None
            for hc in range(8):
                mlast = P.op("tensor", lambda t_, tile=tile, hc=hc, wc=wc, half=half:
                             t_.matmul(tile[:, :],
                                       wo_sb[:, hc, wc * 128:(wc + 1) * 128],
                                       oT[:, hc, half * 512:half * 512 + 512],
                                       start=(hc == 0), stop=(hc == 7)),
                             inc=pe if hc == 7 else None)
            P.wait("scalar", pe, mlast)
            m_y1evac[(wc, half)] = P.op(
                "scalar", lambda s, tile=tile, wc=wc, half=half:
                s.activation(attn_midT[:, wc, half * 512:half * 512 + 512],
                             tile[:, :], AF.Identity, bias=bo_col[:, wc:wc + 1]),
                inc=ac)
    m_y1_ac = ac.n
    dma_tap("attn_midT", lambda: attn_midT[:, :, :], (128, 8, T), BF16,
            [(ac, m_y1_ac)])
    P.mark_phase("y1")

    cyctx.close()
    pc2ctx = ExitStack()
    pc2 = [pc2ctx.enter_context(nc.psum_tensor(f"pc2{i}", [128, 512], F32))
           for i in range(8)]
    pwcnt = [0, 0]
    xccnt = [0, 0]
    xc_need = {}
    x2cnt = [0, 0]
    m_pw_used = {}
    m_xc_load_order = {}
    m_x2evac = {}
    m_sq_used = {}
    for ds in range(8):
        slot = ds % 2
        if SERIALIZE_Y2:
            P.wait("tensor", dv)
            P.wait("tensor", ac)
        if ds >= 2:
            P.wait("gpsimd", pe, m_pw_used[ds - 2])
        P.op("gpsimd", lambda g, ds=ds: g.dma_start(
            out=pwb[ds % 2][:, :, :],
            in_=proj_w.rearrange("(c p) d -> p c d", p=128)[:, :, ds * 512:(ds + 1) * 512]),
            inc=pws[slot], k=16)
        pwcnt[slot] += 1
        P.wait("tensor", pws[slot], 16 * pwcnt[slot])
        if ds == 0:
            P.wait("tensor", ac, m_y1_ac)
            P.wait("tensor", g2s, m_qg_projb)
        for tt in range(NT):
            idx = ds * 8 + tt
            xsl = idx % 2
            key = (idx - 2)
            if key in m_xc_load_order:
                P.wait("sync", dv, m_xc_load_order[key])
            elif ds == 0 and tt < 2:
                P.wait("sync", dv, m_xlnT_dv)
            P.op("sync", lambda s, ds=ds, tt=tt, xsl=xsl: s.dma_start(
                out=xcb[xsl][:, :],
                in_=x_in[1 + tt * 128:1 + tt * 128 + 128,
                         ds * 512:(ds + 1) * 512]), inc=xcs[xsl], k=16)
            xccnt[xsl] += 1
            xc_need[idx] = (xsl, xccnt[xsl])
            if ds >= 1:
                P.wait("tensor", dv, m_x2evac.get((ds - 1, tt)))
            for wc in range(8):
                P.op("tensor", lambda t_, wc=wc, tt=tt, slot=slot:
                     t_.matmul(pc2[tt][:, :],
                               attn_midT[:, wc, tt * 128:(tt + 1) * 128],
                               pwb[slot][:, wc, :],
                               start=(wc == 0), stop=False))
            mlast = P.op("tensor", lambda t_, tt=tt, ds=ds:
                         t_.matmul(pc2[tt][:, :], ones_bf[0:1, 0:128],
                                   projb_row[0:1, ds * 512:ds * 512 + 512],
                                   start=False, stop=True), inc=pe)
            psl = (ds * 8 + tt) % 2
            P.wait("vector", pe, mlast)
            _sl, _n = xc_need[ds * 8 + tt]
            P.wait("vector", xcs[_sl], 16 * _n)
            prev_store = (ds * 8 + tt) - 2
            if prev_store >= 0:
                P.wait("vector", x2st[psl], 16 * (x2cnt[psl]))
                mprev_sq = m_sq_used.get(prev_store)
                if mprev_sq is not None:
                    P.wait("vector", ac, mprev_sq)
            madd = P.op("vector", lambda v, tt=tt, psl=psl, xsl=xsl:
                        v.tensor_add(x2p_sb[psl][:, :], pc2[tt][:, :],
                                     xcb[xsl][:, :]), inc=dv)
            m_xc_load_order[ds * 8 + tt] = madd
            m_x2evac[(ds, tt)] = madd
            P.wait("vector", dv, madd)
            P.op("vector", lambda v, tt=tt, ds=ds, psl=psl: v.tensor_scalar(
                out=dump2[:, :], in0=x2p_sb[psl][:, :], scalar1=1.0,
                scalar2=None, op0=ALU.mult, op1=ALU.add,
                accum_out=c_sums[:, tt, ds:ds + 1]), inc=dv)
            P.wait("scalar", dv, madd)
            m_sq_used[ds * 8 + tt] = P.op(
                "scalar", lambda s, tt=tt, ds=ds, psl=psl:
                s.activation(sqd[psl][:, :], x2p_sb[psl][:, :], AF.Square,
                             accum_out=c_sumsq[:, tt, ds:ds + 1]), inc=ac)
            P.op("scalar", lambda s, tt=tt, ds=ds, psl=psl: s.dma_start(
                out=x2s[tt * 128:(tt + 1) * 128, ds * 512:(ds + 1) * 512],
                in_=x2p_sb[psl][:, :]), inc=x2st[psl], k=16)
            x2cnt[psl] += 1
        m_pw_used[ds] = pe.n
    m_x2_all = (x2cnt[0] * 16, x2cnt[1] * 16)
    m_stats_dv = dv.n
    m_stats_ac = ac.n
    if "x2" in taps:
        ap = tapout("x2", (T, D), F32)
        P.wait("scalar", x2st[0], m_x2_all[0])
        P.wait("scalar", x2st[1], m_x2_all[1])
        P.op("scalar", lambda s, ap=ap: s.dma_start(out=ap, in_=x2s), inc=tps, k=16)
    pc2ctx.close()
    cc.close()
    octx.close()
    bctx.close()
    abctx.close()

    P.mark_phase("y2")
    dectx = ExitStack()
    dsb = lambda *a, **kw: dectx.enter_context(nc.sbuf_tensor(*a, **kw))
    xln2T = dsb("xln2T", [128, DC, T], BF16)
    A_wT = dsb("A_wT", [128, DC, 128], BF16)
    Bmat = dsb("Bmat", [128, D], BF16)
    wgt_cols = dsb("wgt_cols", [128, DC, 9], BF16)
    E_bf = dsb("E_bf", [8, 128], BF16)
    hT_sb = dsb("hT_sb", [128, T], BF16)
    hwT_sb = dsb("hwT_sb", [128, 8, 128], BF16)
    bbase_row = dsb("bbase_row", [1, D], BF16)
    gts_sb = dsb("gts_sb", [128, NT, 9], F32)

    dp1 = ExitStack()
    awtm = dp1.enter_context(nc.sbuf_tensor("awtm", [128, D], BF16, side="right"))
    wg_tm = dp1.enter_context(nc.sbuf_tensor("wg_tm", [8, D], BF16, side="right"))
    wth_row = dp1.enter_context(nc.sbuf_tensor("wth_row", [1, D], BF16, side="right"))
    dpctx = ExitStack()
    tpd = [dpctx.enter_context(nc.psum_tensor(f"tpd{i}", [128, 128], BF16))
           for i in range(2)]
    tp9 = dpctx.enter_context(nc.psum_tensor("tp9", [128, 9], BF16))

    P.op("gpsimd", lambda g: g.memset(E_bf[:, :], MOE_SCALING), inc=po)
    P.wait("gpsimd", po)
    P.op("gpsimd", lambda g: g.affine_select(
        out=E_bf[:, :], in_=E_bf[:, :], pattern=[[1, 128]],
        compare_op=ALU.is_ge, fill=0.0, base=0, channel_multiplier=-16), inc=po)
    P.wait("gpsimd", po)
    m_E = P.op("gpsimd", lambda g: g.affine_select(
        out=E_bf[:, :], in_=E_bf[:, :], pattern=[[-1, 128]],
        compare_op=ALU.is_ge, fill=0.0, base=15, channel_multiplier=16), inc=po)

    g3s = P.mksem("g3s")
    P.op("gpsimd", lambda g: g.dma_start(out=awtm[:, :], in_=A_w), inc=g3s, k=16)
    P.op("gpsimd", lambda g: g.dma_start(out=wg_tm[0:8, :], in_=W_gate), inc=g3s, k=16)
    P.op("gpsimd", lambda g: g.dma_start(out=wth_row[0:1, :], in_=W_thr[None, :]),
         inc=g3s, k=16)
    P.op("gpsimd", lambda g: g.dma_start(out=bbase_row[0:1, :], in_=b_base[None, :]),
         inc=g3s, k=16)
    m_qg_dprep = g3s.n

    P.wait("tensor", g3s, m_qg_dprep)
    P.wait("tensor", dv, m_stats_dv)
    P.wait("tensor", ac, m_stats_ac)
    m_dprep_evac = []
    tpd_guard = [None, None]
    for dc in range(DC):
        s2 = dc % 2
        if tpd_guard[s2] is not None:
            P.wait("tensor", tpd_guard[s2][0], tpd_guard[s2][1])
        mt = P.op("tensor", lambda t_, dc=dc, s2=s2: t_.transpose(
            tpd[s2][:, :], awtm[:, dc * 128:(dc + 1) * 128], ident_bf[:, :]),
            inc=pe)
        P.wait("vector", pe, mt)
        mv = P.op("vector", lambda v, dc=dc, s2=s2: v.tensor_copy(
            A_wT[:, dc, :], tpd[s2][:, :]), inc=dv)
        tpd_guard[s2] = (dv, mv)
    m_w9 = None
    for dc in range(DC):
        P.wait("tensor", dv if m_w9 is None else m_w9[0],
               None if m_w9 is None else m_w9[1])
        mtg = P.op("tensor", lambda t_, dc=dc: t_.transpose(
            tp9[:, 0:8], wg_tm[0:8, dc * 128:(dc + 1) * 128],
            ident_bf[0:8, 0:8]), inc=pe)
        P.wait("tensor", pe, mtg)
        mt = P.op("tensor", lambda t_, dc=dc: t_.transpose(
            tp9[:, 8:9], wth_row[0:1, dc * 128:(dc + 1) * 128],
            ident_bf[0:1, 0:1]), inc=pe)
        P.wait("vector", pe, mt)
        mv = P.op("vector", lambda v, dc=dc: v.tensor_copy(
            wgt_cols[:, dc, :], tp9[:, :]), inc=dv)
        m_w9 = (dv, mv)
    m_wgt_done = m_w9[1]
    m_dp1_done = pe.n
    dp1.close()
    dp2 = ExitStack()
    bwf = dp2.enter_context(nc.sbuf_tensor("bwf", [128, DC, 128], F32, side="right"))
    bwbf = dp2.enter_context(nc.sbuf_tensor("bwbf", [128, DC, 128], BF16, side="right"))
    P.wait("scalar", qc, m_qc_relay)
    P.wait("scalar", x2st[0], m_x2_all[0])
    P.wait("scalar", x2st[1], m_x2_all[1])
    P.wait("scalar", pe, m_dp1_done)
    P.op("scalar", lambda s: s.dma_start(
        out=bwf[:, :, :], in_=bwf_dram.rearrange("p (c k) -> p c k", c=DC)),
        inc=qc, k=16)
    m_qc_bw = qc.n
    P.wait("vector", qc, m_qc_bw)
    mcast = P.op("vector", lambda v: v.tensor_copy(bwbf[:, :, :], bwf[:, :, :]),
                 inc=dv)
    P.wait("tensor", dv, mcast)
    tpd_guard = [None, None]
    for oc in range(DC):
        s2 = oc % 2
        if tpd_guard[s2] is not None:
            P.wait("tensor", tpd_guard[s2][0], tpd_guard[s2][1])
        mt = P.op("tensor", lambda t_, oc=oc, s2=s2: t_.transpose(
            tpd[s2][:, :], bwbf[:, oc, :], ident_bf[:, :]), inc=pe)
        eng, ctr = ("vector", dv) if oc % 2 == 0 else ("scalar", ac)
        P.wait(eng, pe, mt)
        mv = P.op(eng, lambda e, oc=oc, s2=s2: _copy(
            e, Bmat[:, oc * 128:(oc + 1) * 128], tpd[s2][:, :]), inc=ctr)
        tpd_guard[s2] = (ctr, mv)
    m_dprep_dv = dv.n
    m_dprep_ac = ac.n
    m_dprep_pe = pe.n
    dp2.close()

    P.mark_phase("dprep")
    c2ctx = ExitStack()
    c2s = lambda *a, **kw: c2ctx.enter_context(nc.sbuf_tensor(*a, **kw))
    xr_sb = [c2s("xr0s", [128, D], F32), c2s("xr1s", [128, D], F32)]
    xln2_bf = [c2s("xln2bf0", [128, D], BF16), c2s("xln2bf1", [128, D], BF16)]
    ln2sm = c2s("ln2sm", [128, 16], F32)
    tp2 = [c2ctx.enter_context(nc.psum_tensor(f"tp2{i}", [128, 128], BF16))
           for i in range(3)]
    pgt1 = c2ctx.enter_context(nc.psum_tensor("pgt1", [128, 9], F32))
    ph1 = c2ctx.enter_context(nc.psum_tensor("ph1", [128, 512], F32))

    P.wait("sync", x2st[0], m_x2_all[0])
    P.wait("sync", x2st[1], m_x2_all[1])
    m2_norm = {}
    m2_tp = {}
    m2_tile = {}
    m_gtevac = {}
    m_hT = {}
    tp2_evac = [None] * 3

    def plan_gt(tt):
        P.wait("tensor", dv, m2_tile[tt][0])
        P.wait("tensor", ac, m2_tile[tt][1])
        if tt == 0:
            P.wait("tensor", dv, m_wgt_done)
        if (tt - 1) in m_gtevac:
            P.wait("tensor", ac, m_gtevac[tt - 1])
        mg = None
        for dc in range(DC):
            mg = P.op("tensor", lambda t_, tt=tt, dc=dc: t_.matmul(
                pgt1[:, :], xln2T[:, dc, tt * 128:(tt + 1) * 128],
                wgt_cols[:, dc, :], start=(dc == 0), stop=(dc == DC - 1)),
                inc=pe if dc == DC - 1 else None)
        P.wait("scalar", pe, mg)
        m_gtevac[tt] = P.op("scalar", lambda s, tt=tt: s.activation(
            gts_sb[:, tt, :], pgt1[:, :], AF.Copy), inc=ac)

    def plan_hT(half):
        if (half - 1) in m_hT:
            P.wait("tensor", ac, m_hT[half - 1])
        mh = None
        for dc in range(DC):
            mh = P.op("tensor", lambda t_, half=half, dc=dc: t_.matmul(
                ph1[:, :], A_wT[:, dc, :],
                xln2T[:, dc, half * 512:half * 512 + 512],
                start=(dc == 0), stop=(dc == DC - 1)),
                inc=pe if dc == DC - 1 else None)
        P.wait("scalar", pe, mh)
        m_hT[half] = P.op("scalar", lambda s, half=half: s.activation(
            hT_sb[:, half * 512:half * 512 + 512], ph1[:, :], AF.Copy),
            inc=ac)

    xrcnt = [0, 0]
    for tt in range(NT):
        slot = tt % 2
        if tt >= 2:
            P.wait("sync", dv, m2_norm[tt - 2])
        P.op("sync", lambda s, tt=tt, slot=slot: s.dma_start(
            out=xr_sb[slot][:, :], in_=x2s[tt * 128:(tt + 1) * 128, :]),
            inc=xrs[slot], k=16)
        xrcnt[slot] += 1
        if tt == 0:
            P.wait("vector", dv, m_stats_dv)
            P.wait("vector", ac, m_stats_ac)
        P.op("vector", lambda v, tt=tt, slot=slot: v.reduce_sum(
            ln2sm[:, slot:slot + 1], c_sums[:, tt, :], axis=AX.X), inc=dv)
        P.wait("vector", dv)
        P.op("vector", lambda v, tt=tt, slot=slot: v.reduce_sum(
            ln2sm[:, 2 + slot:3 + slot], c_sumsq[:, tt, :], axis=AX.X), inc=dv)
        P.wait("vector", dv)
        c_mn, c_msq, c_nv = 4 + slot, 6 + slot, 8 + slot
        P.op("vector", lambda v, slot=slot, c=c_mn: v.tensor_scalar(
            out=ln2sm[:, c:c + 1], in0=ln2sm[:, slot:slot + 1],
            scalar1=1.0 / D, scalar2=None, op0=ALU.mult), inc=dv)
        P.wait("vector", dv)
        P.op("vector", lambda v, slot=slot, c=c_msq: v.tensor_scalar(
            out=ln2sm[:, c:c + 1], in0=ln2sm[:, 2 + slot:3 + slot],
            scalar1=1.0 / D, scalar2=None, op0=ALU.mult), inc=dv)
        P.wait("vector", dv)
        m_nv = P.op("vector", lambda v, cm=c_mn, cq=c_msq, cn=c_nv:
                    v.scalar_tensor_tensor(
                        out=ln2sm[:, cn:cn + 1], in0=ln2sm[:, cm:cm + 1],
                        scalar=ln2sm[:, cm:cm + 1], in1=ln2sm[:, cq:cq + 1],
                        op0=ALU.mult, op1=ALU.subtract), inc=dv)
        P.wait("scalar", dv, m_nv)
        c_ln, c_rs = 10 + slot, 12 + slot
        P.op("scalar", lambda s, cn=c_nv, cl=c_ln: s.activation(
            ln2sm[:, cl:cl + 1], ln2sm[:, cn:cn + 1], AF.Ln,
            scale=-1.0, bias=eps_col[:, :]), inc=ac)
        P.wait("scalar", ac)
        m_rstd = P.op("scalar", lambda s, cl=c_ln, cr=c_rs: s.activation(
            ln2sm[:, cr:cr + 1], ln2sm[:, cl:cl + 1], AF.Exp, scale=-0.5),
            inc=ac)
        P.wait("vector", ac, m_rstd)
        P.wait("vector", xrs[slot], 16 * xrcnt[slot])
        if tt >= 2:
            P.wait("vector", pe, m2_tp[tt - 2])
        m2_norm[tt] = P.op("vector", lambda v, slot=slot, cm=c_mn, cr=c_rs:
                           v.tensor_scalar(
                               out=xln2_bf[slot][:, :], in0=xr_sb[slot][:, :],
                               scalar1=ln2sm[:, cm:cm + 1],
                               scalar2=ln2sm[:, cr:cr + 1],
                               op0=ALU.subtract, op1=ALU.mult), inc=dv)
        P.wait("tensor", dv, m2_norm[tt])
        for dc in range(DC):
            s4 = dc % 3
            if tp2_evac[s4] is not None:
                P.wait("tensor", tp2_evac[s4][0], tp2_evac[s4][1])
            P.op("tensor", lambda t_, slot=slot, dc=dc, s4=s4: t_.transpose(
                tp2[s4][:, :], xln2_bf[slot][:, dc * 128:(dc + 1) * 128],
                ident_bf[:, :]), inc=pe)
            eng, ctr = ("vector", dv) if dc % 2 == 0 else ("scalar", ac)
            P.wait(eng, pe)
            mv = P.op(eng, lambda e, dc=dc, s4=s4, tt=tt: _copy(
                e, xln2T[:, dc, tt * 128:(tt + 1) * 128], tp2[s4][:, :]), inc=ctr)
            tp2_evac[s4] = (ctr, mv)
        m2_tp[tt] = pe.n
        m2_tile[tt] = (dv.n, ac.n)
        if tt >= 1:
            plan_gt(tt - 1)
        if tt == 4:
            plan_hT(0)
    plan_gt(7)
    plan_hT(1)
    m_xln2T_dv = dv.n
    m_xln2T_ac = ac.n
    c2ctx.close()
    wectx = ExitStack()
    wbT = [wectx.enter_context(nc.sbuf_tensor("wbT0", [128, DC, 512], BF16)),
           wectx.enter_context(nc.sbuf_tensor("wbT1", [128, DC, 512], BF16))]
    wtcnt = [0, 0]
    P.wait("scalar", g4s, m_qg_wbase)
    P.wait("scalar", dv, m_xln2T_dv)
    P.wait("scalar", ac, m_xln2T_ac)
    for os_ in range(2):
        for dc in range(DC):
            P.op("scalar", lambda s, os_=os_, dc=dc:
                 s.dma_start_transpose(
                     wbT[os_][:, dc, :],
                     wbase_bf[os_ * 512:(os_ + 1) * 512, dc * 128:(dc + 1) * 128]),
                 inc=wts[os_], k=16)
        wtcnt[os_] += 32

    dma_tap("xln2T", lambda: xln2T[:, :, :], (128, DC, T), BF16,
            [(dv, m_xln2T_dv), (ac, m_xln2T_ac)])
    P.mark_phase("ln2")

    dpctx.close()
    dctx = ExitStack()
    d2 = lambda *a, **kw: dctx.enter_context(nc.sbuf_tensor(*a, **kw))
    gexp_sb = d2("gexp_sb", [128, NT, 8], F32)
    et_sb = d2("et_sb", [128, NT], F32)
    gsm = d2("gsm", [128, 6 * NT], F32)
    wrel_sb = d2("wrel_sb", [128, NT, 8], F32)
    wt_bf = d2("wt_bf", [128, NT, 8], BF16)
    wtT_sb = [d2("wtT0", [8, 128], BF16), d2("wtT1", [8, 128], BF16)]
    wexp_bf = [d2("wexp0", [128, 128], BF16), d2("wexp1", [128, 128], BF16)]
    pwx = [dctx.enter_context(nc.psum_tensor(f"pwx{i}", [128, 128], F32))
           for i in range(2)]
    ptw = [dctx.enter_context(nc.psum_tensor(f"ptw{i}", [8, 128], BF16))
           for i in range(2)]

    P.wait("scalar", ac, m_gtevac[7])
    mge = P.op("scalar", lambda s: s.activation(
        gexp_sb[:, :, :], gts_sb[:, :, 0:8], AF.Exp), inc=ac)
    P.wait("scalar", ac, mge)
    met = P.op("scalar", lambda s: s.activation(
        et_sb[:, :], gts_sb[:, :, 8], AF.Exp, scale=-1.0,
        bias=nbthr_col[:, 0:1]), inc=ac)
    P.wait("vector", ac, met)
    P.op("vector", lambda v: v.reduce_sum(
        gsm[:, 0:NT], gexp_sb[:, :, :], axis=AX.X), inc=dv)
    P.wait("vector", dv)
    P.op("vector", lambda v: v.reciprocal(gsm[:, NT:2 * NT], gsm[:, 0:NT]),
         inc=dv)
    P.wait("vector", dv)
    P.op("vector", lambda v: v.tensor_scalar(
        out=gsm[:, 2 * NT:3 * NT], in0=et_sb[:, :], scalar1=1.0, scalar2=None,
        op0=ALU.add), inc=dv)
    P.wait("vector", dv)
    P.op("vector", lambda v: v.reciprocal(gsm[:, 3 * NT:4 * NT],
                                          gsm[:, 2 * NT:3 * NT]), inc=dv)
    P.wait("vector", dv)
    P.op("vector", lambda v: v.tensor_scalar(
        out=gsm[:, 3 * NT:4 * NT], in0=gsm[:, 3 * NT:4 * NT],
        scalar1=MAX_THRESHOLD, scalar2=None, op0=ALU.mult), inc=dv)
    P.wait("vector", dv)
    P.op("vector", lambda v: v.tensor_tensor(
        wrel_sb[:, :, :], gexp_sb[:, :, :],
        gsm[:, NT:2 * NT].unsqueeze(2).broadcast_to([128, NT, 8]),
        op=ALU.mult), inc=dv)
    P.wait("vector", dv)
    P.op("vector", lambda v: v.tensor_tensor(
        wrel_sb[:, :, :], wrel_sb[:, :, :],
        gsm[:, 3 * NT:4 * NT].unsqueeze(2).broadcast_to([128, NT, 8]),
        op=ALU.subtract), inc=dv)
    P.wait("vector", dv)
    P.op("vector", lambda v: v.tensor_scalar(
        out=wrel_sb[:, :, :], in0=wrel_sb[:, :, :], scalar1=0.0, scalar2=None,
        op0=ALU.max), inc=dv)
    P.wait("vector", dv)
    P.op("vector", lambda v: v.reduce_sum(
        gsm[:, 4 * NT:5 * NT], wrel_sb[:, :, :], axis=AX.X), inc=dv)
    P.wait("vector", dv)
    P.op("vector", lambda v: v.tensor_scalar(
        out=gsm[:, 4 * NT:5 * NT], in0=gsm[:, 4 * NT:5 * NT], scalar1=1e-20,
        scalar2=None, op0=ALU.max), inc=dv)
    P.wait("vector", dv)
    P.op("vector", lambda v: v.reciprocal(gsm[:, 5 * NT:6 * NT],
                                          gsm[:, 4 * NT:5 * NT]), inc=dv)
    P.wait("vector", dv)
    mwt = P.op("vector", lambda v: v.tensor_tensor(
        wt_bf[:, :, :], wrel_sb[:, :, :],
        gsm[:, 5 * NT:6 * NT].unsqueeze(2).broadcast_to([128, NT, 8]),
        op=ALU.mult), inc=dv)

    m_hw = {}
    m_pwx_used = {}
    m_ptw_used = {}
    m_wtT_used = {}
    P.wait("tensor", dv, mwt)
    P.wait("tensor", po, m_E)
    for tt in range(NT):
        s2 = tt % 2
        if (tt - 2) in m_ptw_used:
            P.wait("tensor", m_ptw_used[tt - 2][0], m_ptw_used[tt - 2][1])
        mtw = P.op("tensor", lambda t_, tt=tt, s2=s2: t_.transpose(
            ptw[s2][:, :], wt_bf[:, tt, :], ident_bf[:, :]), inc=pe)
        P.wait("vector", pe, mtw)
        if (tt - 2) in m_wtT_used:
            P.wait("vector", pe, m_wtT_used[tt - 2])
        mcw = P.op("vector", lambda v, s2=s2: v.tensor_copy(
            wtT_sb[s2][0:8, :], ptw[s2][0:8, :]), inc=dv)
        m_ptw_used[tt] = (dv, mcw)
        P.wait("tensor", dv, mcw)
        if (tt - 2) in m_pwx_used:
            P.wait("tensor", ac, m_pwx_used[tt - 2])
        mwx = P.op("tensor", lambda t_, s2=s2: t_.matmul(
            pwx[s2][:, :], E_bf[0:8, :], wtT_sb[s2][0:8, :],
            start=True, stop=True), inc=pe)
        m_wtT_used[tt] = mwx
        P.wait("scalar", pe, mwx)
        mwe = P.op("scalar", lambda s, s2=s2: s.activation(
            wexp_bf[s2][:, :], pwx[s2][:, :], AF.Copy), inc=ac)
        m_pwx_used[tt] = mwe
        P.wait("vector", ac, mwe)
        if tt == 0:
            P.wait("vector", ac, m_hT[1])
        m_hw[tt] = P.op("vector", lambda v, tt=tt, s2=s2: v.tensor_mul(
            hwT_sb[:, tt, :], hT_sb[:, tt * 128:(tt + 1) * 128],
            wexp_bf[s2][:, :]), inc=dv)
    m_hw_all = dv.n
    dctx.close()

    dma_tap("hwT", lambda: hwT_sb[:, :, :], (128, 8, 128), BF16,
            [(dv, m_hw_all)])
    dma_tap("hT", lambda: hT_sb[:, :], (128, T), BF16,
            [(ac, m_hT[1])])

    P.mark_phase("ffnA")
    ectx = ExitStack()
    es = lambda *a, **kw: ectx.enter_context(nc.sbuf_tensor(*a, **kw))
    xeb = [es(f"xeb{i}", [128, 512], F32) for i in range(16)]
    yob = [es("yob0", [128, 512], F32), es("yob1", [128, 512], F32)]
    pe8 = [ectx.enter_context(nc.psum_tensor(f"pe8{i}", [128, 512], F32))
           for i in range(8)]

    xecnt = [0, 0]
    yscnt = [0, 0]
    m_wbT_used = {}
    m_xe_used = {}
    m_eevac = {}
    P.wait("sync", dv, m_hw_all)
    for tt in range(NT):
        P.op("sync", lambda s, tt=tt: s.dma_start(
            out=xeb[tt][:, :],
            in_=x2s[tt * 128:(tt + 1) * 128, 0:512]), inc=xes[0], k=16)
    xecnt[0] += 8
    for os_ in range(8):
        slot = os_ % 2
        if os_ >= 2:
            P.wait("scalar", pe, m_wbT_used[os_ - 2])
            for dc in range(DC):
                P.op("scalar", lambda s, os_=os_, dc=dc, slot=slot:
                     s.dma_start_transpose(
                         wbT[slot][:, dc, :],
                         wbase_bf[os_ * 512:(os_ + 1) * 512, dc * 128:(dc + 1) * 128]),
                     inc=wts[slot], k=16)
            wtcnt[slot] += 32
        P.wait("tensor", wts[slot], 16 * (wtcnt[slot] if os_ >= 2 else 32))
        if os_ == 0:
            P.wait("tensor", dv, m_hw_all)
            P.wait("tensor", dv, m_dprep_dv)
            P.wait("tensor", ac, m_dprep_ac)
            P.wait("tensor", g3s, m_qg_dprep)
        if os_ + 1 < 8:
            on = os_ + 1
            for tt in range(NT):
                xsl = (on % 2) * 8 + tt
                key = (on - 2, tt)
                if key in m_xe_used:
                    P.wait("sync", dv, m_xe_used[key])
                P.op("sync", lambda s, on=on, tt=tt, xsl=xsl: s.dma_start(
                    out=xeb[xsl][:, :],
                    in_=x2s[tt * 128:(tt + 1) * 128, on * 512:(on + 1) * 512]),
                    inc=xes[on % 2], k=16)
            xecnt[on % 2] += 8
        for tt in range(NT):
            xsl = (os_ % 2) * 8 + tt
            if os_ >= 1:
                P.wait("tensor", dv, m_eevac.get((os_ - 1, tt)))
            for dc in range(DC):
                P.op("tensor", lambda t_, dc=dc, tt=tt, slot=slot:
                     t_.matmul(pe8[tt][:, :],
                               xln2T[:, dc, tt * 128:(tt + 1) * 128],
                               wbT[slot][:, dc, :],
                               start=(dc == 0), stop=False))
            P.op("tensor", lambda t_, tt=tt, os_=os_: t_.matmul(
                pe8[tt][:, :], hwT_sb[:, tt, :],
                Bmat[:, os_ * 512:(os_ + 1) * 512], start=False, stop=False))
            mlast = P.op("tensor", lambda t_, tt=tt, os_=os_: t_.matmul(
                pe8[tt][:, :], ones_bf[0:1, 0:128],
                bbase_row[0:1, os_ * 512:os_ * 512 + 512],
                start=False, stop=True), inc=pe)
            psl = (os_ * 8 + tt) % 2
            P.wait("vector", pe, mlast)
            P.wait("vector", xes[os_ % 2], 16 * xecnt[os_ % 2])
            if (os_ * 8 + tt) - 2 >= 0:
                P.wait("vector", yss[psl], 16 * yscnt[psl])
            madd = P.op("vector", lambda v, tt=tt, psl=psl, xsl=xsl:
                        v.tensor_add(yob[psl][:, :], pe8[tt][:, :],
                                     xeb[xsl][:, :]), inc=dv)
            m_xe_used[(os_, tt)] = madd
            m_eevac[(os_, tt)] = madd
            P.wait("sync", dv, madd)
            P.op("sync", lambda s, tt=tt, os_=os_, psl=psl: s.dma_start(
                out=y_out[tt * 128:(tt + 1) * 128, os_ * 512:(os_ + 1) * 512],
                in_=yob[psl][:, :]), inc=yss[psl], k=16)
            yscnt[psl] += 1
        m_wbT_used[os_] = pe.n

    if "wbase0" in taps:
        ap = tapout("wbase0", (512, D), BF16)
        P.wait("scalar", g4s, m_qg_wbase)
        P.op("scalar", lambda s, ap=ap: s.dma_start(out=ap, in_=wbase_bf[0:512, :]),
             inc=tps, k=16)
    dma_tap("Bmat", lambda: Bmat[:, :], (128, D), BF16,
            [(dv, m_dprep_dv), (ac, m_dprep_ac)])
    dma_tap("bbase", lambda: bbase_row[0:1, :], (1, D), BF16, [(dv, m_hw_all)])
    P.wait("sync", yss[0], 16 * yscnt[0])
    P.wait("sync", yss[1], 16 * yscnt[1])
    if tps.n:
        P.wait("sync", tps, tps.n)
    ectx.close()
    wectx.close()
    dectx.close()

    P.emit(stop_at=stop_at)
    return nc, tap_outs


def kernel(**inputs):
    x = np.asarray(inputs["L0_attn_inputs"], dtype=np.float32).reshape(8192, D)
    xpad = np.zeros((8194, D), dtype=np.float32)
    xpad[1:8193] = x

    def f32c(a, shape=None):
        a = np.ascontiguousarray(np.asarray(a, dtype=np.float32))
        return a.reshape(shape) if shape is not None else a

    common = {
        "conv_w": f32c(inputs["L0_attn_conv_w"]),
        "conv_b": f32c(inputs["L0_attn_conv_b"]),
        "wq": f32c(inputs["L0_attn_wq"], (1024, 1024)),
        "bq": f32c(inputs["L0_attn_bq"], (1024,)),
        "wk": f32c(inputs["L0_attn_wk"], (1024, 1024)),
        "bk": f32c(inputs["L0_attn_bk"], (1024,)),
        "wv": f32c(inputs["L0_attn_wv"], (1024, 1024)),
        "bv": f32c(inputs["L0_attn_bv"], (1024,)),
        "wo": f32c(inputs["L0_attn_wo"], (1024, 1024)),
        "bo": f32c(inputs["L0_attn_bo"], (1024,)),
        "proj_w": f32c(inputs["L0_attn_proj_w"]),
        "proj_b": f32c(inputs["L0_attn_proj_b"]),
        "W_base": f32c(inputs["L0_ffn_W_base"]),
        "b_base": f32c(inputs["L0_ffn_b_base"]),
        "W_gate": f32c(inputs["L0_ffn_W_gate"]),
        "W_thr": f32c(inputs["L0_ffn_W_thr"]),
        "b_thr": f32c(inputs["L0_ffn_b_thr"]),
        "A_w": f32c(inputs["L0_ffn_A_w"], (128, D)),
        "B_w": f32c(inputs["L0_ffn_B_w"]),
    }
    in_maps = []
    for c in range(8):
        m = dict(common)
        m["x_in"] = np.ascontiguousarray(xpad[c * 1024:c * 1024 + 1026])
        in_maps.append(m)

    nc, _ = build_program()
    res = run_bass_kernel_spmd(nc, in_maps, core_ids=list(range(8)))
    y = np.concatenate([res.results[c]["y"] for c in range(8)], axis=0)
    return y.reshape(1, 8192, D).astype(np.float32)


# version 11
# speedup vs baseline: 1.0374x; 1.0374x over previous
import numpy as np
from contextlib import ExitStack

import concourse.bass as bass
import concourse.mybir as mybir
from concourse.bass_utils import run_bass_kernel_spmd

F32 = mybir.dt.float32
BF16 = mybir.dt.bfloat16
AF = mybir.ActivationFunctionType
ALU = mybir.AluOpType
AX = mybir.AxisListType

T = 1024
NT = 8
D = 4096
DC = 32
LN_EPS = 1e-5
NEG_INF = -1.0e9
SCALE_ATTN = 0.125
MOE_SCALING = 2.0
MAX_THRESHOLD = 0.125


def _copy(e, out, in_):
    if hasattr(e, "tensor_copy"):
        return e.tensor_copy(out, in_)
    return e.copy(out, in_)


class Ctr:
    __slots__ = ("h", "n")

    def __init__(self, h):
        self.h = h
        self.n = 0


class Prog:

    ENGINES = ("tensor", "vector", "scalar", "gpsimd", "sync")

    def __init__(self, nc, ctx):
        self.nc = nc
        self.ctx = ctx
        self.items = {e: [] for e in self.ENGINES}
        self.phase_marks = {}

    def mark_phase(self, name):
        self.phase_marks[name] = {e: len(self.items[e]) for e in self.ENGINES}

    def mksem(self, name):
        return Ctr(self.ctx.enter_context(self.nc.semaphore(name)))

    def op(self, eng, fn, inc=None, k=1):
        if inc is not None:
            inc.n += k
            sem, kk = inc.h, k

            def run(e, fn=fn, sem=sem, kk=kk):
                fn(e).then_inc(sem, kk)
            self.items[eng].append(run)
            return inc.n
        self.items[eng].append(lambda e, fn=fn: fn(e))
        return None

    def wait(self, eng, ctr, val=None):
        v = ctr.n if val is None else val
        if v is None or v <= 0:
            return
        self.items[eng].append(lambda e, h=ctr.h, v=v: e.wait_ge(h, v))

    def emit(self, stop_at=None):
        cut = self.phase_marks.get(stop_at) if stop_at else None
        with self.nc.Block() as block:
            for name in self.ENGINES:
                items = self.items[name]
                if cut is not None:
                    items = items[:cut[name]]
                if not items:
                    continue

                def body(eng, items=items):
                    for run in items:
                        run(eng)
                getattr(block, name)(body)


SERIALIZE_Y2 = False


def build_program(taps=(), stop_at=None):
    nc = bass.Bass("TRN2", target_bir_lowering=False)

    def din(name, shape):
        return nc.dram_tensor(name, list(shape), F32, kind="ExternalInput").ap()

    x_in = din("x_in", (T + 2, D))
    conv_w = din("conv_w", (3, D, 3072))
    conv_b = din("conv_b", (3072,))
    wq_d = din("wq", (1024, 1024))
    bq_d = din("bq", (1024,))
    wk_d = din("wk", (1024, 1024))
    bk_d = din("bk", (1024,))
    wv_d = din("wv", (1024, 1024))
    bv_d = din("bv", (1024,))
    wo_d = din("wo", (1024, 1024))
    bo_d = din("bo", (1024,))
    proj_w = din("proj_w", (1024, D))
    proj_b = din("proj_b", (D,))
    W_base = din("W_base", (D, D))
    b_base = din("b_base", (D,))
    W_gate = din("W_gate", (8, D))
    W_thr = din("W_thr", (D,))
    b_thr = din("b_thr", (1,))
    A_w = din("A_w", (128, D))
    B_w = din("B_w", (8, D, 16))

    y_out = nc.dram_tensor("y", [T, D], F32, kind="ExternalOutput").ap()
    x2s = nc.dram_tensor("x2s", [T, D], F32, kind="Internal").ap()
    wbase_bf = nc.dram_tensor("wbase_bf", [D, D], BF16, kind="Internal").ap()
    bwf_dram = nc.dram_tensor("bwf_dram", [128, D], F32, kind="Internal").ap()

    tap_outs = {}

    def tapout(name, shape, dtype):
        ap = nc.dram_tensor("tap_" + name, list(shape), dtype,
                            kind="ExternalOutput").ap()
        tap_outs[name] = ap
        return ap

    ctx = ExitStack()
    nc.ctx.enter_context(ctx)
    P = Prog(nc, ctx)

    pe = P.mksem("pe")
    dv = P.mksem("dv")
    ac = P.mksem("ac")
    po = P.mksem("po")
    qg = P.mksem("qg")
    qs = P.mksem("qs")
    qc = P.mksem("qc")
    cw = tuple(P.mksem(f"cw{i}") for i in range(4))
    xts = (P.mksem("xt0"), P.mksem("xt1"))
    wbs = (P.mksem("wb0"), P.mksem("wb1"))
    pws = (P.mksem("pw0"), P.mksem("pw1"))
    xcs = tuple(P.mksem(f"xc{i}") for i in range(4))
    x2st = (P.mksem("x2sa"), P.mksem("x2sb"))
    xrs = (P.mksem("xr0"), P.mksem("xr1"))
    xes = (P.mksem("xe0"), P.mksem("xe1"))
    wts = (P.mksem("wt0"), P.mksem("wt1"))
    yss = (P.mksem("ys0"), P.mksem("ys1"))
    tps = P.mksem("tps")

    sb = lambda *a, **kw: ctx.enter_context(nc.sbuf_tensor(*a, **kw))
    psum = lambda c, *a: c.enter_context(nc.psum_tensor(*a))

    ident_bf = sb("ident_bf", [128, 128], BF16)
    ident_f = sb("ident_f", [128, 128], F32)
    ones_bf = sb("ones_bf", [1, 128], BF16)
    ones_f = sb("ones_f", [1, 128], F32)
    convb_col = sb("convb_col", [128, 24], F32)
    bqs_col = sb("bqs_col", [128, 8], F32)
    bk_col = sb("bk_col", [128, 8], F32)
    bo_col = sb("bo_col", [128, 8], F32)
    nbthr_col = sb("nbthr_col", [128, 1], F32)
    eps_col = sb("eps_col", [128, 1], F32)
    c_sums = sb("c_sums", [128, NT, 8], F32)
    c_sumsq = sb("c_sumsq", [128, NT, 8], F32)

    def dma_tap(name, src_fn, shape, dtype, waits):
        if name not in taps:
            return
        ap = tapout(name, shape, dtype)
        for (ctr, v) in waits:
            P.wait("scalar", ctr, v)
        P.op("scalar", lambda s, ap=ap: s.dma_start(out=ap, in_=src_fn()),
             inc=tps, k=16)

    pctx = ExitStack()
    stage = pctx.enter_context(nc.sbuf_tensor("stage_row", [1, 3072], F32, side="right"))
    tp_f = (pctx.enter_context(nc.psum_tensor("tp_fa", [128, 128], F32)),
            pctx.enter_context(nc.psum_tensor("tp_fb", [128, 128], F32)))

    P.op("gpsimd", lambda g: g.memset(ident_bf[:, :], 1.0), inc=po)
    P.wait("gpsimd", po)
    P.op("gpsimd", lambda g: g.affine_select(
        out=ident_bf[:, :], in_=ident_bf[:, :], pattern=[[-1, 128]],
        compare_op=ALU.is_equal, fill=0.0, base=0, channel_multiplier=1), inc=po)
    P.op("gpsimd", lambda g: g.memset(ident_f[:, :], 1.0), inc=po)
    P.wait("gpsimd", po)
    P.op("gpsimd", lambda g: g.affine_select(
        out=ident_f[:, :], in_=ident_f[:, :], pattern=[[-1, 128]],
        compare_op=ALU.is_equal, fill=0.0, base=0, channel_multiplier=1), inc=po)
    P.op("gpsimd", lambda g: g.memset(ones_bf[:, :], 1.0), inc=po)
    P.op("gpsimd", lambda g: g.memset(ones_f[:, :], 1.0), inc=po)
    P.op("gpsimd", lambda g: g.memset(eps_col[:, :], LN_EPS), inc=po)
    m_po_const = po.n

    rows = [
        (conv_b, 3072, convb_col, 24),
        (bq_d, 1024, bqs_col, 8),
        (bk_d, 1024, bk_col, 8),
        (bo_d, 1024, bo_col, 8),
    ]
    for src, n, dst_col, nch in rows:
        P.wait("sync", pe)
        P.op("sync", lambda s, src=src, n=n:
             s.dma_start(out=stage[0:1, 0:n], in_=src[None, :]), inc=qs, k=16)
        P.wait("tensor", qs)
        P.wait("tensor", po, m_po_const)
        for c in range(nch):
            slot = tp_f[c % 2]
            if dv.n >= 2:
                P.wait("tensor", dv, dv.n - 1)
            P.op("tensor", lambda t, c=c, slot=slot:
                 t.transpose(slot[:, 0:1], stage[0:1, c * 128:(c + 1) * 128],
                             ident_f[0:1, 0:1]), inc=pe)
            P.wait("vector", pe)
            P.op("vector", lambda v, c=c, slot=slot, dst_col=dst_col:
                 v.tensor_copy(dst_col[:, c:c + 1], slot[:, 0:1]), inc=dv)
    P.wait("vector", dv)
    P.op("vector", lambda v: v.tensor_scalar(
        out=bqs_col[:, :], in0=bqs_col[:, :], scalar1=SCALE_ATTN,
        scalar2=None, op0=ALU.mult), inc=dv)

    P.wait("sync", pe)
    P.op("sync", lambda s: s.dma_start(out=stage[0:1, 0:1], in_=b_thr[None, :]),
         inc=qs, k=16)
    P.wait("tensor", qs)
    P.wait("tensor", dv)
    P.op("tensor", lambda t: t.matmul(tp_f[0][:, 0:1], ones_f[0:1, 0:128],
                                      stage[0:1, 0:1], start=True, stop=True),
         inc=pe)
    P.wait("vector", pe)
    P.op("vector", lambda v: v.tensor_scalar(
        out=nbthr_col[:, :], in0=tp_f[0][:, 0:1], scalar1=-1.0, scalar2=None,
        op0=ALU.mult), inc=dv)
    m_prologue_pe = pe.n

    P.mark_phase("prologue")
    abctx = ExitStack()
    asb = lambda *a, **kw: abctx.enter_context(nc.sbuf_tensor(*a, **kw))
    qkvT = asb("qkvT", [128, 24, T], BF16)

    axctx = ExitStack()
    xlnT = axctx.enter_context(nc.sbuf_tensor("xlnT", [128, DC, T + 2], BF16))
    cwp = [axctx.enter_context(nc.sbuf_tensor(f"cwp{i}", [128, 3, 512], BF16))
           for i in range(4)]
    actx = ExitStack()
    lsb = lambda *a, **kw: actx.enter_context(nc.sbuf_tensor(*a, **kw))
    xt = [lsb("xt0s", [128, D], F32), lsb("xt1s", [128, D], F32)]
    dump1 = lsb("dump1", [128, D], BF16)
    xln_bf = [lsb("xlnbf0", [128, D], BF16), lsb("xlnbf1", [128, D], BF16)]
    lnsm = lsb("lnsm", [128, 16], F32)
    tpb = [actx.enter_context(nc.psum_tensor(f"tpb{i}", [128, 128], BF16))
           for i in range(4)]

    m_norm = {}
    m_tp = {}
    tp_evac = [None] * 4
    m_xlnT_dv = 0
    m_xlnT_ac = 0
    xcnt = [0, 0]
    for t in range(9):
        slot = t % 2
        nrow = 128 if t < 8 else 2
        r0 = t * 128 if t < 8 else 1024
        col0 = r0
        if t >= 2:
            P.wait("sync", dv, m_norm[t - 2])
        else:
            P.wait("sync", dv)
        P.op("sync", lambda s, slot=slot, r0=r0, nrow=nrow:
             s.dma_start(out=xt[slot][0:nrow, :], in_=x_in[r0:r0 + nrow, :]),
             inc=xts[slot], k=16)
        xcnt[slot] += 1
        P.wait("vector", xts[slot], 16 * xcnt[slot])
        m_s1 = P.op("vector", lambda v, slot=slot, nrow=nrow: v.tensor_scalar(
            out=dump1[0:nrow, :], in0=xt[slot][0:nrow, :], scalar1=1.0,
            scalar2=None, op0=ALU.mult, op1=ALU.add,
            accum_out=lnsm[0:nrow, slot:slot + 1]), inc=dv)
        P.wait("scalar", xts[slot], 16 * xcnt[slot])
        if t >= 2:
            P.wait("scalar", pe, m_tp[t - 2])
        m_s2 = P.op("scalar", lambda s, slot=slot, nrow=nrow: s.activation(
            xln_bf[slot][0:nrow, :], xt[slot][0:nrow, :], AF.Square,
            accum_out=lnsm[0:nrow, 2 + slot:3 + slot]), inc=ac)
        P.wait("vector", dv, m_s1)
        P.wait("vector", ac, m_s2)
        c_mn, c_msq, c_nv = 4 + slot, 6 + slot, 8 + slot
        P.op("vector", lambda v, slot=slot, nrow=nrow, c=c_mn: v.tensor_scalar(
            out=lnsm[0:nrow, c:c + 1], in0=lnsm[0:nrow, slot:slot + 1],
            scalar1=1.0 / D, scalar2=None, op0=ALU.mult), inc=dv)
        P.wait("vector", dv)
        P.op("vector", lambda v, slot=slot, nrow=nrow, c=c_msq: v.tensor_scalar(
            out=lnsm[0:nrow, c:c + 1], in0=lnsm[0:nrow, 2 + slot:3 + slot],
            scalar1=1.0 / D, scalar2=None, op0=ALU.mult), inc=dv)
        P.wait("vector", dv)
        m_nv = P.op("vector", lambda v, nrow=nrow, cm=c_mn, cq=c_msq, cn=c_nv:
                    v.scalar_tensor_tensor(
                        out=lnsm[0:nrow, cn:cn + 1], in0=lnsm[0:nrow, cm:cm + 1],
                        scalar=lnsm[0:nrow, cm:cm + 1], in1=lnsm[0:nrow, cq:cq + 1],
                        op0=ALU.mult, op1=ALU.subtract), inc=dv)
        P.wait("scalar", dv, m_nv)
        c_ln, c_rs = 10 + slot, 12 + slot
        P.op("scalar", lambda s, nrow=nrow, cn=c_nv, cl=c_ln: s.activation(
            lnsm[0:nrow, cl:cl + 1], lnsm[0:nrow, cn:cn + 1], AF.Ln,
            scale=-1.0, bias=eps_col[0:nrow, :]), inc=ac)
        P.wait("scalar", ac)
        m_rstd = P.op("scalar", lambda s, nrow=nrow, cl=c_ln, cr=c_rs:
                      s.activation(lnsm[0:nrow, cr:cr + 1],
                                   lnsm[0:nrow, cl:cl + 1], AF.Exp, scale=-0.5),
                      inc=ac)
        P.wait("vector", ac, m_rstd)
        m_norm[t] = P.op("vector", lambda v, slot=slot, nrow=nrow, cm=c_mn, cr=c_rs:
                         v.tensor_scalar(
                             out=xln_bf[slot][0:nrow, :], in0=xt[slot][0:nrow, :],
                             scalar1=lnsm[0:nrow, cm:cm + 1],
                             scalar2=lnsm[0:nrow, cr:cr + 1],
                             op0=ALU.subtract, op1=ALU.mult), inc=dv)
        P.wait("tensor", dv, m_norm[t])
        for dc in range(DC):
            s4 = dc % 4
            if tp_evac[s4] is not None:
                P.wait("tensor", tp_evac[s4][0], tp_evac[s4][1])
            P.op("tensor", lambda t_, slot=slot, nrow=nrow, dc=dc, s4=s4:
                 t_.transpose(tpb[s4][:, 0:nrow],
                              xln_bf[slot][0:nrow, dc * 128:(dc + 1) * 128],
                              ident_bf[0:nrow, 0:nrow]), inc=pe)
            eng, ctr = ("vector", dv) if dc % 2 == 0 else ("scalar", ac)
            P.wait(eng, pe)
            mv = P.op(eng, lambda e, dc=dc, s4=s4, col0=col0, nrow=nrow:
                      _copy(e, xlnT[:, dc, col0:col0 + nrow],
                            tpb[s4][:, 0:nrow]), inc=ctr)
            tp_evac[s4] = (ctr, mv)
        m_tp[t] = pe.n
    m_xlnT_dv = dv.n
    m_xlnT_ac = ac.n

    dma_tap("xlnT", lambda: xlnT[:, :, :], (128, DC, T + 2), BF16,
            [(dv, m_xlnT_dv), (ac, m_xlnT_ac)])
    P.mark_phase("ln1")

    actx.close()
    pctx.close()
    cctx = ExitStack()
    pc = [cctx.enter_context(nc.psum_tensor(f"pc{i}", [128, 512], F32))
          for i in range(8)]

    P.wait("tensor", dv, m_xlnT_dv)
    P.wait("tensor", ac, m_xlnT_ac)
    m_piece = {}
    m_conv_evac = [ (None, None) ] * 6
    conv_bank_evac = {}
    cwcnt = [0, 0, 0, 0]
    for ms in range(6):
        for k in range(DC):
            p = ms * DC + k
            slot = p % 4
            if p >= 4:
                P.wait("gpsimd", pe, m_piece[p - 4])
            P.op("gpsimd", lambda g, k=k, ms=ms, slot=slot: g.dma_start(
                out=cwp[slot][:, :, :],
                in_=conv_w[:, k * 128:(k + 1) * 128,
                           ms * 512:(ms + 1) * 512].rearrange("w d m -> d w m")),
                inc=cw[slot], k=16)
            cwcnt[slot] += 1
            P.wait("tensor", cw[slot], 16 * cwcnt[slot])
            for w in range(3):
                for msub in range(4):
                    for half in range(2):
                        last_of_piece = (w == 2 and msub == 3 and half == 1)
                        bank = msub * 2 + half
                        if k == 0 and w == 0 and ms >= 1:
                            g = conv_bank_evac.get((ms - 1, bank))
                            if g is not None:
                                P.wait("tensor", g[0], g[1])
                        P.op("tensor", lambda t_, k=k, w=w, msub=msub,
                             half=half, slot=slot:
                             t_.matmul(pc[msub * 2 + half][:, :],
                                       cwp[slot][:, w, msub * 128:(msub + 1) * 128],
                                       xlnT[:, k, 512 * half + w: 512 * half + w + 512],
                                       start=(k == 0 and w == 0),
                                       stop=(k == DC - 1 and w == 2)),
                             inc=pe if last_of_piece else None)
            m_piece[p] = pe.n
        m_group = pe.n
        for msub in range(4):
            for half in range(2):
                mchunk = ms * 4 + msub
                bank = msub * 2 + half
                tile = pc[bank]
                if msub < 2:
                    P.wait("vector", pe, m_group)
                    mv = P.op("vector", lambda v, tile=tile, mchunk=mchunk, half=half:
                              v.tensor_scalar(
                                  out=qkvT[:, mchunk, 512 * half:512 * half + 512],
                                  in0=tile[:, :], scalar1=convb_col[:, mchunk:mchunk + 1],
                                  scalar2=None, op0=ALU.add), inc=dv)
                    conv_bank_evac[(ms, bank)] = (dv, mv)
                else:
                    P.wait("scalar", pe, m_group)
                    mv = P.op("scalar", lambda s, tile=tile, mchunk=mchunk, half=half:
                              s.activation(
                                  qkvT[:, mchunk, 512 * half:512 * half + 512],
                                  tile[:, :], AF.Identity,
                                  bias=convb_col[:, mchunk:mchunk + 1]), inc=ac)
                    conv_bank_evac[(ms, bank)] = (ac, mv)
        m_conv_evac[ms] = (dv.n, ac.n)
    m_qkvT_dv, m_qkvT_ac = dv.n, ac.n
    m_conv_done_pe = pe.n

    dma_tap("qkvT", lambda: qkvT[:, :, :], (128, 24, T), BF16,
            [(dv, m_qkvT_dv), (ac, m_qkvT_ac)])
    P.mark_phase("conv")

    cctx.close()
    axctx.close()

    wbctx = ExitStack()
    wbuf = [wbctx.enter_context(nc.sbuf_tensor("wbuf0", [128, 8, 1024], BF16, side="right")),
            wbctx.enter_context(nc.sbuf_tensor("wbuf1", [128, 8, 1024], BF16, side="right"))]
    wbcnt = [0, 0]
    P.op("gpsimd", lambda g: g.dma_start(
        out=wbuf[0][:, :, :],
        in_=wq_d.rearrange("(c p) e -> p c e", p=128)), inc=wbs[0], k=16)
    wbcnt[0] += 1
    P.op("gpsimd", lambda g: g.dma_start(
        out=wbuf[1][:, :, :],
        in_=wk_d.rearrange("(c p) e -> p c e", p=128)), inc=wbs[1], k=16)
    wbcnt[1] += 1

    bctx = ExitStack()
    bsb = lambda *a, **kw: bctx.enter_context(nc.sbuf_tensor(*a, **kw))
    qhT = bsb("qhT", [128, 8, T], BF16)
    khT = bsb("khT", [128, 8, T], BF16)
    vh = bsb("vh", [128, 8, 1024], BF16)

    bqkctx = ExitStack()
    pq = [bqkctx.enter_context(nc.psum_tensor(f"pq{i}", [128, 512], F32))
          for i in range(4)]
    bvrow = bqkctx.enter_context(nc.sbuf_tensor("bvrow", [1, 1024], BF16))
    g1 = P.mksem("g1")
    qg_wo_sem = g1
    m_qkevac = {}
    for proj, (wslot, dstT, bias_col, scale) in enumerate(
            ((0, qhT, bqs_col, SCALE_ATTN), (1, khT, bk_col, 1.0))):
        P.wait("tensor", wbs[wslot], 16 * wbcnt[wslot])
        if proj == 0:
            P.wait("tensor", dv, m_qkvT_dv)
            P.wait("tensor", ac, m_qkvT_ac)
        for hc in range(8):
            for half in range(2):
                tile = pq[(hc % 2) * 2 + half]
                key = (proj, hc - 2, half)
                if key in m_qkevac:
                    P.wait("tensor", ac, m_qkevac[key])
                mlast = None
                for kc in range(8):
                    mlast = P.op("tensor", lambda t_, tile=tile, wslot=wslot,
                                 kc=kc, hc=hc, half=half:
                                 t_.matmul(tile[:, :],
                                           wbuf[wslot][:, kc, hc * 128:(hc + 1) * 128],
                                           qkvT[:, (0 if wslot == 0 else 8) + kc,
                                                half * 512:half * 512 + 512],
                                           start=(kc == 0), stop=(kc == 7)),
                                 inc=pe if kc == 7 else None)
                P.wait("scalar", pe, mlast)
                m_qkevac[(proj, hc, half)] = P.op(
                    "scalar", lambda s, tile=tile, dstT=dstT, hc=hc, half=half,
                    bias_col=bias_col, scale=scale:
                    s.activation(dstT[:, hc, half * 512:half * 512 + 512],
                                 tile[:, :], AF.Identity, scale=scale,
                                 bias=bias_col[:, hc:hc + 1]), inc=ac)
        if proj == 0:
            m_qproj_pe = pe.n
    m_qk_ac = ac.n

    P.wait("gpsimd", pe, m_qproj_pe)
    P.op("gpsimd", lambda g: g.dma_start(
        out=wbuf[0][:, :, :],
        in_=wv_d.rearrange("(c p) e -> p c e", p=128)), inc=wbs[0], k=16)
    wbcnt[0] += 1
    P.op("gpsimd", lambda g: g.dma_start(out=bvrow[0:1, :], in_=bv_d[None, :]),
         inc=g1, k=16)
    m_qg_bv = g1.n
    for e_ in range(8):
        P.op("sync", lambda s, e_=e_: s.dma_start(
            out=bwf_dram.rearrange("p (c e r) -> p c e r", e=8, r=16)[:, :, e_, :],
            in_=B_w[e_, :, :].rearrange("(c p) r -> p c r", p=128)),
            inc=qc, k=16)
    m_qc_relay = qc.n

    m_vevac = {}
    P.wait("tensor", wbs[0], 16 * wbcnt[0])
    P.wait("tensor", g1, m_qg_bv)
    P.wait("tensor", ac, m_qk_ac)
    for tt in range(NT):
        for hh in range(2):
            tile = pq[(tt % 2) * 2 + hh]
            key = (tt - 2, hh)
            if key in m_vevac:
                P.wait("tensor", dv, m_vevac[key])
            for kc in range(8):
                P.op("tensor", lambda t_, tile=tile, kc=kc, tt=tt, hh=hh:
                     t_.matmul(tile[:, :],
                               qkvT[:, 16 + kc, tt * 128:(tt + 1) * 128],
                               wbuf[0][:, kc, hh * 512:hh * 512 + 512],
                               start=(kc == 0), stop=False))
            mlast = P.op("tensor", lambda t_, tile=tile, hh=hh:
                         t_.matmul(tile[:, :], ones_bf[0:1, 0:128],
                                   bvrow[0:1, hh * 512:hh * 512 + 512],
                                   start=False, stop=True), inc=pe)
            P.wait("vector", pe, mlast)
            m_vevac[(tt, hh)] = P.op(
                "vector", lambda v, tile=tile, tt=tt, hh=hh:
                v.tensor_copy(vh[:, tt, hh * 512:hh * 512 + 512], tile[:, :]),
                inc=dv)
    m_vh_dv = dv.n
    m_vproj_pe = pe.n
    bqkctx.close()
    wbctx.close()

    P.mark_phase("qkv")
    dma_tap("qhT", lambda: qhT[:, :, :], (128, 8, T), BF16, [(ac, m_qk_ac)])
    dma_tap("khT", lambda: khT[:, :, :], (128, 8, T), BF16, [(ac, m_qk_ac)])
    dma_tap("vh", lambda: vh[:, :, :], (128, 8, 1024), BF16, [(dv, m_vh_dv)])
    P.mark_phase("qkv_taps")

    octx = ExitStack()
    oT = octx.enter_context(nc.sbuf_tensor("oT", [128, 8, T], BF16))
    wo_sb = octx.enter_context(nc.sbuf_tensor("wo_sb", [128, 8, 1024], BF16))
    P.wait("gpsimd", pe, m_conv_done_pe)
    P.op("gpsimd", lambda g: g.dma_start(
        out=wo_sb[:, :, :],
        in_=wo_d.rearrange("(c p) e -> p c e", p=128)), inc=g1, k=16)
    m_qg_wo = g1.n

    b2ctx = ExitStack()
    a2 = lambda *a, **kw: b2ctx.enter_context(nc.sbuf_tensor(*a, **kw))
    mask_bf = a2("mask_bf", [128, 128], BF16)
    Pf = [a2("Pf0", [128, 512], F32), a2("Pf1", [128, 512], F32)]
    Pbf = [a2("Pbf0", [128, 512], BF16), a2("Pbf1", [128, 512], BF16)]
    PT_sb = [a2("PTsb0", [128, 128], BF16), a2("PTsb1", [128, 128], BF16)]
    att_sm = a2("att_sm", [128, 24], F32)
    SpB = b2ctx.enter_context(nc.psum_tensor("SpB", [128, 2048], F32))
    PTp = [b2ctx.enter_context(nc.psum_tensor(f"PTp{i}", [128, 128], BF16))
           for i in range(2)]
    OTp = [b2ctx.enter_context(nc.psum_tensor(f"OTp{i}", [128, 128], F32))
           for i in range(2)]

    P.op("gpsimd", lambda g: g.memset(mask_bf[:, :], 0.0), inc=po)
    P.wait("gpsimd", po)
    m_mask = P.op("gpsimd", lambda g: g.affine_select(
        out=mask_bf[:, :], in_=mask_bf[:, :], pattern=[[-1, 128]],
        compare_op=ALU.is_ge, fill=NEG_INF, base=0, channel_multiplier=1),
        inc=po)
    g4s = P.mksem("g4s")
    for i in range(8):
        P.op("gpsimd", lambda g, i=i: g.dma_start(
            out=wbase_bf[i * 512:(i + 1) * 512, :],
            in_=W_base[i * 512:(i + 1) * 512, :]), inc=g4s, k=16)
    m_qg_wbase = g4s.n

    m_spb_used = {}
    m_pbf_used = [None, None]
    m_ptevac = [None, None]
    m_ptpevac = [None, None]
    m_otevac = [None, None]
    P.wait("tensor", ac, m_qk_ac)
    P.wait("tensor", dv, m_vh_dv)
    P.wait("tensor", po, m_mask)
    groups = [(b, hg) for b in range(NT) for hg in range(4)]
    gstate = {}

    def attn_p1a(g):
        b, hg = groups[g]
        gs = g % 2
        if g >= 1:
            P.wait("tensor", ac, m_spb_used[g - 1])
        mS = None
        for i in range(4):
            h = 4 * hg + i
            ch, hf = h // 2, h % 2
            P.op("tensor", lambda t_, i=i, ch=ch, hf=hf, b=b:
                 t_.matmul(SpB[:, 512 * i:512 * i + 128],
                           qhT[64 * hf:64 * hf + 64, ch, 128 * b:128 * b + 128],
                           khT[64 * hf:64 * hf + 64, ch, 128 * b:128 * b + 128],
                           start=True, stop=False,
                           tile_position=(64 * hf, 0)))
            mS = P.op("tensor", lambda t_, i=i:
                      t_.matmul(SpB[:, 512 * i:512 * i + 128],
                                ident_bf[:, :], mask_bf[:, :],
                                start=False, stop=True), inc=pe)
        P.wait("vector", pe, mS)
        mneg = P.op("vector", lambda v, gs=gs: v.reduce_max(
            att_sm[:, 12 * gs:12 * gs + 4],
            SpB.rearrange("p (i k) -> p i k", i=4)[:, :, 0:128],
            axis=AX.X, negate=True), inc=dv)
        P.wait("scalar", dv, mneg)
        mexp = None
        for i in range(4):
            mexp = P.op("scalar", lambda s, gs=gs, i=i: s.activation(
                Pf[gs][:, 128 * i:128 * (i + 1)],
                SpB[:, 512 * i:512 * i + 128], AF.Exp,
                bias=att_sm[:, 12 * gs + i:12 * gs + i + 1],
                accum_out=att_sm[:, 12 * gs + 4 + i:12 * gs + 5 + i]),
                inc=ac if i == 3 else None)
        m_spb_used[g] = mexp
        gstate[g] = mexp

    def attn_p1b(g):
        gs = g % 2
        mexp = gstate.pop(g)
        P.wait("vector", ac, mexp)
        mri = P.op("vector", lambda v, gs=gs: v.reciprocal(
            att_sm[:, 12 * gs + 8:12 * gs + 12],
            att_sm[:, 12 * gs + 4:12 * gs + 8]), inc=dv)
        P.wait("vector", dv, mri)
        if m_pbf_used[gs] is not None:
            P.wait("vector", pe, m_pbf_used[gs])
        mpn = None
        for i in range(4):
            mpn = P.op("vector", lambda v, gs=gs, i=i: v.tensor_scalar(
                out=Pbf[gs][:, 128 * i:128 * (i + 1)],
                in0=Pf[gs][:, 128 * i:128 * (i + 1)],
                scalar1=att_sm[:, 12 * gs + 8 + i:12 * gs + 9 + i],
                scalar2=None, op0=ALU.mult), inc=dv if i == 3 else None)
        gstate[(g, "pn")] = mpn

    def attn_p2(g):
        b, hg = groups[g]
        gs = g % 2
        mpn = gstate.pop((g, "pn"))
        P.wait("tensor", dv, mpn)
        mo_bank = [None, None]
        mt = None
        for i in range(4):
            h = 4 * hg + i
            ps_slot = i % 2
            ot_bank = i // 2
            if m_ptpevac[ps_slot] is not None:
                P.wait("tensor", m_ptpevac[ps_slot][0], m_ptpevac[ps_slot][1])
            mt = P.op("tensor", lambda t_, gs=gs, i=i, ps_slot=ps_slot:
                      t_.transpose(PTp[ps_slot][:, :],
                                   Pbf[gs][:, 128 * i:128 * (i + 1)],
                                   ident_bf[:, :]), inc=pe)
            if m_ptevac[ps_slot] is not None:
                P.wait("vector", m_ptevac[ps_slot][0], m_ptevac[ps_slot][1])
            P.wait("vector", pe, mt)
            mc = P.op("vector", lambda v, ps_slot=ps_slot:
                      v.tensor_copy(PT_sb[ps_slot][:, :], PTp[ps_slot][:, :]),
                      inc=dv)
            m_ptpevac[ps_slot] = (dv, mc)
            P.wait("tensor", dv, mc)
            if i % 2 == 0 and m_otevac[ot_bank] is not None:
                P.wait("tensor", m_otevac[ot_bank][0], m_otevac[ot_bank][1])
            if i % 2 == 1:
                P.wait("tensor", pe, mo_bank[ot_bank])
            mo = P.op("tensor", lambda t_, i=i, h=h, b=b, ps_slot=ps_slot,
                      ot_bank=ot_bank:
                      t_.matmul(OTp[ot_bank][64 * (i % 2):64 * (i % 2) + 64, :],
                                vh[:, b, 64 * h:64 * h + 64],
                                PT_sb[ps_slot][:, :],
                                start=True, stop=True,
                                tile_position=(0, 64 * (i % 2))),
                      inc=pe)
            mo_bank[ot_bank] = mo
            m_ptevac[ps_slot] = (pe, mo)
        m_pbf_used[gs] = mt
        for j in range(2):
            P.wait("scalar", pe, mo_bank[j])
            mev = P.op("scalar", lambda s, hg=hg, b=b, j=j:
                       s.activation(oT[:, 2 * hg + j, 128 * b:128 * b + 128],
                                    OTp[j][:, :], AF.Copy), inc=ac)
            m_otevac[j] = (ac, mev)

    for g in range(len(groups)):
        attn_p1a(g)
        if g >= 1:
            attn_p2(g - 1)
        attn_p1b(g)
    attn_p2(len(groups) - 1)
    m_oT_ac = ac.n
    b2ctx.close()

    dma_tap("oT", lambda: oT[:, :, :], (128, 8, T), BF16, [(ac, m_oT_ac)])
    P.mark_phase("attn")

    cc = ExitStack()
    csb = lambda *a, **kw: cc.enter_context(nc.sbuf_tensor(*a, **kw))
    attn_midT = csb("attn_midT", [128, 8, T], BF16)
    pwb = [csb("pwb0", [128, 8, 512], BF16), csb("pwb1", [128, 8, 512], BF16)]
    xcb = [csb("xcb0", [128, 512], F32), csb("xcb1", [128, 512], F32)]
    x2p_sb = [csb("x2p0", [128, 512], F32), csb("x2p1", [128, 512], F32)]
    dump2 = csb("dump2", [128, 512], F32)
    sqd = [csb("sqd0", [128, 512], BF16), csb("sqd1", [128, 512], BF16)]
    projb_row = csb("projb_row", [1, D], BF16)
    cyctx = ExitStack()
    py = [cyctx.enter_context(nc.psum_tensor(f"py{i}", [128, 512], F32))
          for i in range(4)]

    g2s = P.mksem("g2s")
    P.wait("gpsimd", pe, m_vproj_pe)
    P.op("gpsimd", lambda g: g.dma_start(out=projb_row[0:1, :],
                                         in_=proj_b[None, :]), inc=g2s, k=16)
    m_qg_projb = g2s.n

    m_y1evac = {}
    P.wait("tensor", ac, m_oT_ac)
    P.wait("tensor", qg_wo_sem, m_qg_wo)
    for wc in range(8):
        for half in range(2):
            tile = py[(wc % 2) * 2 + half]
            key = (wc - 2, half)
            if key in m_y1evac:
                P.wait("tensor", ac, m_y1evac[key])
            mlast = None
            for hc in range(8):
                mlast = P.op("tensor", lambda t_, tile=tile, hc=hc, wc=wc, half=half:
                             t_.matmul(tile[:, :],
                                       wo_sb[:, hc, wc * 128:(wc + 1) * 128],
                                       oT[:, hc, half * 512:half * 512 + 512],
                                       start=(hc == 0), stop=(hc == 7)),
                             inc=pe if hc == 7 else None)
            P.wait("scalar", pe, mlast)
            m_y1evac[(wc, half)] = P.op(
                "scalar", lambda s, tile=tile, wc=wc, half=half:
                s.activation(attn_midT[:, wc, half * 512:half * 512 + 512],
                             tile[:, :], AF.Identity, bias=bo_col[:, wc:wc + 1]),
                inc=ac)
    m_y1_ac = ac.n
    dma_tap("attn_midT", lambda: attn_midT[:, :, :], (128, 8, T), BF16,
            [(ac, m_y1_ac)])
    P.mark_phase("y1")

    cyctx.close()
    pc2ctx = ExitStack()
    pc2 = [pc2ctx.enter_context(nc.psum_tensor(f"pc2{i}", [128, 512], F32))
           for i in range(8)]
    pwcnt = [0, 0]
    xccnt = [0, 0]
    xc_need = {}
    x2cnt = [0, 0]
    m_pw_used = {}
    m_xc_load_order = {}
    m_x2evac = {}
    m_sq_used = {}
    for ds in range(8):
        slot = ds % 2
        if SERIALIZE_Y2:
            P.wait("tensor", dv)
            P.wait("tensor", ac)
        if ds >= 2:
            P.wait("gpsimd", pe, m_pw_used[ds - 2])
        P.op("gpsimd", lambda g, ds=ds: g.dma_start(
            out=pwb[ds % 2][:, :, :],
            in_=proj_w.rearrange("(c p) d -> p c d", p=128)[:, :, ds * 512:(ds + 1) * 512]),
            inc=pws[slot], k=16)
        pwcnt[slot] += 1
        P.wait("tensor", pws[slot], 16 * pwcnt[slot])
        if ds == 0:
            P.wait("tensor", ac, m_y1_ac)
            P.wait("tensor", g2s, m_qg_projb)
        for tt in range(NT):
            idx = ds * 8 + tt
            xsl = idx % 2
            key = (idx - 2)
            if key in m_xc_load_order:
                P.wait("sync", dv, m_xc_load_order[key])
            elif ds == 0 and tt < 2:
                P.wait("sync", dv, m_xlnT_dv)
            P.op("sync", lambda s, ds=ds, tt=tt, xsl=xsl: s.dma_start(
                out=xcb[xsl][:, :],
                in_=x_in[1 + tt * 128:1 + tt * 128 + 128,
                         ds * 512:(ds + 1) * 512]), inc=xcs[xsl], k=16)
            xccnt[xsl] += 1
            xc_need[idx] = (xsl, xccnt[xsl])
            if ds >= 1:
                P.wait("tensor", dv, m_x2evac.get((ds - 1, tt)))
            for wc in range(8):
                P.op("tensor", lambda t_, wc=wc, tt=tt, slot=slot:
                     t_.matmul(pc2[tt][:, :],
                               attn_midT[:, wc, tt * 128:(tt + 1) * 128],
                               pwb[slot][:, wc, :],
                               start=(wc == 0), stop=False))
            mlast = P.op("tensor", lambda t_, tt=tt, ds=ds:
                         t_.matmul(pc2[tt][:, :], ones_bf[0:1, 0:128],
                                   projb_row[0:1, ds * 512:ds * 512 + 512],
                                   start=False, stop=True), inc=pe)
            psl = (ds * 8 + tt) % 2
            P.wait("vector", pe, mlast)
            _sl, _n = xc_need[ds * 8 + tt]
            P.wait("vector", xcs[_sl], 16 * _n)
            prev_store = (ds * 8 + tt) - 2
            if prev_store >= 0:
                P.wait("vector", x2st[psl], 16 * (x2cnt[psl]))
                mprev_sq = m_sq_used.get(prev_store)
                if mprev_sq is not None:
                    P.wait("vector", ac, mprev_sq)
            madd = P.op("vector", lambda v, tt=tt, psl=psl, xsl=xsl:
                        v.tensor_add(x2p_sb[psl][:, :], pc2[tt][:, :],
                                     xcb[xsl][:, :]), inc=dv)
            m_xc_load_order[ds * 8 + tt] = madd
            m_x2evac[(ds, tt)] = madd
            P.wait("vector", dv, madd)
            P.op("vector", lambda v, tt=tt, ds=ds, psl=psl: v.tensor_scalar(
                out=dump2[:, :], in0=x2p_sb[psl][:, :], scalar1=1.0,
                scalar2=None, op0=ALU.mult, op1=ALU.add,
                accum_out=c_sums[:, tt, ds:ds + 1]), inc=dv)
            P.wait("scalar", dv, madd)
            m_sq_used[ds * 8 + tt] = P.op(
                "scalar", lambda s, tt=tt, ds=ds, psl=psl:
                s.activation(sqd[psl][:, :], x2p_sb[psl][:, :], AF.Square,
                             accum_out=c_sumsq[:, tt, ds:ds + 1]), inc=ac)
            P.op("scalar", lambda s, tt=tt, ds=ds, psl=psl: s.dma_start(
                out=x2s[tt * 128:(tt + 1) * 128, ds * 512:(ds + 1) * 512],
                in_=x2p_sb[psl][:, :]), inc=x2st[psl], k=16)
            x2cnt[psl] += 1
        m_pw_used[ds] = pe.n
    m_x2_all = (x2cnt[0] * 16, x2cnt[1] * 16)
    m_stats_dv = dv.n
    m_stats_ac = ac.n
    if "x2" in taps:
        ap = tapout("x2", (T, D), F32)
        P.wait("scalar", x2st[0], m_x2_all[0])
        P.wait("scalar", x2st[1], m_x2_all[1])
        P.op("scalar", lambda s, ap=ap: s.dma_start(out=ap, in_=x2s), inc=tps, k=16)
    pc2ctx.close()
    cc.close()
    octx.close()
    bctx.close()
    abctx.close()

    P.mark_phase("y2")
    dectx = ExitStack()
    dsb = lambda *a, **kw: dectx.enter_context(nc.sbuf_tensor(*a, **kw))
    xln2T = dsb("xln2T", [128, DC, T], BF16)
    A_wT = dsb("A_wT", [128, DC, 128], BF16)
    Bmat = dsb("Bmat", [128, D], BF16)
    wgt_cols = dsb("wgt_cols", [128, DC, 9], BF16)
    E_bf = dsb("E_bf", [8, 128], BF16)
    hT_sb = dsb("hT_sb", [128, T], BF16)
    hwT_sb = dsb("hwT_sb", [128, 8, 128], BF16)
    bbase_row = dsb("bbase_row", [1, D], BF16)
    gts_sb = dsb("gts_sb", [128, NT, 9], F32)

    dp1 = ExitStack()
    awtm = dp1.enter_context(nc.sbuf_tensor("awtm", [128, D], BF16, side="right"))
    wg_tm = dp1.enter_context(nc.sbuf_tensor("wg_tm", [8, D], BF16, side="right"))
    wth_row = dp1.enter_context(nc.sbuf_tensor("wth_row", [1, D], BF16, side="right"))
    dpctx = ExitStack()
    tpd = [dpctx.enter_context(nc.psum_tensor(f"tpd{i}", [128, 128], BF16))
           for i in range(2)]
    tp9 = dpctx.enter_context(nc.psum_tensor("tp9", [128, 9], BF16))

    P.op("gpsimd", lambda g: g.memset(E_bf[:, :], MOE_SCALING), inc=po)
    P.wait("gpsimd", po)
    P.op("gpsimd", lambda g: g.affine_select(
        out=E_bf[:, :], in_=E_bf[:, :], pattern=[[1, 128]],
        compare_op=ALU.is_ge, fill=0.0, base=0, channel_multiplier=-16), inc=po)
    P.wait("gpsimd", po)
    m_E = P.op("gpsimd", lambda g: g.affine_select(
        out=E_bf[:, :], in_=E_bf[:, :], pattern=[[-1, 128]],
        compare_op=ALU.is_ge, fill=0.0, base=15, channel_multiplier=16), inc=po)

    g3s = P.mksem("g3s")
    P.op("gpsimd", lambda g: g.dma_start(out=awtm[:, :], in_=A_w), inc=g3s, k=16)
    P.op("gpsimd", lambda g: g.dma_start(out=wg_tm[0:8, :], in_=W_gate), inc=g3s, k=16)
    P.op("gpsimd", lambda g: g.dma_start(out=wth_row[0:1, :], in_=W_thr[None, :]),
         inc=g3s, k=16)
    P.op("gpsimd", lambda g: g.dma_start(out=bbase_row[0:1, :], in_=b_base[None, :]),
         inc=g3s, k=16)
    m_qg_dprep = g3s.n

    P.wait("tensor", g3s, m_qg_dprep)
    P.wait("tensor", dv, m_stats_dv)
    P.wait("tensor", ac, m_stats_ac)
    m_dprep_evac = []
    tpd_guard = [None, None]
    for dc in range(DC):
        s2 = dc % 2
        if tpd_guard[s2] is not None:
            P.wait("tensor", tpd_guard[s2][0], tpd_guard[s2][1])
        mt = P.op("tensor", lambda t_, dc=dc, s2=s2: t_.transpose(
            tpd[s2][:, :], awtm[:, dc * 128:(dc + 1) * 128], ident_bf[:, :]),
            inc=pe)
        P.wait("vector", pe, mt)
        mv = P.op("vector", lambda v, dc=dc, s2=s2: v.tensor_copy(
            A_wT[:, dc, :], tpd[s2][:, :]), inc=dv)
        tpd_guard[s2] = (dv, mv)
    m_w9 = None
    for dc in range(DC):
        P.wait("tensor", dv if m_w9 is None else m_w9[0],
               None if m_w9 is None else m_w9[1])
        mtg = P.op("tensor", lambda t_, dc=dc: t_.transpose(
            tp9[:, 0:8], wg_tm[0:8, dc * 128:(dc + 1) * 128],
            ident_bf[0:8, 0:8]), inc=pe)
        P.wait("tensor", pe, mtg)
        mt = P.op("tensor", lambda t_, dc=dc: t_.transpose(
            tp9[:, 8:9], wth_row[0:1, dc * 128:(dc + 1) * 128],
            ident_bf[0:1, 0:1]), inc=pe)
        P.wait("vector", pe, mt)
        mv = P.op("vector", lambda v, dc=dc: v.tensor_copy(
            wgt_cols[:, dc, :], tp9[:, :]), inc=dv)
        m_w9 = (dv, mv)
    m_wgt_done = m_w9[1]
    m_dp1_done = pe.n
    dp1.close()
    dp2 = ExitStack()
    bwf = dp2.enter_context(nc.sbuf_tensor("bwf", [128, DC, 128], F32, side="right"))
    bwbf = dp2.enter_context(nc.sbuf_tensor("bwbf", [128, DC, 128], BF16, side="right"))
    P.wait("scalar", qc, m_qc_relay)
    P.wait("scalar", x2st[0], m_x2_all[0])
    P.wait("scalar", x2st[1], m_x2_all[1])
    P.wait("scalar", pe, m_dp1_done)
    P.op("scalar", lambda s: s.dma_start(
        out=bwf[:, :, :], in_=bwf_dram.rearrange("p (c k) -> p c k", c=DC)),
        inc=qc, k=16)
    m_qc_bw = qc.n
    P.wait("vector", qc, m_qc_bw)
    mcast = P.op("vector", lambda v: v.tensor_copy(bwbf[:, :, :], bwf[:, :, :]),
                 inc=dv)
    P.wait("tensor", dv, mcast)
    tpd_guard = [None, None]
    for oc in range(DC):
        s2 = oc % 2
        if tpd_guard[s2] is not None:
            P.wait("tensor", tpd_guard[s2][0], tpd_guard[s2][1])
        mt = P.op("tensor", lambda t_, oc=oc, s2=s2: t_.transpose(
            tpd[s2][:, :], bwbf[:, oc, :], ident_bf[:, :]), inc=pe)
        eng, ctr = ("vector", dv) if oc % 2 == 0 else ("scalar", ac)
        P.wait(eng, pe, mt)
        mv = P.op(eng, lambda e, oc=oc, s2=s2: _copy(
            e, Bmat[:, oc * 128:(oc + 1) * 128], tpd[s2][:, :]), inc=ctr)
        tpd_guard[s2] = (ctr, mv)
    m_dprep_dv = dv.n
    m_dprep_ac = ac.n
    m_dprep_pe = pe.n
    dp2.close()

    P.mark_phase("dprep")
    c2ctx = ExitStack()
    c2s = lambda *a, **kw: c2ctx.enter_context(nc.sbuf_tensor(*a, **kw))
    xr_sb = [c2s("xr0s", [128, D], F32), c2s("xr1s", [128, D], F32)]
    xln2_bf = [c2s("xln2bf0", [128, D], BF16), c2s("xln2bf1", [128, D], BF16)]
    ln2sm = c2s("ln2sm", [128, 16], F32)
    tp2 = [c2ctx.enter_context(nc.psum_tensor(f"tp2{i}", [128, 128], BF16))
           for i in range(3)]
    pgt1 = c2ctx.enter_context(nc.psum_tensor("pgt1", [128, 9], F32))
    ph1 = c2ctx.enter_context(nc.psum_tensor("ph1", [128, 512], F32))

    P.wait("sync", x2st[0], m_x2_all[0])
    P.wait("sync", x2st[1], m_x2_all[1])
    m2_norm = {}
    m2_tp = {}
    m2_tile = {}
    m_gtevac = {}
    m_hT = {}
    tp2_evac = [None] * 3

    def plan_gt(tt):
        P.wait("tensor", dv, m2_tile[tt][0])
        P.wait("tensor", ac, m2_tile[tt][1])
        if tt == 0:
            P.wait("tensor", dv, m_wgt_done)
        if (tt - 1) in m_gtevac:
            P.wait("tensor", ac, m_gtevac[tt - 1])
        mg = None
        for dc in range(DC):
            mg = P.op("tensor", lambda t_, tt=tt, dc=dc: t_.matmul(
                pgt1[:, :], xln2T[:, dc, tt * 128:(tt + 1) * 128],
                wgt_cols[:, dc, :], start=(dc == 0), stop=(dc == DC - 1)),
                inc=pe if dc == DC - 1 else None)
        P.wait("scalar", pe, mg)
        m_gtevac[tt] = P.op("scalar", lambda s, tt=tt: s.activation(
            gts_sb[:, tt, :], pgt1[:, :], AF.Copy), inc=ac)

    def plan_hT(half):
        if (half - 1) in m_hT:
            P.wait("tensor", ac, m_hT[half - 1])
        mh = None
        for dc in range(DC):
            mh = P.op("tensor", lambda t_, half=half, dc=dc: t_.matmul(
                ph1[:, :], A_wT[:, dc, :],
                xln2T[:, dc, half * 512:half * 512 + 512],
                start=(dc == 0), stop=(dc == DC - 1)),
                inc=pe if dc == DC - 1 else None)
        P.wait("scalar", pe, mh)
        m_hT[half] = P.op("scalar", lambda s, half=half: s.activation(
            hT_sb[:, half * 512:half * 512 + 512], ph1[:, :], AF.Copy),
            inc=ac)

    xrcnt = [0, 0]
    for tt in range(NT):
        slot = tt % 2
        if tt >= 2:
            P.wait("sync", dv, m2_norm[tt - 2])
        P.op("sync", lambda s, tt=tt, slot=slot: s.dma_start(
            out=xr_sb[slot][:, :], in_=x2s[tt * 128:(tt + 1) * 128, :]),
            inc=xrs[slot], k=16)
        xrcnt[slot] += 1
        if tt == 0:
            P.wait("vector", dv, m_stats_dv)
            P.wait("vector", ac, m_stats_ac)
        P.op("vector", lambda v, tt=tt, slot=slot: v.reduce_sum(
            ln2sm[:, slot:slot + 1], c_sums[:, tt, :], axis=AX.X), inc=dv)
        P.wait("vector", dv)
        P.op("vector", lambda v, tt=tt, slot=slot: v.reduce_sum(
            ln2sm[:, 2 + slot:3 + slot], c_sumsq[:, tt, :], axis=AX.X), inc=dv)
        P.wait("vector", dv)
        c_mn, c_msq, c_nv = 4 + slot, 6 + slot, 8 + slot
        P.op("vector", lambda v, slot=slot, c=c_mn: v.tensor_scalar(
            out=ln2sm[:, c:c + 1], in0=ln2sm[:, slot:slot + 1],
            scalar1=1.0 / D, scalar2=None, op0=ALU.mult), inc=dv)
        P.wait("vector", dv)
        P.op("vector", lambda v, slot=slot, c=c_msq: v.tensor_scalar(
            out=ln2sm[:, c:c + 1], in0=ln2sm[:, 2 + slot:3 + slot],
            scalar1=1.0 / D, scalar2=None, op0=ALU.mult), inc=dv)
        P.wait("vector", dv)
        m_nv = P.op("vector", lambda v, cm=c_mn, cq=c_msq, cn=c_nv:
                    v.scalar_tensor_tensor(
                        out=ln2sm[:, cn:cn + 1], in0=ln2sm[:, cm:cm + 1],
                        scalar=ln2sm[:, cm:cm + 1], in1=ln2sm[:, cq:cq + 1],
                        op0=ALU.mult, op1=ALU.subtract), inc=dv)
        P.wait("scalar", dv, m_nv)
        c_ln, c_rs = 10 + slot, 12 + slot
        P.op("scalar", lambda s, cn=c_nv, cl=c_ln: s.activation(
            ln2sm[:, cl:cl + 1], ln2sm[:, cn:cn + 1], AF.Ln,
            scale=-1.0, bias=eps_col[:, :]), inc=ac)
        P.wait("scalar", ac)
        m_rstd = P.op("scalar", lambda s, cl=c_ln, cr=c_rs: s.activation(
            ln2sm[:, cr:cr + 1], ln2sm[:, cl:cl + 1], AF.Exp, scale=-0.5),
            inc=ac)
        P.wait("vector", ac, m_rstd)
        P.wait("vector", xrs[slot], 16 * xrcnt[slot])
        if tt >= 2:
            P.wait("vector", pe, m2_tp[tt - 2])
        m2_norm[tt] = P.op("vector", lambda v, slot=slot, cm=c_mn, cr=c_rs:
                           v.tensor_scalar(
                               out=xln2_bf[slot][:, :], in0=xr_sb[slot][:, :],
                               scalar1=ln2sm[:, cm:cm + 1],
                               scalar2=ln2sm[:, cr:cr + 1],
                               op0=ALU.subtract, op1=ALU.mult), inc=dv)
        P.wait("tensor", dv, m2_norm[tt])
        for dc in range(DC):
            s4 = dc % 3
            if tp2_evac[s4] is not None:
                P.wait("tensor", tp2_evac[s4][0], tp2_evac[s4][1])
            P.op("tensor", lambda t_, slot=slot, dc=dc, s4=s4: t_.transpose(
                tp2[s4][:, :], xln2_bf[slot][:, dc * 128:(dc + 1) * 128],
                ident_bf[:, :]), inc=pe)
            eng, ctr = ("vector", dv) if dc % 2 == 0 else ("scalar", ac)
            P.wait(eng, pe)
            mv = P.op(eng, lambda e, dc=dc, s4=s4, tt=tt: _copy(
                e, xln2T[:, dc, tt * 128:(tt + 1) * 128], tp2[s4][:, :]), inc=ctr)
            tp2_evac[s4] = (ctr, mv)
        m2_tp[tt] = pe.n
        m2_tile[tt] = (dv.n, ac.n)
        if tt >= 1:
            plan_gt(tt - 1)
        if tt == 4:
            plan_hT(0)
    plan_gt(7)
    plan_hT(1)
    m_xln2T_dv = dv.n
    m_xln2T_ac = ac.n
    c2ctx.close()
    wectx = ExitStack()
    wbT = [wectx.enter_context(nc.sbuf_tensor("wbT0", [128, DC, 512], BF16)),
           wectx.enter_context(nc.sbuf_tensor("wbT1", [128, DC, 512], BF16))]
    wtcnt = [0, 0]
    P.wait("sync", g4s, m_qg_wbase)
    P.wait("sync", dv, m_xln2T_dv)
    P.wait("sync", ac, m_xln2T_ac)
    for os_ in range(2):
        for dc in range(DC):
            P.op("sync", lambda s, os_=os_, dc=dc:
                 s.dma_start_transpose(
                     wbT[os_][:, dc, :],
                     wbase_bf[os_ * 512:(os_ + 1) * 512, dc * 128:(dc + 1) * 128]),
                 inc=wts[os_], k=16)
        wtcnt[os_] += 32

    dma_tap("xln2T", lambda: xln2T[:, :, :], (128, DC, T), BF16,
            [(dv, m_xln2T_dv), (ac, m_xln2T_ac)])
    P.mark_phase("ln2")

    dpctx.close()
    dctx = ExitStack()
    d2 = lambda *a, **kw: dctx.enter_context(nc.sbuf_tensor(*a, **kw))
    gexp_sb = d2("gexp_sb", [128, NT, 8], F32)
    et_sb = d2("et_sb", [128, NT], F32)
    gsm = d2("gsm", [128, 6 * NT], F32)
    wrel_sb = d2("wrel_sb", [128, NT, 8], F32)
    wt_bf = d2("wt_bf", [128, NT, 8], BF16)
    wtT_sb = [d2("wtT0", [8, 128], BF16), d2("wtT1", [8, 128], BF16)]
    wexp_bf = [d2("wexp0", [128, 128], BF16), d2("wexp1", [128, 128], BF16)]
    pwx = [dctx.enter_context(nc.psum_tensor(f"pwx{i}", [128, 128], F32))
           for i in range(2)]
    ptw = [dctx.enter_context(nc.psum_tensor(f"ptw{i}", [8, 128], BF16))
           for i in range(2)]

    P.wait("scalar", ac, m_gtevac[7])
    mge = P.op("scalar", lambda s: s.activation(
        gexp_sb[:, :, :], gts_sb[:, :, 0:8], AF.Exp), inc=ac)
    P.wait("scalar", ac, mge)
    met = P.op("scalar", lambda s: s.activation(
        et_sb[:, :], gts_sb[:, :, 8], AF.Exp, scale=-1.0,
        bias=nbthr_col[:, 0:1]), inc=ac)
    P.wait("vector", ac, met)
    P.op("vector", lambda v: v.reduce_sum(
        gsm[:, 0:NT], gexp_sb[:, :, :], axis=AX.X), inc=dv)
    P.wait("vector", dv)
    P.op("vector", lambda v: v.reciprocal(gsm[:, NT:2 * NT], gsm[:, 0:NT]),
         inc=dv)
    P.wait("vector", dv)
    P.op("vector", lambda v: v.tensor_scalar(
        out=gsm[:, 2 * NT:3 * NT], in0=et_sb[:, :], scalar1=1.0, scalar2=None,
        op0=ALU.add), inc=dv)
    P.wait("vector", dv)
    P.op("vector", lambda v: v.reciprocal(gsm[:, 3 * NT:4 * NT],
                                          gsm[:, 2 * NT:3 * NT]), inc=dv)
    P.wait("vector", dv)
    P.op("vector", lambda v: v.tensor_scalar(
        out=gsm[:, 3 * NT:4 * NT], in0=gsm[:, 3 * NT:4 * NT],
        scalar1=MAX_THRESHOLD, scalar2=None, op0=ALU.mult), inc=dv)
    P.wait("vector", dv)
    P.op("vector", lambda v: v.tensor_tensor(
        wrel_sb[:, :, :], gexp_sb[:, :, :],
        gsm[:, NT:2 * NT].unsqueeze(2).broadcast_to([128, NT, 8]),
        op=ALU.mult), inc=dv)
    P.wait("vector", dv)
    P.op("vector", lambda v: v.tensor_tensor(
        wrel_sb[:, :, :], wrel_sb[:, :, :],
        gsm[:, 3 * NT:4 * NT].unsqueeze(2).broadcast_to([128, NT, 8]),
        op=ALU.subtract), inc=dv)
    P.wait("vector", dv)
    P.op("vector", lambda v: v.tensor_scalar(
        out=wrel_sb[:, :, :], in0=wrel_sb[:, :, :], scalar1=0.0, scalar2=None,
        op0=ALU.max), inc=dv)
    P.wait("vector", dv)
    P.op("vector", lambda v: v.reduce_sum(
        gsm[:, 4 * NT:5 * NT], wrel_sb[:, :, :], axis=AX.X), inc=dv)
    P.wait("vector", dv)
    P.op("vector", lambda v: v.tensor_scalar(
        out=gsm[:, 4 * NT:5 * NT], in0=gsm[:, 4 * NT:5 * NT], scalar1=1e-20,
        scalar2=None, op0=ALU.max), inc=dv)
    P.wait("vector", dv)
    P.op("vector", lambda v: v.reciprocal(gsm[:, 5 * NT:6 * NT],
                                          gsm[:, 4 * NT:5 * NT]), inc=dv)
    P.wait("vector", dv)
    mwt = P.op("vector", lambda v: v.tensor_tensor(
        wt_bf[:, :, :], wrel_sb[:, :, :],
        gsm[:, 5 * NT:6 * NT].unsqueeze(2).broadcast_to([128, NT, 8]),
        op=ALU.mult), inc=dv)

    m_hw = {}
    m_pwx_used = {}
    m_ptw_used = {}
    m_wtT_used = {}
    P.wait("tensor", dv, mwt)
    P.wait("tensor", po, m_E)
    for tt in range(NT):
        s2 = tt % 2
        if (tt - 2) in m_ptw_used:
            P.wait("tensor", m_ptw_used[tt - 2][0], m_ptw_used[tt - 2][1])
        mtw = P.op("tensor", lambda t_, tt=tt, s2=s2: t_.transpose(
            ptw[s2][:, :], wt_bf[:, tt, :], ident_bf[:, :]), inc=pe)
        P.wait("vector", pe, mtw)
        if (tt - 2) in m_wtT_used:
            P.wait("vector", pe, m_wtT_used[tt - 2])
        mcw = P.op("vector", lambda v, s2=s2: v.tensor_copy(
            wtT_sb[s2][0:8, :], ptw[s2][0:8, :]), inc=dv)
        m_ptw_used[tt] = (dv, mcw)
        P.wait("tensor", dv, mcw)
        if (tt - 2) in m_pwx_used:
            P.wait("tensor", ac, m_pwx_used[tt - 2])
        mwx = P.op("tensor", lambda t_, s2=s2: t_.matmul(
            pwx[s2][:, :], E_bf[0:8, :], wtT_sb[s2][0:8, :],
            start=True, stop=True), inc=pe)
        m_wtT_used[tt] = mwx
        P.wait("scalar", pe, mwx)
        mwe = P.op("scalar", lambda s, s2=s2: s.activation(
            wexp_bf[s2][:, :], pwx[s2][:, :], AF.Copy), inc=ac)
        m_pwx_used[tt] = mwe
        P.wait("vector", ac, mwe)
        if tt == 0:
            P.wait("vector", ac, m_hT[1])
        m_hw[tt] = P.op("vector", lambda v, tt=tt, s2=s2: v.tensor_mul(
            hwT_sb[:, tt, :], hT_sb[:, tt * 128:(tt + 1) * 128],
            wexp_bf[s2][:, :]), inc=dv)
    m_hw_all = dv.n
    dctx.close()

    dma_tap("hwT", lambda: hwT_sb[:, :, :], (128, 8, 128), BF16,
            [(dv, m_hw_all)])
    dma_tap("hT", lambda: hT_sb[:, :], (128, T), BF16,
            [(ac, m_hT[1])])

    P.mark_phase("ffnA")
    ectx = ExitStack()
    es = lambda *a, **kw: ectx.enter_context(nc.sbuf_tensor(*a, **kw))
    xeb = [es(f"xeb{i}", [128, 512], F32) for i in range(16)]
    yob = [es("yob0", [128, 512], F32), es("yob1", [128, 512], F32)]
    pe8 = [ectx.enter_context(nc.psum_tensor(f"pe8{i}", [128, 512], F32))
           for i in range(8)]

    xecnt = [0, 0]
    yscnt = [0, 0]
    m_wbT_used = {}
    m_xe_used = {}
    m_eevac = {}
    P.wait("sync", dv, m_hw_all)
    for tt in range(NT):
        P.op("sync", lambda s, tt=tt: s.dma_start(
            out=xeb[tt][:, :],
            in_=x2s[tt * 128:(tt + 1) * 128, 0:512]), inc=xes[0], k=16)
    xecnt[0] += 8
    for os_ in range(8):
        slot = os_ % 2
        if os_ >= 2:
            P.wait("scalar", pe, m_wbT_used[os_ - 2])
            for dc in range(DC):
                P.op("scalar", lambda s, os_=os_, dc=dc, slot=slot:
                     s.dma_start_transpose(
                         wbT[slot][:, dc, :],
                         wbase_bf[os_ * 512:(os_ + 1) * 512, dc * 128:(dc + 1) * 128]),
                     inc=wts[slot], k=16)
            wtcnt[slot] += 32
        P.wait("tensor", wts[slot], 16 * (wtcnt[slot] if os_ >= 2 else 32))
        if os_ == 0:
            P.wait("tensor", dv, m_hw_all)
            P.wait("tensor", dv, m_dprep_dv)
            P.wait("tensor", ac, m_dprep_ac)
            P.wait("tensor", g3s, m_qg_dprep)
        if os_ + 1 < 8:
            on = os_ + 1
            for tt in range(NT):
                xsl = (on % 2) * 8 + tt
                key = (on - 2, tt)
                if key in m_xe_used:
                    P.wait("sync", dv, m_xe_used[key])
                P.op("sync", lambda s, on=on, tt=tt, xsl=xsl: s.dma_start(
                    out=xeb[xsl][:, :],
                    in_=x2s[tt * 128:(tt + 1) * 128, on * 512:(on + 1) * 512]),
                    inc=xes[on % 2], k=16)
            xecnt[on % 2] += 8
        for tt in range(NT):
            xsl = (os_ % 2) * 8 + tt
            if os_ >= 1:
                P.wait("tensor", dv, m_eevac.get((os_ - 1, tt)))
            for dc in range(DC):
                P.op("tensor", lambda t_, dc=dc, tt=tt, slot=slot:
                     t_.matmul(pe8[tt][:, :],
                               xln2T[:, dc, tt * 128:(tt + 1) * 128],
                               wbT[slot][:, dc, :],
                               start=(dc == 0), stop=False))
            P.op("tensor", lambda t_, tt=tt, os_=os_: t_.matmul(
                pe8[tt][:, :], hwT_sb[:, tt, :],
                Bmat[:, os_ * 512:(os_ + 1) * 512], start=False, stop=False))
            mlast = P.op("tensor", lambda t_, tt=tt, os_=os_: t_.matmul(
                pe8[tt][:, :], ones_bf[0:1, 0:128],
                bbase_row[0:1, os_ * 512:os_ * 512 + 512],
                start=False, stop=True), inc=pe)
            psl = (os_ * 8 + tt) % 2
            P.wait("vector", pe, mlast)
            P.wait("vector", xes[os_ % 2], 16 * xecnt[os_ % 2])
            if (os_ * 8 + tt) - 2 >= 0:
                P.wait("vector", yss[psl], 16 * yscnt[psl])
            madd = P.op("vector", lambda v, tt=tt, psl=psl, xsl=xsl:
                        v.tensor_add(yob[psl][:, :], pe8[tt][:, :],
                                     xeb[xsl][:, :]), inc=dv)
            m_xe_used[(os_, tt)] = madd
            m_eevac[(os_, tt)] = madd
            P.wait("sync", dv, madd)
            P.op("sync", lambda s, tt=tt, os_=os_, psl=psl: s.dma_start(
                out=y_out[tt * 128:(tt + 1) * 128, os_ * 512:(os_ + 1) * 512],
                in_=yob[psl][:, :]), inc=yss[psl], k=16)
            yscnt[psl] += 1
        m_wbT_used[os_] = pe.n

    if "wbase0" in taps:
        ap = tapout("wbase0", (512, D), BF16)
        P.wait("scalar", g4s, m_qg_wbase)
        P.op("scalar", lambda s, ap=ap: s.dma_start(out=ap, in_=wbase_bf[0:512, :]),
             inc=tps, k=16)
    dma_tap("Bmat", lambda: Bmat[:, :], (128, D), BF16,
            [(dv, m_dprep_dv), (ac, m_dprep_ac)])
    dma_tap("bbase", lambda: bbase_row[0:1, :], (1, D), BF16, [(dv, m_hw_all)])
    P.wait("sync", yss[0], 16 * yscnt[0])
    P.wait("sync", yss[1], 16 * yscnt[1])
    if tps.n:
        P.wait("sync", tps, tps.n)
    ectx.close()
    wectx.close()
    dectx.close()

    P.emit(stop_at=stop_at)
    return nc, tap_outs


def kernel(**inputs):
    x = np.asarray(inputs["L0_attn_inputs"], dtype=np.float32).reshape(8192, D)
    xpad = np.zeros((8194, D), dtype=np.float32)
    xpad[1:8193] = x

    def f32c(a, shape=None):
        a = np.ascontiguousarray(np.asarray(a, dtype=np.float32))
        return a.reshape(shape) if shape is not None else a

    common = {
        "conv_w": f32c(inputs["L0_attn_conv_w"]),
        "conv_b": f32c(inputs["L0_attn_conv_b"]),
        "wq": f32c(inputs["L0_attn_wq"], (1024, 1024)),
        "bq": f32c(inputs["L0_attn_bq"], (1024,)),
        "wk": f32c(inputs["L0_attn_wk"], (1024, 1024)),
        "bk": f32c(inputs["L0_attn_bk"], (1024,)),
        "wv": f32c(inputs["L0_attn_wv"], (1024, 1024)),
        "bv": f32c(inputs["L0_attn_bv"], (1024,)),
        "wo": f32c(inputs["L0_attn_wo"], (1024, 1024)),
        "bo": f32c(inputs["L0_attn_bo"], (1024,)),
        "proj_w": f32c(inputs["L0_attn_proj_w"]),
        "proj_b": f32c(inputs["L0_attn_proj_b"]),
        "W_base": f32c(inputs["L0_ffn_W_base"]),
        "b_base": f32c(inputs["L0_ffn_b_base"]),
        "W_gate": f32c(inputs["L0_ffn_W_gate"]),
        "W_thr": f32c(inputs["L0_ffn_W_thr"]),
        "b_thr": f32c(inputs["L0_ffn_b_thr"]),
        "A_w": f32c(inputs["L0_ffn_A_w"], (128, D)),
        "B_w": f32c(inputs["L0_ffn_B_w"]),
    }
    in_maps = []
    for c in range(8):
        m = dict(common)
        m["x_in"] = np.ascontiguousarray(xpad[c * 1024:c * 1024 + 1026])
        in_maps.append(m)

    nc, _ = build_program()
    res = run_bass_kernel_spmd(nc, in_maps, core_ids=list(range(8)))
    y = np.concatenate([res.results[c]["y"] for c in range(8)], axis=0)
    return y.reshape(1, 8192, D).astype(np.float32)


# version 13
# speedup vs baseline: 1.0608x; 1.0226x over previous
import numpy as np
from contextlib import ExitStack

import concourse.bass as bass
import concourse.mybir as mybir
from concourse.bass_utils import run_bass_kernel_spmd

F32 = mybir.dt.float32
BF16 = mybir.dt.bfloat16
AF = mybir.ActivationFunctionType
ALU = mybir.AluOpType
AX = mybir.AxisListType

T = 1024
NT = 8
D = 4096
DC = 32
LN_EPS = 1e-5
NEG_INF = -1.0e9
SCALE_ATTN = 0.125
MOE_SCALING = 2.0
MAX_THRESHOLD = 0.125


def _copy(e, out, in_):
    if hasattr(e, "tensor_copy"):
        return e.tensor_copy(out, in_)
    return e.copy(out, in_)


class Ctr:
    __slots__ = ("h", "n")

    def __init__(self, h):
        self.h = h
        self.n = 0


class Prog:

    ENGINES = ("tensor", "vector", "scalar", "gpsimd", "sync")

    def __init__(self, nc, ctx):
        self.nc = nc
        self.ctx = ctx
        self.items = {e: [] for e in self.ENGINES}
        self.phase_marks = {}

    def mark_phase(self, name):
        self.phase_marks[name] = {e: len(self.items[e]) for e in self.ENGINES}

    def mksem(self, name):
        return Ctr(self.ctx.enter_context(self.nc.semaphore(name)))

    def op(self, eng, fn, inc=None, k=1):
        if inc is not None:
            inc.n += k
            sem, kk = inc.h, k

            def run(e, fn=fn, sem=sem, kk=kk):
                fn(e).then_inc(sem, kk)
            self.items[eng].append(run)
            return inc.n
        self.items[eng].append(lambda e, fn=fn: fn(e))
        return None

    def wait(self, eng, ctr, val=None):
        v = ctr.n if val is None else val
        if v is None or v <= 0:
            return
        self.items[eng].append(lambda e, h=ctr.h, v=v: e.wait_ge(h, v))

    def emit(self, stop_at=None):
        cut = self.phase_marks.get(stop_at) if stop_at else None
        with self.nc.Block() as block:
            for name in self.ENGINES:
                items = self.items[name]
                if cut is not None:
                    items = items[:cut[name]]
                if not items:
                    continue

                def body(eng, items=items):
                    for run in items:
                        run(eng)
                getattr(block, name)(body)


SERIALIZE_Y2 = False


def build_program(taps=(), stop_at=None):
    nc = bass.Bass("TRN2", target_bir_lowering=False)

    def din(name, shape):
        return nc.dram_tensor(name, list(shape), F32, kind="ExternalInput").ap()

    x_in = din("x_in", (T + 2, D))
    conv_w = din("conv_w", (3, D, 3072))
    conv_b = din("conv_b", (3072,))
    wq_d = din("wq", (1024, 1024))
    bq_d = din("bq", (1024,))
    wk_d = din("wk", (1024, 1024))
    bk_d = din("bk", (1024,))
    wv_d = din("wv", (1024, 1024))
    bv_d = din("bv", (1024,))
    wo_d = din("wo", (1024, 1024))
    bo_d = din("bo", (1024,))
    proj_w = din("proj_w", (1024, D))
    proj_b = din("proj_b", (D,))
    W_base = din("W_base", (D, D))
    b_base = din("b_base", (D,))
    W_gate = din("W_gate", (8, D))
    W_thr = din("W_thr", (D,))
    b_thr = din("b_thr", (1,))
    A_w = din("A_w", (128, D))
    B_w = din("B_w", (8, D, 16))

    y_out = nc.dram_tensor("y", [T, D], F32, kind="ExternalOutput").ap()
    x2s = nc.dram_tensor("x2s", [T, D], F32, kind="Internal").ap()
    wbase_bf = nc.dram_tensor("wbase_bf", [D, D], BF16, kind="Internal").ap()
    bwf_dram = nc.dram_tensor("bwf_dram", [128, D], BF16, kind="Internal").ap()

    tap_outs = {}

    def tapout(name, shape, dtype):
        ap = nc.dram_tensor("tap_" + name, list(shape), dtype,
                            kind="ExternalOutput").ap()
        tap_outs[name] = ap
        return ap

    ctx = ExitStack()
    nc.ctx.enter_context(ctx)
    P = Prog(nc, ctx)

    pe = P.mksem("pe")
    dv = P.mksem("dv")
    ac = P.mksem("ac")
    po = P.mksem("po")
    qg = P.mksem("qg")
    qs = P.mksem("qs")
    qc = P.mksem("qc")
    cw = tuple(P.mksem(f"cw{i}") for i in range(4))
    xts = (P.mksem("xt0"), P.mksem("xt1"))
    wbs = (P.mksem("wb0"), P.mksem("wb1"))
    pws = (P.mksem("pw0"), P.mksem("pw1"))
    xcs = tuple(P.mksem(f"xc{i}") for i in range(4))
    x2st = (P.mksem("x2sa"), P.mksem("x2sb"))
    xrs = (P.mksem("xr0"), P.mksem("xr1"))
    xes = (P.mksem("xe0"), P.mksem("xe1"))
    wts = (P.mksem("wt0"), P.mksem("wt1"))
    yss = (P.mksem("ys0"), P.mksem("ys1"))
    tps = P.mksem("tps")

    sb = lambda *a, **kw: ctx.enter_context(nc.sbuf_tensor(*a, **kw))
    psum = lambda c, *a: c.enter_context(nc.psum_tensor(*a))

    ident_bf = sb("ident_bf", [128, 128], BF16)
    ident_f = sb("ident_f", [128, 128], F32)
    ones_bf = sb("ones_bf", [1, 128], BF16)
    ones_f = sb("ones_f", [1, 128], F32)
    convb_col = sb("convb_col", [128, 24], F32)
    bqs_col = sb("bqs_col", [128, 8], F32)
    bk_col = sb("bk_col", [128, 8], F32)
    bo_col = sb("bo_col", [128, 8], F32)
    nbthr_col = sb("nbthr_col", [128, 1], F32)
    eps_col = sb("eps_col", [128, 1], F32)
    c_sums = sb("c_sums", [128, NT, 8], F32)
    c_sumsq = sb("c_sumsq", [128, NT, 8], F32)

    def dma_tap(name, src_fn, shape, dtype, waits):
        if name not in taps:
            return
        ap = tapout(name, shape, dtype)
        for (ctr, v) in waits:
            P.wait("scalar", ctr, v)
        P.op("scalar", lambda s, ap=ap: s.dma_start(out=ap, in_=src_fn()),
             inc=tps, k=16)

    pctx = ExitStack()
    stage = pctx.enter_context(nc.sbuf_tensor("stage_row", [1, 3072], F32, side="right"))
    tp_f = (pctx.enter_context(nc.psum_tensor("tp_fa", [128, 128], F32)),
            pctx.enter_context(nc.psum_tensor("tp_fb", [128, 128], F32)))

    P.op("gpsimd", lambda g: g.memset(ident_bf[:, :], 1.0), inc=po)
    P.wait("gpsimd", po)
    P.op("gpsimd", lambda g: g.affine_select(
        out=ident_bf[:, :], in_=ident_bf[:, :], pattern=[[-1, 128]],
        compare_op=ALU.is_equal, fill=0.0, base=0, channel_multiplier=1), inc=po)
    P.op("gpsimd", lambda g: g.memset(ident_f[:, :], 1.0), inc=po)
    P.wait("gpsimd", po)
    P.op("gpsimd", lambda g: g.affine_select(
        out=ident_f[:, :], in_=ident_f[:, :], pattern=[[-1, 128]],
        compare_op=ALU.is_equal, fill=0.0, base=0, channel_multiplier=1), inc=po)
    P.op("gpsimd", lambda g: g.memset(ones_bf[:, :], 1.0), inc=po)
    P.op("gpsimd", lambda g: g.memset(ones_f[:, :], 1.0), inc=po)
    P.op("gpsimd", lambda g: g.memset(eps_col[:, :], LN_EPS), inc=po)
    m_po_const = po.n

    rows = [
        (conv_b, 3072, convb_col, 24),
        (bq_d, 1024, bqs_col, 8),
        (bk_d, 1024, bk_col, 8),
        (bo_d, 1024, bo_col, 8),
    ]
    for src, n, dst_col, nch in rows:
        P.wait("sync", pe)
        P.op("sync", lambda s, src=src, n=n:
             s.dma_start(out=stage[0:1, 0:n], in_=src[None, :]), inc=qs, k=16)
        P.wait("tensor", qs)
        P.wait("tensor", po, m_po_const)
        for c in range(nch):
            slot = tp_f[c % 2]
            if dv.n >= 2:
                P.wait("tensor", dv, dv.n - 1)
            P.op("tensor", lambda t, c=c, slot=slot:
                 t.transpose(slot[:, 0:1], stage[0:1, c * 128:(c + 1) * 128],
                             ident_f[0:1, 0:1]), inc=pe)
            P.wait("vector", pe)
            P.op("vector", lambda v, c=c, slot=slot, dst_col=dst_col:
                 v.tensor_copy(dst_col[:, c:c + 1], slot[:, 0:1]), inc=dv)
    P.wait("vector", dv)
    P.op("vector", lambda v: v.tensor_scalar(
        out=bqs_col[:, :], in0=bqs_col[:, :], scalar1=SCALE_ATTN,
        scalar2=None, op0=ALU.mult), inc=dv)

    P.wait("sync", pe)
    P.op("sync", lambda s: s.dma_start(out=stage[0:1, 0:1], in_=b_thr[None, :]),
         inc=qs, k=16)
    P.wait("tensor", qs)
    P.wait("tensor", dv)
    P.op("tensor", lambda t: t.matmul(tp_f[0][:, 0:1], ones_f[0:1, 0:128],
                                      stage[0:1, 0:1], start=True, stop=True),
         inc=pe)
    P.wait("vector", pe)
    P.op("vector", lambda v: v.tensor_scalar(
        out=nbthr_col[:, :], in0=tp_f[0][:, 0:1], scalar1=-1.0, scalar2=None,
        op0=ALU.mult), inc=dv)
    m_prologue_pe = pe.n

    P.mark_phase("prologue")
    abctx = ExitStack()
    asb = lambda *a, **kw: abctx.enter_context(nc.sbuf_tensor(*a, **kw))
    qkvT = asb("qkvT", [128, 24, T], BF16)

    axctx = ExitStack()
    xlnT = axctx.enter_context(nc.sbuf_tensor("xlnT", [128, DC, T + 2], BF16))
    cwp = [axctx.enter_context(nc.sbuf_tensor(f"cwp{i}", [128, 3, 512], BF16))
           for i in range(4)]
    actx = ExitStack()
    lsb = lambda *a, **kw: actx.enter_context(nc.sbuf_tensor(*a, **kw))
    xt = [lsb("xt0s", [128, D], F32), lsb("xt1s", [128, D], F32)]
    dump1 = lsb("dump1", [128, D], BF16)
    xln_bf = [lsb("xlnbf0", [128, D], BF16), lsb("xlnbf1", [128, D], BF16)]
    lnsm = lsb("lnsm", [128, 16], F32)
    tpb = [actx.enter_context(nc.psum_tensor(f"tpb{i}", [128, 128], BF16))
           for i in range(4)]

    m_norm = {}
    m_tp = {}
    tp_evac = [None] * 4
    m_xlnT_dv = 0
    m_xlnT_ac = 0
    xcnt = [0, 0]
    for t in range(9):
        slot = t % 2
        nrow = 128 if t < 8 else 2
        r0 = t * 128 if t < 8 else 1024
        col0 = r0
        if t >= 2:
            P.wait("sync", dv, m_norm[t - 2])
        else:
            P.wait("sync", dv)
        P.op("sync", lambda s, slot=slot, r0=r0, nrow=nrow:
             s.dma_start(out=xt[slot][0:nrow, :], in_=x_in[r0:r0 + nrow, :]),
             inc=xts[slot], k=16)
        xcnt[slot] += 1
        P.wait("vector", xts[slot], 16 * xcnt[slot])
        m_s1 = P.op("vector", lambda v, slot=slot, nrow=nrow: v.tensor_scalar(
            out=dump1[0:nrow, :], in0=xt[slot][0:nrow, :], scalar1=1.0,
            scalar2=None, op0=ALU.mult, op1=ALU.add,
            accum_out=lnsm[0:nrow, slot:slot + 1]), inc=dv)
        P.wait("scalar", xts[slot], 16 * xcnt[slot])
        if t >= 2:
            P.wait("scalar", pe, m_tp[t - 2])
        m_s2 = P.op("scalar", lambda s, slot=slot, nrow=nrow: s.activation(
            xln_bf[slot][0:nrow, :], xt[slot][0:nrow, :], AF.Square,
            accum_out=lnsm[0:nrow, 2 + slot:3 + slot]), inc=ac)
        P.wait("vector", dv, m_s1)
        P.wait("vector", ac, m_s2)
        c_mn, c_msq, c_nv = 4 + slot, 6 + slot, 8 + slot
        P.op("vector", lambda v, slot=slot, nrow=nrow, c=c_mn: v.tensor_scalar(
            out=lnsm[0:nrow, c:c + 1], in0=lnsm[0:nrow, slot:slot + 1],
            scalar1=1.0 / D, scalar2=None, op0=ALU.mult), inc=dv)
        P.wait("vector", dv)
        P.op("vector", lambda v, slot=slot, nrow=nrow, c=c_msq: v.tensor_scalar(
            out=lnsm[0:nrow, c:c + 1], in0=lnsm[0:nrow, 2 + slot:3 + slot],
            scalar1=1.0 / D, scalar2=None, op0=ALU.mult), inc=dv)
        P.wait("vector", dv)
        m_nv = P.op("vector", lambda v, nrow=nrow, cm=c_mn, cq=c_msq, cn=c_nv:
                    v.scalar_tensor_tensor(
                        out=lnsm[0:nrow, cn:cn + 1], in0=lnsm[0:nrow, cm:cm + 1],
                        scalar=lnsm[0:nrow, cm:cm + 1], in1=lnsm[0:nrow, cq:cq + 1],
                        op0=ALU.mult, op1=ALU.subtract), inc=dv)
        P.wait("scalar", dv, m_nv)
        c_ln, c_rs = 10 + slot, 12 + slot
        P.op("scalar", lambda s, nrow=nrow, cn=c_nv, cl=c_ln: s.activation(
            lnsm[0:nrow, cl:cl + 1], lnsm[0:nrow, cn:cn + 1], AF.Ln,
            scale=-1.0, bias=eps_col[0:nrow, :]), inc=ac)
        P.wait("scalar", ac)
        m_rstd = P.op("scalar", lambda s, nrow=nrow, cl=c_ln, cr=c_rs:
                      s.activation(lnsm[0:nrow, cr:cr + 1],
                                   lnsm[0:nrow, cl:cl + 1], AF.Exp, scale=-0.5),
                      inc=ac)
        P.wait("vector", ac, m_rstd)
        m_norm[t] = P.op("vector", lambda v, slot=slot, nrow=nrow, cm=c_mn, cr=c_rs:
                         v.tensor_scalar(
                             out=xln_bf[slot][0:nrow, :], in0=xt[slot][0:nrow, :],
                             scalar1=lnsm[0:nrow, cm:cm + 1],
                             scalar2=lnsm[0:nrow, cr:cr + 1],
                             op0=ALU.subtract, op1=ALU.mult), inc=dv)
        P.wait("tensor", dv, m_norm[t])
        for dc in range(DC):
            s4 = dc % 4
            if tp_evac[s4] is not None:
                P.wait("tensor", tp_evac[s4][0], tp_evac[s4][1])
            P.op("tensor", lambda t_, slot=slot, nrow=nrow, dc=dc, s4=s4:
                 t_.transpose(tpb[s4][:, 0:nrow],
                              xln_bf[slot][0:nrow, dc * 128:(dc + 1) * 128],
                              ident_bf[0:nrow, 0:nrow]), inc=pe)
            eng, ctr = ("vector", dv) if dc % 2 == 0 else ("scalar", ac)
            P.wait(eng, pe)
            mv = P.op(eng, lambda e, dc=dc, s4=s4, col0=col0, nrow=nrow:
                      _copy(e, xlnT[:, dc, col0:col0 + nrow],
                            tpb[s4][:, 0:nrow]), inc=ctr)
            tp_evac[s4] = (ctr, mv)
        m_tp[t] = pe.n
    m_xlnT_dv = dv.n
    m_xlnT_ac = ac.n

    dma_tap("xlnT", lambda: xlnT[:, :, :], (128, DC, T + 2), BF16,
            [(dv, m_xlnT_dv), (ac, m_xlnT_ac)])
    P.mark_phase("ln1")

    actx.close()
    pctx.close()
    cctx = ExitStack()
    pc = [cctx.enter_context(nc.psum_tensor(f"pc{i}", [128, 512], F32))
          for i in range(8)]

    P.wait("tensor", dv, m_xlnT_dv)
    P.wait("tensor", ac, m_xlnT_ac)
    m_piece = {}
    m_conv_evac = [ (None, None) ] * 6
    conv_bank_evac = {}
    cwcnt = [0, 0, 0, 0]
    for ms in range(6):
        for k in range(DC):
            p = ms * DC + k
            slot = p % 4
            if p >= 4:
                P.wait("gpsimd", pe, m_piece[p - 4])
            P.op("gpsimd", lambda g, k=k, ms=ms, slot=slot: g.dma_start(
                out=cwp[slot][:, :, :],
                in_=conv_w[:, k * 128:(k + 1) * 128,
                           ms * 512:(ms + 1) * 512].rearrange("w d m -> d w m")),
                inc=cw[slot], k=16)
            cwcnt[slot] += 1
            P.wait("tensor", cw[slot], 16 * cwcnt[slot])
            for w in range(3):
                for msub in range(4):
                    for half in range(2):
                        last_of_piece = (w == 2 and msub == 3 and half == 1)
                        bank = msub * 2 + half
                        if k == 0 and w == 0 and ms >= 1:
                            g = conv_bank_evac.get((ms - 1, bank))
                            if g is not None:
                                P.wait("tensor", g[0], g[1])
                        P.op("tensor", lambda t_, k=k, w=w, msub=msub,
                             half=half, slot=slot:
                             t_.matmul(pc[msub * 2 + half][:, :],
                                       cwp[slot][:, w, msub * 128:(msub + 1) * 128],
                                       xlnT[:, k, 512 * half + w: 512 * half + w + 512],
                                       start=(k == 0 and w == 0),
                                       stop=(k == DC - 1 and w == 2)),
                             inc=pe if last_of_piece else None)
            m_piece[p] = pe.n
        m_group = pe.n
        for msub in range(4):
            for half in range(2):
                mchunk = ms * 4 + msub
                bank = msub * 2 + half
                tile = pc[bank]
                if msub < 2:
                    P.wait("vector", pe, m_group)
                    mv = P.op("vector", lambda v, tile=tile, mchunk=mchunk, half=half:
                              v.tensor_scalar(
                                  out=qkvT[:, mchunk, 512 * half:512 * half + 512],
                                  in0=tile[:, :], scalar1=convb_col[:, mchunk:mchunk + 1],
                                  scalar2=None, op0=ALU.add), inc=dv)
                    conv_bank_evac[(ms, bank)] = (dv, mv)
                else:
                    P.wait("scalar", pe, m_group)
                    mv = P.op("scalar", lambda s, tile=tile, mchunk=mchunk, half=half:
                              s.activation(
                                  qkvT[:, mchunk, 512 * half:512 * half + 512],
                                  tile[:, :], AF.Identity,
                                  bias=convb_col[:, mchunk:mchunk + 1]), inc=ac)
                    conv_bank_evac[(ms, bank)] = (ac, mv)
        m_conv_evac[ms] = (dv.n, ac.n)
    m_qkvT_dv, m_qkvT_ac = dv.n, ac.n
    m_conv_done_pe = pe.n

    dma_tap("qkvT", lambda: qkvT[:, :, :], (128, 24, T), BF16,
            [(dv, m_qkvT_dv), (ac, m_qkvT_ac)])
    P.mark_phase("conv")

    cctx.close()
    axctx.close()

    wbctx = ExitStack()
    wbuf = [wbctx.enter_context(nc.sbuf_tensor("wbuf0", [128, 8, 1024], BF16, side="right")),
            wbctx.enter_context(nc.sbuf_tensor("wbuf1", [128, 8, 1024], BF16, side="right"))]
    wbcnt = [0, 0]
    P.op("gpsimd", lambda g: g.dma_start(
        out=wbuf[0][:, :, :],
        in_=wq_d.rearrange("(c p) e -> p c e", p=128)), inc=wbs[0], k=16)
    wbcnt[0] += 1
    P.op("gpsimd", lambda g: g.dma_start(
        out=wbuf[1][:, :, :],
        in_=wk_d.rearrange("(c p) e -> p c e", p=128)), inc=wbs[1], k=16)
    wbcnt[1] += 1

    bctx = ExitStack()
    bsb = lambda *a, **kw: bctx.enter_context(nc.sbuf_tensor(*a, **kw))
    qhT = bsb("qhT", [128, 8, T], BF16)
    khT = bsb("khT", [128, 8, T], BF16)
    vh = bsb("vh", [128, 8, 1024], BF16)

    bqkctx = ExitStack()
    pq = [bqkctx.enter_context(nc.psum_tensor(f"pq{i}", [128, 512], F32))
          for i in range(4)]
    bvrow = bqkctx.enter_context(nc.sbuf_tensor("bvrow", [1, 1024], BF16))
    g1 = P.mksem("g1")
    qg_wo_sem = g1
    m_qkevac = {}
    for proj, (wslot, dstT, bias_col, scale) in enumerate(
            ((0, qhT, bqs_col, SCALE_ATTN), (1, khT, bk_col, 1.0))):
        P.wait("tensor", wbs[wslot], 16 * wbcnt[wslot])
        if proj == 0:
            P.wait("tensor", dv, m_qkvT_dv)
            P.wait("tensor", ac, m_qkvT_ac)
        for hc in range(8):
            for half in range(2):
                tile = pq[(hc % 2) * 2 + half]
                key = (proj, hc - 2, half)
                if key in m_qkevac:
                    P.wait("tensor", ac, m_qkevac[key])
                mlast = None
                for kc in range(8):
                    mlast = P.op("tensor", lambda t_, tile=tile, wslot=wslot,
                                 kc=kc, hc=hc, half=half:
                                 t_.matmul(tile[:, :],
                                           wbuf[wslot][:, kc, hc * 128:(hc + 1) * 128],
                                           qkvT[:, (0 if wslot == 0 else 8) + kc,
                                                half * 512:half * 512 + 512],
                                           start=(kc == 0), stop=(kc == 7)),
                                 inc=pe if kc == 7 else None)
                P.wait("scalar", pe, mlast)
                m_qkevac[(proj, hc, half)] = P.op(
                    "scalar", lambda s, tile=tile, dstT=dstT, hc=hc, half=half,
                    bias_col=bias_col, scale=scale:
                    s.activation(dstT[:, hc, half * 512:half * 512 + 512],
                                 tile[:, :], AF.Identity, scale=scale,
                                 bias=bias_col[:, hc:hc + 1]), inc=ac)
        if proj == 0:
            m_qproj_pe = pe.n
    m_qk_ac = ac.n

    P.wait("gpsimd", pe, m_qproj_pe)
    P.op("gpsimd", lambda g: g.dma_start(
        out=wbuf[0][:, :, :],
        in_=wv_d.rearrange("(c p) e -> p c e", p=128)), inc=wbs[0], k=16)
    wbcnt[0] += 1
    P.op("gpsimd", lambda g: g.dma_start(out=bvrow[0:1, :], in_=bv_d[None, :]),
         inc=g1, k=16)
    m_qg_bv = g1.n


    m_vevac = {}
    P.wait("tensor", wbs[0], 16 * wbcnt[0])
    P.wait("tensor", g1, m_qg_bv)
    P.wait("tensor", ac, m_qk_ac)
    for tt in range(NT):
        for hh in range(2):
            tile = pq[(tt % 2) * 2 + hh]
            key = (tt - 2, hh)
            if key in m_vevac:
                P.wait("tensor", dv, m_vevac[key])
            for kc in range(8):
                P.op("tensor", lambda t_, tile=tile, kc=kc, tt=tt, hh=hh:
                     t_.matmul(tile[:, :],
                               qkvT[:, 16 + kc, tt * 128:(tt + 1) * 128],
                               wbuf[0][:, kc, hh * 512:hh * 512 + 512],
                               start=(kc == 0), stop=False))
            mlast = P.op("tensor", lambda t_, tile=tile, hh=hh:
                         t_.matmul(tile[:, :], ones_bf[0:1, 0:128],
                                   bvrow[0:1, hh * 512:hh * 512 + 512],
                                   start=False, stop=True), inc=pe)
            P.wait("vector", pe, mlast)
            m_vevac[(tt, hh)] = P.op(
                "vector", lambda v, tile=tile, tt=tt, hh=hh:
                v.tensor_copy(vh[:, tt, hh * 512:hh * 512 + 512], tile[:, :]),
                inc=dv)
    m_vh_dv = dv.n
    m_vproj_pe = pe.n
    bqkctx.close()
    wbctx.close()

    P.mark_phase("qkv")
    dma_tap("qhT", lambda: qhT[:, :, :], (128, 8, T), BF16, [(ac, m_qk_ac)])
    dma_tap("khT", lambda: khT[:, :, :], (128, 8, T), BF16, [(ac, m_qk_ac)])
    dma_tap("vh", lambda: vh[:, :, :], (128, 8, 1024), BF16, [(dv, m_vh_dv)])
    P.mark_phase("qkv_taps")

    octx = ExitStack()
    oT = octx.enter_context(nc.sbuf_tensor("oT", [128, 8, T], BF16))
    wo_sb = octx.enter_context(nc.sbuf_tensor("wo_sb", [128, 8, 1024], BF16))
    P.wait("gpsimd", pe, m_conv_done_pe)
    P.op("gpsimd", lambda g: g.dma_start(
        out=wo_sb[:, :, :],
        in_=wo_d.rearrange("(c p) e -> p c e", p=128)), inc=g1, k=16)
    m_qg_wo = g1.n

    b2ctx = ExitStack()
    a2 = lambda *a, **kw: b2ctx.enter_context(nc.sbuf_tensor(*a, **kw))
    mask_bf = a2("mask_bf", [128, 128], BF16)
    Pf = [a2("Pf0", [128, 512], F32), a2("Pf1", [128, 512], F32)]
    Pbf = [a2("Pbf0", [128, 512], BF16), a2("Pbf1", [128, 512], BF16)]
    PT_sb = [a2("PTsb0", [128, 128], BF16), a2("PTsb1", [128, 128], BF16)]
    att_sm = a2("att_sm", [128, 24], F32)
    SpB = b2ctx.enter_context(nc.psum_tensor("SpB", [128, 2048], F32))
    PTp = [b2ctx.enter_context(nc.psum_tensor(f"PTp{i}", [128, 128], BF16))
           for i in range(2)]
    OTp = [b2ctx.enter_context(nc.psum_tensor(f"OTp{i}", [128, 128], F32))
           for i in range(2)]

    P.op("gpsimd", lambda g: g.memset(mask_bf[:, :], 0.0), inc=po)
    P.wait("gpsimd", po)
    m_mask = P.op("gpsimd", lambda g: g.affine_select(
        out=mask_bf[:, :], in_=mask_bf[:, :], pattern=[[-1, 128]],
        compare_op=ALU.is_ge, fill=NEG_INF, base=0, channel_multiplier=1),
        inc=po)
    for e_ in range(8):
        P.op("gpsimd", lambda g, e_=e_: g.dma_start(
            out=bwf_dram.rearrange("p (c e r) -> p c e r", e=8, r=16)[:, :, e_, :],
            in_=B_w[e_, :, :].rearrange("(c p) r -> p c r", p=128)),
            inc=qc, k=16)
    m_qc_relay = qc.n
    g4s = P.mksem("g4s")
    for i in range(8):
        P.op("gpsimd", lambda g, i=i: g.dma_start(
            out=wbase_bf[i * 512:(i + 1) * 512, :],
            in_=W_base[i * 512:(i + 1) * 512, :]), inc=g4s, k=16)
    m_qg_wbase = g4s.n

    m_spb_used = {}
    m_pbf_used = [None, None]
    m_ptevac = [None, None]
    m_ptpevac = [None, None]
    m_otevac = [None, None]
    P.wait("tensor", ac, m_qk_ac)
    P.wait("tensor", dv, m_vh_dv)
    P.wait("tensor", po, m_mask)
    groups = [(b, hg) for b in range(NT) for hg in range(4)]
    gstate = {}

    def attn_p1a(g):
        b, hg = groups[g]
        gs = g % 2
        if g >= 1:
            P.wait("tensor", ac, m_spb_used[g - 1])
        mS = None
        for i in range(4):
            h = 4 * hg + i
            ch, hf = h // 2, h % 2
            P.op("tensor", lambda t_, i=i, ch=ch, hf=hf, b=b:
                 t_.matmul(SpB[:, 512 * i:512 * i + 128],
                           qhT[64 * hf:64 * hf + 64, ch, 128 * b:128 * b + 128],
                           khT[64 * hf:64 * hf + 64, ch, 128 * b:128 * b + 128],
                           start=True, stop=False,
                           tile_position=(64 * hf, 0)))
            mS = P.op("tensor", lambda t_, i=i:
                      t_.matmul(SpB[:, 512 * i:512 * i + 128],
                                ident_bf[:, :], mask_bf[:, :],
                                start=False, stop=True), inc=pe)
        P.wait("vector", pe, mS)
        mneg = P.op("vector", lambda v, gs=gs: v.reduce_max(
            att_sm[:, 12 * gs:12 * gs + 4],
            SpB.rearrange("p (i k) -> p i k", i=4)[:, :, 0:128],
            axis=AX.X, negate=True), inc=dv)
        P.wait("scalar", dv, mneg)
        mexp = None
        for i in range(4):
            mexp = P.op("scalar", lambda s, gs=gs, i=i: s.activation(
                Pf[gs][:, 128 * i:128 * (i + 1)],
                SpB[:, 512 * i:512 * i + 128], AF.Exp,
                bias=att_sm[:, 12 * gs + i:12 * gs + i + 1],
                accum_out=att_sm[:, 12 * gs + 4 + i:12 * gs + 5 + i]),
                inc=ac if i == 3 else None)
        m_spb_used[g] = mexp
        gstate[g] = mexp

    def attn_p1b(g):
        gs = g % 2
        mexp = gstate.pop(g)
        P.wait("vector", ac, mexp)
        mri = P.op("vector", lambda v, gs=gs: v.reciprocal(
            att_sm[:, 12 * gs + 8:12 * gs + 12],
            att_sm[:, 12 * gs + 4:12 * gs + 8]), inc=dv)
        P.wait("vector", dv, mri)
        if m_pbf_used[gs] is not None:
            P.wait("vector", pe, m_pbf_used[gs])
        mpn = None
        for i in range(4):
            mpn = P.op("vector", lambda v, gs=gs, i=i: v.tensor_scalar(
                out=Pbf[gs][:, 128 * i:128 * (i + 1)],
                in0=Pf[gs][:, 128 * i:128 * (i + 1)],
                scalar1=att_sm[:, 12 * gs + 8 + i:12 * gs + 9 + i],
                scalar2=None, op0=ALU.mult), inc=dv if i == 3 else None)
        gstate[(g, "pn")] = mpn

    def attn_p2(g):
        b, hg = groups[g]
        gs = g % 2
        mpn = gstate.pop((g, "pn"))
        P.wait("tensor", dv, mpn)
        mo_bank = [None, None]
        mt = None
        for i in range(4):
            h = 4 * hg + i
            ps_slot = i % 2
            ot_bank = i // 2
            if m_ptpevac[ps_slot] is not None:
                P.wait("tensor", m_ptpevac[ps_slot][0], m_ptpevac[ps_slot][1])
            mt = P.op("tensor", lambda t_, gs=gs, i=i, ps_slot=ps_slot:
                      t_.transpose(PTp[ps_slot][:, :],
                                   Pbf[gs][:, 128 * i:128 * (i + 1)],
                                   ident_bf[:, :]), inc=pe)
            if m_ptevac[ps_slot] is not None:
                P.wait("vector", m_ptevac[ps_slot][0], m_ptevac[ps_slot][1])
            P.wait("vector", pe, mt)
            mc = P.op("vector", lambda v, ps_slot=ps_slot:
                      v.tensor_copy(PT_sb[ps_slot][:, :], PTp[ps_slot][:, :]),
                      inc=dv)
            m_ptpevac[ps_slot] = (dv, mc)
            P.wait("tensor", dv, mc)
            if i % 2 == 0 and m_otevac[ot_bank] is not None:
                P.wait("tensor", m_otevac[ot_bank][0], m_otevac[ot_bank][1])
            if i % 2 == 1:
                P.wait("tensor", pe, mo_bank[ot_bank])
            mo = P.op("tensor", lambda t_, i=i, h=h, b=b, ps_slot=ps_slot,
                      ot_bank=ot_bank:
                      t_.matmul(OTp[ot_bank][64 * (i % 2):64 * (i % 2) + 64, :],
                                vh[:, b, 64 * h:64 * h + 64],
                                PT_sb[ps_slot][:, :],
                                start=True, stop=True,
                                tile_position=(0, 64 * (i % 2))),
                      inc=pe)
            mo_bank[ot_bank] = mo
            m_ptevac[ps_slot] = (pe, mo)
        m_pbf_used[gs] = mt
        for j in range(2):
            P.wait("scalar", pe, mo_bank[j])
            mev = P.op("scalar", lambda s, hg=hg, b=b, j=j:
                       s.activation(oT[:, 2 * hg + j, 128 * b:128 * b + 128],
                                    OTp[j][:, :], AF.Copy), inc=ac)
            m_otevac[j] = (ac, mev)

    for g in range(len(groups)):
        attn_p1a(g)
        if g >= 1:
            attn_p2(g - 1)
        attn_p1b(g)
    attn_p2(len(groups) - 1)
    m_oT_ac = ac.n
    b2ctx.close()

    dma_tap("oT", lambda: oT[:, :, :], (128, 8, T), BF16, [(ac, m_oT_ac)])
    P.mark_phase("attn")

    cc = ExitStack()
    csb = lambda *a, **kw: cc.enter_context(nc.sbuf_tensor(*a, **kw))
    attn_midT = csb("attn_midT", [128, 8, T], BF16)
    pwb = [csb("pwb0", [128, 8, 512], BF16), csb("pwb1", [128, 8, 512], BF16)]
    xcb = [csb("xcb0", [128, 512], F32), csb("xcb1", [128, 512], F32)]
    x2p_sb = [csb("x2p0", [128, 512], F32), csb("x2p1", [128, 512], F32)]
    dump2 = csb("dump2", [128, 512], F32)
    sqd = [csb("sqd0", [128, 512], BF16), csb("sqd1", [128, 512], BF16)]
    projb_row = csb("projb_row", [1, D], BF16)
    cyctx = ExitStack()
    py = [cyctx.enter_context(nc.psum_tensor(f"py{i}", [128, 512], F32))
          for i in range(4)]

    g2s = P.mksem("g2s")
    P.wait("gpsimd", pe, m_vproj_pe)
    P.op("gpsimd", lambda g: g.dma_start(out=projb_row[0:1, :],
                                         in_=proj_b[None, :]), inc=g2s, k=16)
    m_qg_projb = g2s.n

    m_y1evac = {}
    P.wait("tensor", ac, m_oT_ac)
    P.wait("tensor", qg_wo_sem, m_qg_wo)
    for wc in range(8):
        for half in range(2):
            tile = py[(wc % 2) * 2 + half]
            key = (wc - 2, half)
            if key in m_y1evac:
                P.wait("tensor", ac, m_y1evac[key])
            mlast = None
            for hc in range(8):
                mlast = P.op("tensor", lambda t_, tile=tile, hc=hc, wc=wc, half=half:
                             t_.matmul(tile[:, :],
                                       wo_sb[:, hc, wc * 128:(wc + 1) * 128],
                                       oT[:, hc, half * 512:half * 512 + 512],
                                       start=(hc == 0), stop=(hc == 7)),
                             inc=pe if hc == 7 else None)
            P.wait("scalar", pe, mlast)
            m_y1evac[(wc, half)] = P.op(
                "scalar", lambda s, tile=tile, wc=wc, half=half:
                s.activation(attn_midT[:, wc, half * 512:half * 512 + 512],
                             tile[:, :], AF.Identity, bias=bo_col[:, wc:wc + 1]),
                inc=ac)
    m_y1_ac = ac.n
    dma_tap("attn_midT", lambda: attn_midT[:, :, :], (128, 8, T), BF16,
            [(ac, m_y1_ac)])
    P.mark_phase("y1")

    cyctx.close()
    pc2ctx = ExitStack()
    pc2 = [pc2ctx.enter_context(nc.psum_tensor(f"pc2{i}", [128, 512], F32))
           for i in range(8)]
    pwcnt = [0, 0]
    xccnt = [0, 0]
    xc_need = {}
    x2cnt = [0, 0]
    m_pw_used = {}
    m_xc_load_order = {}
    m_x2evac = {}
    m_sq_used = {}
    for ds in range(8):
        slot = ds % 2
        if SERIALIZE_Y2:
            P.wait("tensor", dv)
            P.wait("tensor", ac)
        if ds >= 2:
            P.wait("gpsimd", pe, m_pw_used[ds - 2])
        P.op("gpsimd", lambda g, ds=ds: g.dma_start(
            out=pwb[ds % 2][:, :, :],
            in_=proj_w.rearrange("(c p) d -> p c d", p=128)[:, :, ds * 512:(ds + 1) * 512]),
            inc=pws[slot], k=16)
        pwcnt[slot] += 1
        P.wait("tensor", pws[slot], 16 * pwcnt[slot])
        if ds == 0:
            P.wait("tensor", ac, m_y1_ac)
            P.wait("tensor", g2s, m_qg_projb)
        for tt in range(NT):
            idx = ds * 8 + tt
            xsl = idx % 2
            key = (idx - 2)
            if key in m_xc_load_order:
                P.wait("sync", dv, m_xc_load_order[key])
            elif ds == 0 and tt < 2:
                P.wait("sync", dv, m_xlnT_dv)
            P.op("sync", lambda s, ds=ds, tt=tt, xsl=xsl: s.dma_start(
                out=xcb[xsl][:, :],
                in_=x_in[1 + tt * 128:1 + tt * 128 + 128,
                         ds * 512:(ds + 1) * 512]), inc=xcs[xsl], k=16)
            xccnt[xsl] += 1
            xc_need[idx] = (xsl, xccnt[xsl])
            if ds >= 1:
                P.wait("tensor", dv, m_x2evac.get((ds - 1, tt)))
            for wc in range(8):
                P.op("tensor", lambda t_, wc=wc, tt=tt, slot=slot:
                     t_.matmul(pc2[tt][:, :],
                               attn_midT[:, wc, tt * 128:(tt + 1) * 128],
                               pwb[slot][:, wc, :],
                               start=(wc == 0), stop=False))
            mlast = P.op("tensor", lambda t_, tt=tt, ds=ds:
                         t_.matmul(pc2[tt][:, :], ones_bf[0:1, 0:128],
                                   projb_row[0:1, ds * 512:ds * 512 + 512],
                                   start=False, stop=True), inc=pe)
            psl = (ds * 8 + tt) % 2
            P.wait("vector", pe, mlast)
            _sl, _n = xc_need[ds * 8 + tt]
            P.wait("vector", xcs[_sl], 16 * _n)
            prev_store = (ds * 8 + tt) - 2
            if prev_store >= 0:
                P.wait("vector", x2st[psl], 16 * (x2cnt[psl]))
                mprev_sq = m_sq_used.get(prev_store)
                if mprev_sq is not None:
                    P.wait("vector", ac, mprev_sq)
            madd = P.op("vector", lambda v, tt=tt, psl=psl, xsl=xsl:
                        v.tensor_add(x2p_sb[psl][:, :], pc2[tt][:, :],
                                     xcb[xsl][:, :]), inc=dv)
            m_xc_load_order[ds * 8 + tt] = madd
            m_x2evac[(ds, tt)] = madd
            P.wait("vector", dv, madd)
            P.op("vector", lambda v, tt=tt, ds=ds, psl=psl: v.tensor_scalar(
                out=dump2[:, :], in0=x2p_sb[psl][:, :], scalar1=1.0,
                scalar2=None, op0=ALU.mult, op1=ALU.add,
                accum_out=c_sums[:, tt, ds:ds + 1]), inc=dv)
            P.wait("scalar", dv, madd)
            m_sq_used[ds * 8 + tt] = P.op(
                "scalar", lambda s, tt=tt, ds=ds, psl=psl:
                s.activation(sqd[psl][:, :], x2p_sb[psl][:, :], AF.Square,
                             accum_out=c_sumsq[:, tt, ds:ds + 1]), inc=ac)
            P.op("scalar", lambda s, tt=tt, ds=ds, psl=psl: s.dma_start(
                out=x2s[tt * 128:(tt + 1) * 128, ds * 512:(ds + 1) * 512],
                in_=x2p_sb[psl][:, :]), inc=x2st[psl], k=16)
            x2cnt[psl] += 1
        m_pw_used[ds] = pe.n
    m_x2_all = (x2cnt[0] * 16, x2cnt[1] * 16)
    m_stats_dv = dv.n
    m_stats_ac = ac.n
    if "x2" in taps:
        ap = tapout("x2", (T, D), F32)
        P.wait("scalar", x2st[0], m_x2_all[0])
        P.wait("scalar", x2st[1], m_x2_all[1])
        P.op("scalar", lambda s, ap=ap: s.dma_start(out=ap, in_=x2s), inc=tps, k=16)
    pc2ctx.close()
    cc.close()
    octx.close()
    bctx.close()
    abctx.close()

    P.mark_phase("y2")
    dectx = ExitStack()
    dsb = lambda *a, **kw: dectx.enter_context(nc.sbuf_tensor(*a, **kw))
    xln2T = dsb("xln2T", [128, DC, T], BF16)
    A_wT = dsb("A_wT", [128, DC, 128], BF16)
    Bmat = dsb("Bmat", [128, D], BF16)
    wgt_cols = dsb("wgt_cols", [128, DC, 9], BF16)
    E_bf = dsb("E_bf", [8, 128], BF16)
    hT_sb = dsb("hT_sb", [128, T], BF16)
    hwT_sb = dsb("hwT_sb", [128, 8, 128], BF16)
    bbase_row = dsb("bbase_row", [1, D], BF16)
    gts_sb = dsb("gts_sb", [128, NT, 9], F32)
    wbT = [dsb("wbT0", [128, DC, 512], BF16), dsb("wbT1", [128, DC, 512], BF16)]
    wtcnt = [0, 0]

    dp1 = ExitStack()
    awtm = dp1.enter_context(nc.sbuf_tensor("awtm", [128, D], BF16, side="right"))
    wg_tm = dp1.enter_context(nc.sbuf_tensor("wg_tm", [8, D], BF16, side="right"))
    wth_row = dp1.enter_context(nc.sbuf_tensor("wth_row", [1, D], BF16, side="right"))
    dpctx = ExitStack()
    tpd = [dpctx.enter_context(nc.psum_tensor(f"tpd{i}", [128, 128], BF16))
           for i in range(2)]
    tp9 = dpctx.enter_context(nc.psum_tensor("tp9", [128, 9], BF16))

    P.op("gpsimd", lambda g: g.memset(E_bf[:, :], MOE_SCALING), inc=po)
    P.wait("gpsimd", po)
    P.op("gpsimd", lambda g: g.affine_select(
        out=E_bf[:, :], in_=E_bf[:, :], pattern=[[1, 128]],
        compare_op=ALU.is_ge, fill=0.0, base=0, channel_multiplier=-16), inc=po)
    P.wait("gpsimd", po)
    m_E = P.op("gpsimd", lambda g: g.affine_select(
        out=E_bf[:, :], in_=E_bf[:, :], pattern=[[-1, 128]],
        compare_op=ALU.is_ge, fill=0.0, base=15, channel_multiplier=16), inc=po)

    g3s = P.mksem("g3s")
    P.op("gpsimd", lambda g: g.dma_start(out=awtm[:, :], in_=A_w), inc=g3s, k=16)
    P.op("gpsimd", lambda g: g.dma_start(out=wg_tm[0:8, :], in_=W_gate), inc=g3s, k=16)
    P.op("gpsimd", lambda g: g.dma_start(out=wth_row[0:1, :], in_=W_thr[None, :]),
         inc=g3s, k=16)
    P.op("gpsimd", lambda g: g.dma_start(out=bbase_row[0:1, :], in_=b_base[None, :]),
         inc=g3s, k=16)
    m_qg_dprep = g3s.n

    P.wait("tensor", g3s, m_qg_dprep)
    P.wait("tensor", dv, m_stats_dv)
    P.wait("tensor", ac, m_stats_ac)
    m_dprep_evac = []
    tpd_guard = [None, None]
    for dc in range(DC):
        s2 = dc % 2
        if tpd_guard[s2] is not None:
            P.wait("tensor", tpd_guard[s2][0], tpd_guard[s2][1])
        mt = P.op("tensor", lambda t_, dc=dc, s2=s2: t_.transpose(
            tpd[s2][:, :], awtm[:, dc * 128:(dc + 1) * 128], ident_bf[:, :]),
            inc=pe)
        P.wait("vector", pe, mt)
        mv = P.op("vector", lambda v, dc=dc, s2=s2: v.tensor_copy(
            A_wT[:, dc, :], tpd[s2][:, :]), inc=dv)
        tpd_guard[s2] = (dv, mv)
    m_w9 = None
    for dc in range(DC):
        P.wait("tensor", dv if m_w9 is None else m_w9[0],
               None if m_w9 is None else m_w9[1])
        mtg = P.op("tensor", lambda t_, dc=dc: t_.transpose(
            tp9[:, 0:8], wg_tm[0:8, dc * 128:(dc + 1) * 128],
            ident_bf[0:8, 0:8]), inc=pe)
        P.wait("tensor", pe, mtg)
        mt = P.op("tensor", lambda t_, dc=dc: t_.transpose(
            tp9[:, 8:9], wth_row[0:1, dc * 128:(dc + 1) * 128],
            ident_bf[0:1, 0:1]), inc=pe)
        P.wait("vector", pe, mt)
        mv = P.op("vector", lambda v, dc=dc: v.tensor_copy(
            wgt_cols[:, dc, :], tp9[:, :]), inc=dv)
        m_w9 = (dv, mv)
    m_wgt_done = m_w9[1]
    m_dp1_done = pe.n
    dp1.close()
    dp2 = ExitStack()
    bwbf = dp2.enter_context(nc.sbuf_tensor("bwbf", [128, DC, 128], BF16, side="right"))
    P.wait("scalar", qc, m_qc_relay)
    P.wait("scalar", pe, m_dp1_done)
    P.op("scalar", lambda s: s.dma_start(
        out=bwbf[:, :, :], in_=bwf_dram.rearrange("p (c k) -> p c k", c=DC)),
        inc=qc, k=16)
    m_qc_bw = qc.n
    P.wait("tensor", qc, m_qc_bw)
    tpd_guard = [None, None]
    for oc in range(DC):
        s2 = oc % 2
        if tpd_guard[s2] is not None:
            P.wait("tensor", tpd_guard[s2][0], tpd_guard[s2][1])
        mt = P.op("tensor", lambda t_, oc=oc, s2=s2: t_.transpose(
            tpd[s2][:, :], bwbf[:, oc, :], ident_bf[:, :]), inc=pe)
        eng, ctr = ("vector", dv) if oc % 2 == 0 else ("scalar", ac)
        P.wait(eng, pe, mt)
        mv = P.op(eng, lambda e, oc=oc, s2=s2: _copy(
            e, Bmat[:, oc * 128:(oc + 1) * 128], tpd[s2][:, :]), inc=ctr)
        tpd_guard[s2] = (ctr, mv)
    m_dprep_dv = dv.n
    m_dprep_ac = ac.n
    m_dprep_pe = pe.n
    dp2.close()

    P.mark_phase("dprep")
    c2ctx = ExitStack()
    c2s = lambda *a, **kw: c2ctx.enter_context(nc.sbuf_tensor(*a, **kw))
    xr_sb = [c2s("xr0s", [128, D], F32)]
    xln2_bf = [c2s("xln2bf0", [128, D], BF16)]
    ln2sm = c2s("ln2sm", [128, 16], F32)
    tp2 = [c2ctx.enter_context(nc.psum_tensor(f"tp2{i}", [128, 128], BF16))
           for i in range(3)]
    pgt1 = c2ctx.enter_context(nc.psum_tensor("pgt1", [128, 9], F32))
    ph1 = c2ctx.enter_context(nc.psum_tensor("ph1", [128, 512], F32))

    P.wait("sync", x2st[0], m_x2_all[0])
    P.wait("sync", x2st[1], m_x2_all[1])
    m2_norm = {}
    m2_tp = {}
    m2_tile = {}
    m_gtevac = {}
    m_hT = {}
    tp2_evac = [None] * 3

    def plan_gt(tt):
        P.wait("tensor", dv, m2_tile[tt][0])
        P.wait("tensor", ac, m2_tile[tt][1])
        if tt == 0:
            P.wait("tensor", dv, m_wgt_done)
        if (tt - 1) in m_gtevac:
            P.wait("tensor", ac, m_gtevac[tt - 1])
        mg = None
        for dc in range(DC):
            mg = P.op("tensor", lambda t_, tt=tt, dc=dc: t_.matmul(
                pgt1[:, :], xln2T[:, dc, tt * 128:(tt + 1) * 128],
                wgt_cols[:, dc, :], start=(dc == 0), stop=(dc == DC - 1)),
                inc=pe if dc == DC - 1 else None)
        P.wait("scalar", pe, mg)
        m_gtevac[tt] = P.op("scalar", lambda s, tt=tt: s.activation(
            gts_sb[:, tt, :], pgt1[:, :], AF.Copy), inc=ac)

    def plan_hT(half):
        if (half - 1) in m_hT:
            P.wait("tensor", ac, m_hT[half - 1])
        mh = None
        for dc in range(DC):
            mh = P.op("tensor", lambda t_, half=half, dc=dc: t_.matmul(
                ph1[:, :], A_wT[:, dc, :],
                xln2T[:, dc, half * 512:half * 512 + 512],
                start=(dc == 0), stop=(dc == DC - 1)),
                inc=pe if dc == DC - 1 else None)
        P.wait("scalar", pe, mh)
        m_hT[half] = P.op("scalar", lambda s, half=half: s.activation(
            hT_sb[:, half * 512:half * 512 + 512], ph1[:, :], AF.Copy),
            inc=ac)

    xrcnt = [0, 0]
    for tt in range(NT):
        slot = 0
        if tt >= 1:
            P.wait("sync", dv, m2_norm[tt - 1])
        P.op("sync", lambda s, tt=tt, slot=slot: s.dma_start(
            out=xr_sb[slot][:, :], in_=x2s[tt * 128:(tt + 1) * 128, :]),
            inc=xrs[slot], k=16)
        xrcnt[slot] += 1
        if tt == 0:
            P.wait("sync", g4s, m_qg_wbase)
        os_pre = tt // 4
        for dc in range(8 * (tt % 4), 8 * (tt % 4) + 8):
            P.op("sync", lambda s, os_=os_pre, dc=dc:
                 s.dma_start_transpose(
                     wbT[os_][:, dc, :],
                     wbase_bf[os_ * 512:(os_ + 1) * 512, dc * 128:(dc + 1) * 128]),
                 inc=wts[os_pre], k=16)
        wtcnt[os_pre] += 8
        if tt == 0:
            P.wait("vector", dv, m_stats_dv)
            P.wait("vector", ac, m_stats_ac)
        P.op("vector", lambda v, tt=tt, slot=slot: v.reduce_sum(
            ln2sm[:, slot:slot + 1], c_sums[:, tt, :], axis=AX.X), inc=dv)
        P.wait("vector", dv)
        P.op("vector", lambda v, tt=tt, slot=slot: v.reduce_sum(
            ln2sm[:, 2 + slot:3 + slot], c_sumsq[:, tt, :], axis=AX.X), inc=dv)
        P.wait("vector", dv)
        c_mn, c_msq, c_nv = 4 + slot, 6 + slot, 8 + slot
        P.op("vector", lambda v, slot=slot, c=c_mn: v.tensor_scalar(
            out=ln2sm[:, c:c + 1], in0=ln2sm[:, slot:slot + 1],
            scalar1=1.0 / D, scalar2=None, op0=ALU.mult), inc=dv)
        P.wait("vector", dv)
        P.op("vector", lambda v, slot=slot, c=c_msq: v.tensor_scalar(
            out=ln2sm[:, c:c + 1], in0=ln2sm[:, 2 + slot:3 + slot],
            scalar1=1.0 / D, scalar2=None, op0=ALU.mult), inc=dv)
        P.wait("vector", dv)
        m_nv = P.op("vector", lambda v, cm=c_mn, cq=c_msq, cn=c_nv:
                    v.scalar_tensor_tensor(
                        out=ln2sm[:, cn:cn + 1], in0=ln2sm[:, cm:cm + 1],
                        scalar=ln2sm[:, cm:cm + 1], in1=ln2sm[:, cq:cq + 1],
                        op0=ALU.mult, op1=ALU.subtract), inc=dv)
        P.wait("scalar", dv, m_nv)
        c_ln, c_rs = 10 + slot, 12 + slot
        P.op("scalar", lambda s, cn=c_nv, cl=c_ln: s.activation(
            ln2sm[:, cl:cl + 1], ln2sm[:, cn:cn + 1], AF.Ln,
            scale=-1.0, bias=eps_col[:, :]), inc=ac)
        P.wait("scalar", ac)
        m_rstd = P.op("scalar", lambda s, cl=c_ln, cr=c_rs: s.activation(
            ln2sm[:, cr:cr + 1], ln2sm[:, cl:cl + 1], AF.Exp, scale=-0.5),
            inc=ac)
        P.wait("vector", ac, m_rstd)
        P.wait("vector", xrs[slot], 16 * xrcnt[slot])
        if tt >= 1:
            P.wait("vector", pe, m2_tp[tt - 1])
        m2_norm[tt] = P.op("vector", lambda v, slot=slot, cm=c_mn, cr=c_rs:
                           v.tensor_scalar(
                               out=xln2_bf[slot][:, :], in0=xr_sb[slot][:, :],
                               scalar1=ln2sm[:, cm:cm + 1],
                               scalar2=ln2sm[:, cr:cr + 1],
                               op0=ALU.subtract, op1=ALU.mult), inc=dv)
        P.wait("tensor", dv, m2_norm[tt])
        for dc in range(DC):
            s4 = dc % 3
            if tp2_evac[s4] is not None:
                P.wait("tensor", tp2_evac[s4][0], tp2_evac[s4][1])
            P.op("tensor", lambda t_, slot=slot, dc=dc, s4=s4: t_.transpose(
                tp2[s4][:, :], xln2_bf[slot][:, dc * 128:(dc + 1) * 128],
                ident_bf[:, :]), inc=pe)
            eng, ctr = ("vector", dv) if dc % 2 == 0 else ("scalar", ac)
            P.wait(eng, pe)
            mv = P.op(eng, lambda e, dc=dc, s4=s4, tt=tt: _copy(
                e, xln2T[:, dc, tt * 128:(tt + 1) * 128], tp2[s4][:, :]), inc=ctr)
            tp2_evac[s4] = (ctr, mv)
        m2_tp[tt] = pe.n
        m2_tile[tt] = (dv.n, ac.n)
        if tt >= 1:
            plan_gt(tt - 1)
        if tt == 4:
            plan_hT(0)
    plan_gt(7)
    plan_hT(1)
    m_xln2T_dv = dv.n
    m_xln2T_ac = ac.n
    c2ctx.close()

    dma_tap("xln2T", lambda: xln2T[:, :, :], (128, DC, T), BF16,
            [(dv, m_xln2T_dv), (ac, m_xln2T_ac)])
    P.mark_phase("ln2")

    dpctx.close()
    dctx = ExitStack()
    d2 = lambda *a, **kw: dctx.enter_context(nc.sbuf_tensor(*a, **kw))
    gexp_sb = d2("gexp_sb", [128, NT, 8], F32)
    et_sb = d2("et_sb", [128, NT], F32)
    gsm = d2("gsm", [128, 6 * NT], F32)
    wrel_sb = d2("wrel_sb", [128, NT, 8], F32)
    wt_bf = d2("wt_bf", [128, NT, 8], BF16)
    wtT_sb = [d2("wtT0", [8, 128], BF16), d2("wtT1", [8, 128], BF16)]
    wexp_bf = [d2("wexp0", [128, 128], BF16), d2("wexp1", [128, 128], BF16)]
    pwx = [dctx.enter_context(nc.psum_tensor(f"pwx{i}", [128, 128], F32))
           for i in range(2)]
    ptw = [dctx.enter_context(nc.psum_tensor(f"ptw{i}", [8, 128], BF16))
           for i in range(2)]

    P.wait("scalar", ac, m_gtevac[7])
    mge = P.op("scalar", lambda s: s.activation(
        gexp_sb[:, :, :], gts_sb[:, :, 0:8], AF.Exp), inc=ac)
    P.wait("scalar", ac, mge)
    met = P.op("scalar", lambda s: s.activation(
        et_sb[:, :], gts_sb[:, :, 8], AF.Exp, scale=-1.0,
        bias=nbthr_col[:, 0:1]), inc=ac)
    P.wait("vector", ac, met)
    P.op("vector", lambda v: v.reduce_sum(
        gsm[:, 0:NT], gexp_sb[:, :, :], axis=AX.X), inc=dv)
    P.wait("vector", dv)
    P.op("vector", lambda v: v.reciprocal(gsm[:, NT:2 * NT], gsm[:, 0:NT]),
         inc=dv)
    P.wait("vector", dv)
    P.op("vector", lambda v: v.tensor_scalar(
        out=gsm[:, 2 * NT:3 * NT], in0=et_sb[:, :], scalar1=1.0, scalar2=None,
        op0=ALU.add), inc=dv)
    P.wait("vector", dv)
    P.op("vector", lambda v: v.reciprocal(gsm[:, 3 * NT:4 * NT],
                                          gsm[:, 2 * NT:3 * NT]), inc=dv)
    P.wait("vector", dv)
    P.op("vector", lambda v: v.tensor_scalar(
        out=gsm[:, 3 * NT:4 * NT], in0=gsm[:, 3 * NT:4 * NT],
        scalar1=MAX_THRESHOLD, scalar2=None, op0=ALU.mult), inc=dv)
    P.wait("vector", dv)
    P.op("vector", lambda v: v.tensor_tensor(
        wrel_sb[:, :, :], gexp_sb[:, :, :],
        gsm[:, NT:2 * NT].unsqueeze(2).broadcast_to([128, NT, 8]),
        op=ALU.mult), inc=dv)
    P.wait("vector", dv)
    P.op("vector", lambda v: v.tensor_tensor(
        wrel_sb[:, :, :], wrel_sb[:, :, :],
        gsm[:, 3 * NT:4 * NT].unsqueeze(2).broadcast_to([128, NT, 8]),
        op=ALU.subtract), inc=dv)
    P.wait("vector", dv)
    P.op("vector", lambda v: v.tensor_scalar(
        out=wrel_sb[:, :, :], in0=wrel_sb[:, :, :], scalar1=0.0, scalar2=None,
        op0=ALU.max), inc=dv)
    P.wait("vector", dv)
    P.op("vector", lambda v: v.reduce_sum(
        gsm[:, 4 * NT:5 * NT], wrel_sb[:, :, :], axis=AX.X), inc=dv)
    P.wait("vector", dv)
    P.op("vector", lambda v: v.tensor_scalar(
        out=gsm[:, 4 * NT:5 * NT], in0=gsm[:, 4 * NT:5 * NT], scalar1=1e-20,
        scalar2=None, op0=ALU.max), inc=dv)
    P.wait("vector", dv)
    P.op("vector", lambda v: v.reciprocal(gsm[:, 5 * NT:6 * NT],
                                          gsm[:, 4 * NT:5 * NT]), inc=dv)
    P.wait("vector", dv)
    mwt = P.op("vector", lambda v: v.tensor_tensor(
        wt_bf[:, :, :], wrel_sb[:, :, :],
        gsm[:, 5 * NT:6 * NT].unsqueeze(2).broadcast_to([128, NT, 8]),
        op=ALU.mult), inc=dv)

    m_hw = {}
    m_pwx_used = {}
    m_ptw_used = {}
    m_wtT_used = {}
    P.wait("tensor", dv, mwt)
    P.wait("tensor", po, m_E)
    for tt in range(NT):
        s2 = tt % 2
        if (tt - 2) in m_ptw_used:
            P.wait("tensor", m_ptw_used[tt - 2][0], m_ptw_used[tt - 2][1])
        mtw = P.op("tensor", lambda t_, tt=tt, s2=s2: t_.transpose(
            ptw[s2][:, :], wt_bf[:, tt, :], ident_bf[:, :]), inc=pe)
        P.wait("vector", pe, mtw)
        if (tt - 2) in m_wtT_used:
            P.wait("vector", pe, m_wtT_used[tt - 2])
        mcw = P.op("vector", lambda v, s2=s2: v.tensor_copy(
            wtT_sb[s2][0:8, :], ptw[s2][0:8, :]), inc=dv)
        m_ptw_used[tt] = (dv, mcw)
        P.wait("tensor", dv, mcw)
        if (tt - 2) in m_pwx_used:
            P.wait("tensor", ac, m_pwx_used[tt - 2])
        mwx = P.op("tensor", lambda t_, s2=s2: t_.matmul(
            pwx[s2][:, :], E_bf[0:8, :], wtT_sb[s2][0:8, :],
            start=True, stop=True), inc=pe)
        m_wtT_used[tt] = mwx
        P.wait("scalar", pe, mwx)
        mwe = P.op("scalar", lambda s, s2=s2: s.activation(
            wexp_bf[s2][:, :], pwx[s2][:, :], AF.Copy), inc=ac)
        m_pwx_used[tt] = mwe
        P.wait("vector", ac, mwe)
        if tt == 0:
            P.wait("vector", ac, m_hT[1])
        m_hw[tt] = P.op("vector", lambda v, tt=tt, s2=s2: v.tensor_mul(
            hwT_sb[:, tt, :], hT_sb[:, tt * 128:(tt + 1) * 128],
            wexp_bf[s2][:, :]), inc=dv)
    m_hw_all = dv.n
    dctx.close()

    dma_tap("hwT", lambda: hwT_sb[:, :, :], (128, 8, 128), BF16,
            [(dv, m_hw_all)])
    dma_tap("hT", lambda: hT_sb[:, :], (128, T), BF16,
            [(ac, m_hT[1])])

    P.mark_phase("ffnA")
    ectx = ExitStack()
    es = lambda *a, **kw: ectx.enter_context(nc.sbuf_tensor(*a, **kw))
    xeb = [es(f"xeb{i}", [128, 512], F32) for i in range(16)]
    yob = [es("yob0", [128, 512], F32), es("yob1", [128, 512], F32)]
    pe8 = [ectx.enter_context(nc.psum_tensor(f"pe8{i}", [128, 512], F32))
           for i in range(8)]

    xecnt = [0, 0]
    yscnt = [0, 0]
    m_wbT_used = {}
    m_xe_used = {}
    m_eevac = {}
    P.wait("sync", dv, m_hw_all)
    for tt in range(NT):
        P.op("sync", lambda s, tt=tt: s.dma_start(
            out=xeb[tt][:, :],
            in_=x2s[tt * 128:(tt + 1) * 128, 0:512]), inc=xes[0], k=16)
    xecnt[0] += 8
    for os_ in range(8):
        slot = os_ % 2
        if os_ >= 2:
            P.wait("scalar", pe, m_wbT_used[os_ - 2])
            for dc in range(DC):
                P.op("scalar", lambda s, os_=os_, dc=dc, slot=slot:
                     s.dma_start_transpose(
                         wbT[slot][:, dc, :],
                         wbase_bf[os_ * 512:(os_ + 1) * 512, dc * 128:(dc + 1) * 128]),
                     inc=wts[slot], k=16)
            wtcnt[slot] += 32
        P.wait("tensor", wts[slot], 16 * (wtcnt[slot] if os_ >= 2 else 32))
        if os_ == 0:
            P.wait("tensor", dv, m_hw_all)
            P.wait("tensor", dv, m_dprep_dv)
            P.wait("tensor", ac, m_dprep_ac)
            P.wait("tensor", g3s, m_qg_dprep)
        if os_ + 1 < 8:
            on = os_ + 1
            for tt in range(NT):
                xsl = (on % 2) * 8 + tt
                key = (on - 2, tt)
                if key in m_xe_used:
                    P.wait("sync", dv, m_xe_used[key])
                P.op("sync", lambda s, on=on, tt=tt, xsl=xsl: s.dma_start(
                    out=xeb[xsl][:, :],
                    in_=x2s[tt * 128:(tt + 1) * 128, on * 512:(on + 1) * 512]),
                    inc=xes[on % 2], k=16)
            xecnt[on % 2] += 8
        for tt in range(NT):
            xsl = (os_ % 2) * 8 + tt
            if os_ >= 1:
                P.wait("tensor", dv, m_eevac.get((os_ - 1, tt)))
            for dc in range(DC):
                P.op("tensor", lambda t_, dc=dc, tt=tt, slot=slot:
                     t_.matmul(pe8[tt][:, :],
                               xln2T[:, dc, tt * 128:(tt + 1) * 128],
                               wbT[slot][:, dc, :],
                               start=(dc == 0), stop=False))
            P.op("tensor", lambda t_, tt=tt, os_=os_: t_.matmul(
                pe8[tt][:, :], hwT_sb[:, tt, :],
                Bmat[:, os_ * 512:(os_ + 1) * 512], start=False, stop=False))
            mlast = P.op("tensor", lambda t_, tt=tt, os_=os_: t_.matmul(
                pe8[tt][:, :], ones_bf[0:1, 0:128],
                bbase_row[0:1, os_ * 512:os_ * 512 + 512],
                start=False, stop=True), inc=pe)
            psl = (os_ * 8 + tt) % 2
            P.wait("vector", pe, mlast)
            P.wait("vector", xes[os_ % 2], 16 * xecnt[os_ % 2])
            if (os_ * 8 + tt) - 2 >= 0:
                P.wait("vector", yss[psl], 16 * yscnt[psl])
            madd = P.op("vector", lambda v, tt=tt, psl=psl, xsl=xsl:
                        v.tensor_add(yob[psl][:, :], pe8[tt][:, :],
                                     xeb[xsl][:, :]), inc=dv)
            m_xe_used[(os_, tt)] = madd
            m_eevac[(os_, tt)] = madd
            P.wait("sync", dv, madd)
            P.op("sync", lambda s, tt=tt, os_=os_, psl=psl: s.dma_start(
                out=y_out[tt * 128:(tt + 1) * 128, os_ * 512:(os_ + 1) * 512],
                in_=yob[psl][:, :]), inc=yss[psl], k=16)
            yscnt[psl] += 1
        m_wbT_used[os_] = pe.n

    if "wbase0" in taps:
        ap = tapout("wbase0", (512, D), BF16)
        P.wait("scalar", g4s, m_qg_wbase)
        P.op("scalar", lambda s, ap=ap: s.dma_start(out=ap, in_=wbase_bf[0:512, :]),
             inc=tps, k=16)
    dma_tap("Bmat", lambda: Bmat[:, :], (128, D), BF16,
            [(dv, m_dprep_dv), (ac, m_dprep_ac)])
    dma_tap("bbase", lambda: bbase_row[0:1, :], (1, D), BF16, [(dv, m_hw_all)])
    P.wait("sync", yss[0], 16 * yscnt[0])
    P.wait("sync", yss[1], 16 * yscnt[1])
    if tps.n:
        P.wait("sync", tps, tps.n)
    ectx.close()
    dectx.close()

    P.emit(stop_at=stop_at)
    return nc, tap_outs


def kernel(**inputs):
    x = np.asarray(inputs["L0_attn_inputs"], dtype=np.float32).reshape(8192, D)
    xpad = np.zeros((8194, D), dtype=np.float32)
    xpad[1:8193] = x

    def f32c(a, shape=None):
        a = np.ascontiguousarray(np.asarray(a, dtype=np.float32))
        return a.reshape(shape) if shape is not None else a

    common = {
        "conv_w": f32c(inputs["L0_attn_conv_w"]),
        "conv_b": f32c(inputs["L0_attn_conv_b"]),
        "wq": f32c(inputs["L0_attn_wq"], (1024, 1024)),
        "bq": f32c(inputs["L0_attn_bq"], (1024,)),
        "wk": f32c(inputs["L0_attn_wk"], (1024, 1024)),
        "bk": f32c(inputs["L0_attn_bk"], (1024,)),
        "wv": f32c(inputs["L0_attn_wv"], (1024, 1024)),
        "bv": f32c(inputs["L0_attn_bv"], (1024,)),
        "wo": f32c(inputs["L0_attn_wo"], (1024, 1024)),
        "bo": f32c(inputs["L0_attn_bo"], (1024,)),
        "proj_w": f32c(inputs["L0_attn_proj_w"]),
        "proj_b": f32c(inputs["L0_attn_proj_b"]),
        "W_base": f32c(inputs["L0_ffn_W_base"]),
        "b_base": f32c(inputs["L0_ffn_b_base"]),
        "W_gate": f32c(inputs["L0_ffn_W_gate"]),
        "W_thr": f32c(inputs["L0_ffn_W_thr"]),
        "b_thr": f32c(inputs["L0_ffn_b_thr"]),
        "A_w": f32c(inputs["L0_ffn_A_w"], (128, D)),
        "B_w": f32c(inputs["L0_ffn_B_w"]),
    }
    in_maps = []
    for c in range(8):
        m = dict(common)
        m["x_in"] = np.ascontiguousarray(xpad[c * 1024:c * 1024 + 1026])
        in_maps.append(m)

    nc, _ = build_program()
    res = run_bass_kernel_spmd(nc, in_maps, core_ids=list(range(8)))
    y = np.concatenate([res.results[c]["y"] for c in range(8)], axis=0)
    return y.reshape(1, 8192, D).astype(np.float32)
